# version 35
# baseline (speedup 1.0000x reference)
from contextlib import ExitStack

import numpy as np

import concourse.bass as bass
import concourse.mybir as mybir
from concourse.bass_utils import run_bass_kernel_spmd
from concourse.alu_op_type import AluOpType as OP

F32 = mybir.dt.float32
BF16 = mybir.dt.bfloat16
U16 = mybir.dt.uint16
FP8 = mybir.dt.float8e4
AF = mybir.ActivationFunctionType
AX = mybir.AxisListType.X

D = 1024
H = 16
DK = 64
NCORES = 8
AMAX_MODE = "core"
DBG_SEGS = None
TRUNC = None
_last_counts = {}

ACT_CONSTS = [-40.0, 0.125, 1e-5, -0.5, 1.0, 0.0, -1.0, 448.0, 0.5]


class Prog:

    ENGS = ("te", "dv", "ac", "gp", "sy")

    NQ = 8
    NGQ = 4

    def __init__(self, nc):
        self.nc = nc
        self.gidx = 0
        self.body = {e: [] for e in self.ENGS}
        self.sem = {}
        self.cnt = {e: 0 for e in self.ENGS}
        for i in range(self.NQ):
            self.cnt[f"syq{i}"] = 0
        for i in range(self.NGQ):
            self.cnt[f"gpq{i}"] = 0
        self._gprot = 0
        self.waited = {e: {} for e in self.ENGS}
        self.lastw = {}
        self.readers = {}
        self._xfence = False
        self._dmarot = 0

    def alloc_sems(self, ctx):
        for s in (list(self.ENGS) + [f"syq{i}" for i in range(self.NQ)]
                  + [f"gpq{i}" for i in range(self.NGQ)]):
            self.sem[s] = ctx.enter_context(self.nc.semaphore(s))

    def _dep(self, eng, deps):
        merged = {}
        for d in deps:
            if d is None:
                continue
            s, vv = d
            if eng == "te" and s == "te":
                continue
            if merged.get(s, 0) < vv:
                merged[s] = vv
        for s, vv in sorted(merged.items()):
            if self.waited[eng].get(s, 0) >= vv:
                continue
            self.waited[eng][s] = vv
            sem = self.sem[s]
            self.body[eng].append((self.gidx, lambda e, sem=sem, vv=vv: e.wait_ge(sem, vv)))

    def op(self, eng, fn, ins=(), outs=(), sem=None, inc=1):
        deps = []
        for k in ins:
            deps.append(self.lastw.get(k))
        for k in outs:
            deps.append(self.lastw.get(k))
            for s, vv in self.readers.get(k, {}).items():
                deps.append((s, vv))
        self._dep(eng, deps)
        semname = sem or eng
        self.cnt[semname] += inc
        val = self.cnt[semname]
        semh = self.sem[semname]

        def emit(e, fn=fn, semh=semh, inc=inc):
            fn(e).then_inc(semh, inc)
        self.gidx += 1
        self.body[eng].append((self.gidx, emit))
        for k in outs:
            self.lastw[k] = (semname, val)
            self.readers[k] = {}
        for k in ins:
            r = self.readers.setdefault(k, {})
            if r.get(semname, 0) < val:
                r[semname] = val
        return (semname, val)

    def dma_fence(self):
        self._dep("sy", [(f"syq{i}", self.cnt[f"syq{i}"]) for i in range(self.NQ)]
                  + [(f"gpq{i}", self.cnt[f"gpq{i}"]) for i in range(self.NGQ)])

    def dma(self, fn, ins=(), outs=(), xpose=False, eng="sy"):
        if xpose or self._xfence:
            self.dma_fence()
            self._xfence = xpose
        if eng == "gp":
            r = self._gprot
            self._gprot = (r + 1) % self.NGQ
            sname = f"gpq{r}"
        else:
            r = self._dmarot
            self._dmarot = (r + 1) % self.NQ
            sname = f"syq{r}"
        deps = [(sname, self.cnt[sname])]
        if getattr(self, "_last_xpose", None) is not None:
            deps.append(self._last_xpose)
        self._dep(eng, deps)
        res = self.op(eng, fn, ins=ins, outs=outs, sem=sname, inc=16)
        if xpose:
            self._last_xpose = res
        return res


def build_program(T=2048, CH=256):
    NT = T // 128
    NCH = T // CH
    TPC = CH // 128
    assert NT <= 16
    nc = bass.Bass("TRN2", target_bir_lowering=False)
    p = Prog(nc)

    x_in = nc.dram_tensor("x", [T, D], F32, kind="ExternalInput").ap()
    win = {
        "q": nc.dram_tensor("wq", [D, D], F32, kind="ExternalInput").ap(),
        "k": nc.dram_tensor("wk", [D, D], F32, kind="ExternalInput").ap(),
        "v": nc.dram_tensor("wv", [D, D], F32, kind="ExternalInput").ap(),
        "o": nc.dram_tensor("wo", [D, D], F32, kind="ExternalInput").ap(),
        "f": nc.dram_tensor("wf", [D, D], F32, kind="ExternalInput").ap(),
    }
    bias_in = nc.dram_tensor("fbias", [1, D], F32, kind="ExternalInput").ap()
    beta_in = nc.dram_tensor("beta", [1, H], F32, kind="ExternalInput").ap()
    y_out = nc.dram_tensor("y", [T, D], F32, kind="ExternalOutput").ap()

    ctx = ExitStack()
    sb = lambda name, shape, dt: ctx.enter_context(nc.sbuf_tensor(name, shape, dt))
    psum = lambda name, shape: ctx.enter_context(nc.psum_tensor(name, shape, F32))

    with ctx:
        p.alloc_sems(ctx)

        x_tok = sb("x_tok", [128, NT * D], F32)

        def xq(tt):
            return f"x_t{tt}"
        wT = {n: sb(f"w{n}T", [128, 8 * D], BF16) for n in ("q", "k", "v", "o", "f")}
        stage = sb("stage", [128, D], F32)
        lnT = sb("lnT", [128, 8 * CH], BF16)
        qT = sb("qT", [128, 8 * CH], BF16)
        kT = sb("kT", [128, 8 * CH], BF16)
        sqT = sb("sqT", [128, 8 * CH], BF16)
        skT = sb("skT", [128, 8 * CH], BF16)
        sk_tok = sb("sk_tok", [128, TPC * D], BF16)
        v_tok = sb("v_tok", [128, TPC * D], BF16)
        a_s = sb("a_s", [128, TPC * D], BF16)
        ln_tok = sb("ln_tok", [128, TPC * D], BF16)
        ptbuf = sb("ptbuf", [128, 2 * D], BF16)
        pt_un = ptbuf[:, 0:D]
        pt_c2 = ptbuf[:, D:2*D]
        scr1 = sb("scr1", [128, D], BF16)
        scr2 = sb("scr2", [128, D], BF16)
        ib_sb = sb("ib_sb", [128, 512], BF16)
        m_sb = sb("m_sb", [128, 512], BF16)
        stats = sb("stats", [128, 112], F32)
        zfm = sb("zfm", [128, 8], F32)
        z2 = sb("z2", [128, 64], BF16)
        csum_f = sb("csum_f", [128, 64], F32)
        dq_sb = sb("dq_sb", [128, 128], F32)
        rdq = sb("rdq", [128, 128], F32)
        c1p = sb("c1p", [128, 16], BF16)
        c1pB = sb("c1pB", [128, 16], BF16)
        c3_sb = sb("c3_sb", [128, 64], BF16)
        c2pool = sb("c2pool", [2, 1024], BF16)
        rowred = sb("rowred", [128, 48], F32)
        rows = sb("rows", [1, 200], F32)
        scal = sb("scal", [1, 16], F32)
        scol = sb("scol", [128, 4], F32)
        betaE = sb("betaE", [2, 512], BF16)
        betaO = sb("betaO", [2, 512], BF16)
        betac = sb("betac", [128, 16], BF16)
        b4 = sb("b4", [4, 16], BF16)
        bias_bc = sb("bias_bc", [128, D], BF16)
        beta_row = sb("beta_row", [1, H], F32)
        cpool = sb("cpool", [128, 12], F32)
        ones1 = sb("ones1", [1, 128], BF16)
        ones1f = sb("ones1f", [1, 128], F32)
        idf = sb("idf", [128, 128], F32)
        idb = sb("idb", [128, 128], BF16)
        bt2 = sb("bt2", [2, 128], BF16)
        oneh2 = sb("oneh2", [128, 2], BF16)
        irep = sb("irep", [128, 64], BF16)

        stage_bf = pt_c2
        c1fm = ln_tok
        sk3_tok = skT
        asT = ptbuf
        scr3 = pt_un
        c2bE = c2pool[:, 0:512]
        c2bO = c2pool[:, 512:1024]

        ps_sc = psum("ps_sc", [128, 1024])
        ps_a = psum("ps_a", [128, 1024])
        ps_m = psum("ps_m", [128, 1024])
        ps_w1 = psum("ps_w1", [128, 512])
        ps_w2 = psum("ps_w2", [128, 512])

        for i, vv in enumerate(ACT_CONSTS):
            nc.const_aps.register(F32, float(vv), cpool[:, i:i+1])

        import sys
        print(f"[build] sbuf bytes remaining/partition: {nc.sbuf_bytes_remaining}",
              file=sys.stderr)
        blk = ctx.enter_context(nc.Block())

        def t(fn, ins=(), outs=()):
            return p.op("te", fn, ins=ins, outs=outs)

        def v(fn, ins=(), outs=()):
            return p.op("dv", fn, ins=ins, outs=outs)

        def a(fn, ins=(), outs=()):
            return p.op("ac", fn, ins=ins, outs=outs)

        def g(fn, ins=(), outs=()):
            return p.op("gp", fn, ins=ins, outs=outs)

        MM = dict(skip_group_check=True)

        for i, vv in enumerate(ACT_CONSTS):
            g(lambda e, i=i, vv=vv: e.memset(cpool[:, i:i+1], float(vv)),
              outs=(f"cp{i}",))
        g(lambda e: e.memset(ones1[:], 1.0), outs=("ones1",))
        g(lambda e: e.memset(ones1f[:], 1.0), outs=("ones1f",))
        for nm, tile in (("idf", idf), ("idb", idb)):
            g(lambda e, tile=tile: e.memset(tile[:], 1.0), outs=(nm,))
            g(lambda e, tile=tile: e.affine_select(
                tile[:], tile[:], pattern=[[-1, 128]], compare_op=OP.is_equal,
                fill=0.0, base=0, channel_multiplier=1), ins=(nm,), outs=(nm,))
        g(lambda e: e.memset(irep[:], 1.0), outs=("irep",))
        g(lambda e: e.affine_select(irep[0:64, :], irep[0:64, :], pattern=[[-1, 64]],
                                    compare_op=OP.is_equal, fill=0.0, base=0,
                                    channel_multiplier=1), ins=("irep",), outs=("irep",))
        g(lambda e: e.affine_select(irep[64:128, :], irep[64:128, :], pattern=[[-1, 64]],
                                    compare_op=OP.is_equal, fill=0.0, base=0,
                                    channel_multiplier=1), ins=("irep",), outs=("irep",))
        g(lambda e: e.memset(oneh2[:], 1.0), outs=("oneh2",))
        g(lambda e: e.affine_select(oneh2[:, 0:1], oneh2[:, 0:1], pattern=[[0, 1]],
                                    compare_op=OP.is_ge, fill=0.0, base=63,
                                    channel_multiplier=-1), ins=("oneh2",), outs=("oneh2",))
        g(lambda e: e.affine_select(oneh2[:, 1:2], oneh2[:, 1:2], pattern=[[0, 1]],
                                    compare_op=OP.is_ge, fill=0.0, base=-64,
                                    channel_multiplier=1), ins=("oneh2",), outs=("oneh2",))
        g(lambda e: e.memset(bt2[:], 1.0), outs=("bt2",))
        g(lambda e: e.affine_select(bt2[:], bt2[:], pattern=[[1, 128]],
                                    compare_op=OP.is_ge, fill=0.0, base=0,
                                    channel_multiplier=-64),
          ins=("bt2",), outs=("bt2",))
        g(lambda e: e.affine_select(bt2[:], bt2[:], pattern=[[-1, 128]],
                                    compare_op=OP.is_ge, fill=0.0, base=63,
                                    channel_multiplier=64),
          ins=("bt2",), outs=("bt2",))
        g(lambda e: e.memset(zfm[:], 0.0), outs=("zfm",))
        g(lambda e: e.memset(z2[:], 0.0), outs=("z2",))
        g(lambda e: e.memset(m_sb[:], 0.0), outs=("m_sbA", "m_sbB"))
        if DBG_SEGS is not None:
            g(lambda e: e.memset(a_s[:], 0.0), outs=("a_s",))

        xv = x_in.rearrange("(tt p) d -> p tt d", p=128)
        nsplit = min(4, NT)
        for q4 in range(nsplit):
            lo, hi = q4 * NT // nsplit, (q4 + 1) * NT // nsplit
            p.dma(lambda e, lo=lo, hi=hi: e.dma_start(
                out=x_tok[:, lo*D:hi*D].rearrange("p (tt d) -> p tt d", d=D),
                in_=xv[:, lo:hi, :]),
                outs=tuple({xq(tt) for tt in range(lo, hi)}), eng="gp")

        def quant_buf(src_ap, src_key, dst_ap, dst_key, scol_idx):
            SC1 = ("scr1", "scr1b")
            SC2 = ("scr2", "scr2b")
            a(lambda e: e.activation(
                scr1[:, 0:D], src_ap, AF.Identity, scale=scol[:, scol_idx:scol_idx+1]),
              ins=(src_key, f"scol{scol_idx}"), outs=SC1)
            v(lambda e: e.tensor_scalar(scr3[:, 0:D].bitcast(U16), scr1[:, 0:D].bitcast(U16),
                                        0x7F80, None, OP.bitwise_and),
              ins=SC1, outs=("pt_un",))
            v(lambda e: e.tensor_scalar(scr2[:, 0:D], scr3[:, 0:D],
                                        24.0, 0.375, OP.mult, OP.max),
              ins=("pt_un",), outs=SC2)
            v(lambda e: e.tensor_add(scr1[:, 0:D], scr1[:, 0:D], scr2[:, 0:D]),
              ins=SC1 + SC2, outs=SC1)
            v(lambda e: e.tensor_sub(dst_ap, scr1[:, 0:D], scr2[:, 0:D]),
              ins=SC1 + SC2, outs=(dst_key,))

        def cross_part_max(colkey, col_ap, out_idx, out_key):
            t(lambda e: e.transpose(ps_w1[0:1, 0:128], col_ap, idf[:]),
              ins=(colkey, "idf"), outs=("psw1",))
            a(lambda e: e.activation(rows[0:1, 0:128], ps_w1[0:1, 0:128], AF.Copy),
              ins=("psw1",), outs=("rows128",))
            v(lambda e: e.tensor_reduce(scal[0:1, out_idx:out_idx+1],
                                        rows[0:1, 0:128], AX, OP.max),
              ins=("rows128",), outs=(out_key,))

        def bcast_scalar(in_idx, in_key, scol_idx):
            t(lambda e: e.matmul(ps_w2[:, 0:1], ones1f[:],
                                 scal[0:1, in_idx:in_idx+1], start=True, stop=True,
                                 **MM),
              ins=(in_key, "ones1f"), outs=("psw2",))
            a(lambda e: e.activation(scol[:, scol_idx:scol_idx+1], ps_w2[:, 0:1],
                                     AF.Copy),
              ins=("psw2",), outs=(f"scol{scol_idx}",))

        kTf = kT.ap().bitcast(F32)
        lanes_f32 = [(stage[:], "stage"), (kTf, "kT")]
        lanes_bf = [(sqT[:, 0:1024], "sqT"), (sqT[:, 1024:2048], "sqT")]

        def emit_prep_block(n, j):
            wsrc = win[n].rearrange("(j p) i -> p j i", p=128)
            ldst, lkey = lanes_f32[j % 2]
            conv, ckey = lanes_bf[j % 2]
            p.dma(lambda e, j=j, wsrc=wsrc, ldst=ldst: e.dma_start(
                out=ldst, in_=wsrc[:, j, :]), outs=(lkey,), eng="gp")
            if n == "f":
                quant_buf(ldst, lkey, conv, ckey, 0)
            else:
                v(lambda e, ldst=ldst, conv=conv: e.tensor_copy(conv, ldst),
                  ins=(lkey,), outs=(ckey,))
            pb, pbkey = (ps_w1, "psw1") if j % 2 == 0 else (ps_w2, "psw2")
            for cidx in range(8):
                t(lambda e, cidx=cidx, pb=pb, conv=conv: e.transpose(
                    pb[:, cidx*64:(cidx+1)*64].bitcast(BF16),
                    conv[:, cidx*128:(cidx+1)*128], idb[:]),
                  ins=(ckey, "idb"), outs=(pbkey,))
            if j % 2 == 0:
                v(lambda e, n=n, j=j, pb=pb: e.tensor_copy(
                    wT[n].ap().rearrange("p (c o) -> p c o", o=D)
                    [:, :, 128*j:128*j+128],
                    pb[:, 0:512].bitcast(BF16).rearrange(
                        "p (c o) -> p c o", o=128)),
                    ins=(pbkey,), outs=(f"w{n}T",))
            else:
                a(lambda e, n=n, j=j, pb=pb: e.activation(
                    wT[n].ap().rearrange("p (c o) -> p c o", o=D)
                    [:, :, 128*j:128*j+128],
                    pb[:, 0:512].bitcast(BF16).rearrange(
                        "p (c o) -> p c o", o=128),
                    AF.Copy),
                    ins=(pbkey,), outs=(f"w{n}T",))

        def prep_wf():
            wsrc = win["f"].rearrange("(j p) i -> p j i", p=128)
            for j in range(8):
                ldst, lkey = lanes_f32[j % 2]
                p.dma(lambda e, j=j, ldst=ldst: e.dma_start(
                    out=ldst, in_=wsrc[:, j, :]), outs=(lkey,), eng="gp")
                v(lambda e, j=j, ldst=ldst: e.tensor_reduce(
                    rowred[:, j:j+1], ldst, AX, OP.max,
                    apply_absolute_value=True), ins=(lkey,), outs=(f"wr{j}",))
            v(lambda e: e.tensor_reduce(rowred[:, 8:9], rowred[:, 0:8], AX, OP.max),
              ins=tuple(f"wr{j}" for j in range(8)), outs=("wamax_col",))
            cross_part_max("wamax_col", rowred[:, 8:9], 0, "wamax")
            v(lambda e: e.tensor_scalar(scal[0:1, 1:2], scal[0:1, 0:1], 1e-12,
                                        None, OP.max), ins=("wamax",), outs=("wt1",))
            v(lambda e: e.reciprocal(scal[0:1, 2:3], scal[0:1, 1:2]),
              ins=("wt1",), outs=("wt2",))
            v(lambda e: e.tensor_scalar(scal[0:1, 3:4], scal[0:1, 2:3], 448.0,
                                        448.0, OP.mult, OP.min),
              ins=("wt2",), outs=("s_w",))
            v(lambda e: e.reciprocal(scal[0:1, 4:5], scal[0:1, 3:4]),
              ins=("s_w",), outs=("rsw",))
            v(lambda e: e.tensor_scalar(scal[0:1, 4:5], scal[0:1, 4:5],
                                        0.5, None, OP.mult),
              ins=("rsw",), outs=("rsw",))
            bcast_scalar(3, "s_w", 0)

        def wf_pair(jp, extra=()):
            wsrc = win["f"].rearrange("(j p) i -> p j i", p=128)
            if True:
                j0 = 2 * jp
                p.dma(lambda e, j0=j0: e.dma_start(
                    out=stage[:], in_=wsrc[:, j0, :]), outs=("stage",),
                    eng="gp")
                p.dma(lambda e, j0=j0: e.dma_start(
                    out=kTf, in_=wsrc[:, j0 + 1, :]), outs=("kT",), eng="gp")
                a(lambda e: e.activation(sqT[:, 0:1024], stage[:], AF.Identity,
                                         scale=scol[:, 0:1]),
                  ins=("stage", "scol0"), outs=("sqT",))
                a(lambda e: e.activation(sqT[:, 1024:2048], kTf, AF.Identity,
                                         scale=scol[:, 0:1]),
                  ins=("kT", "scol0"), outs=("sqT",))
                v(lambda e: e.tensor_scalar(
                    skT[:, 0:2048].bitcast(U16), sqT[:, 0:2048].bitcast(U16),
                    0x7F80, None, OP.bitwise_and),
                  ins=("sqT",), outs=("skT",))
                v(lambda e: e.tensor_scalar(skT[:, 0:2048], skT[:, 0:2048],
                                            24.0, 0.375, OP.mult, OP.max),
                  ins=("skT",), outs=("skT",))
                v(lambda e: e.tensor_add(sqT[:, 0:2048], sqT[:, 0:2048],
                                         skT[:, 0:2048]),
                  ins=("sqT", "skT"), outs=("sqT",))
                v(lambda e: e.tensor_sub(sqT[:, 0:2048], sqT[:, 0:2048],
                                         skT[:, 0:2048]),
                  ins=("sqT", "skT"), outs=("sqT",))
                for dj in range(2):
                    j = j0 + dj
                    pb, pbkey = (ps_w1, "psw1") if dj == 0 else (ps_w2, "psw2")
                    for cidx in range(8):
                        t(lambda e, dj=dj, cidx=cidx, pb=pb: e.transpose(
                            pb[:, cidx*64:(cidx+1)*64].bitcast(BF16),
                            sqT[:, dj*1024+cidx*128:dj*1024+(cidx+1)*128],
                            idb[:]),
                          ins=("sqT", "idb"), outs=(pbkey,))
                    if dj == 0:
                        v(lambda e, j=j, pb=pb: e.tensor_copy(
                            wT["f"].ap().rearrange("p (c o) -> p c o", o=D)
                            [:, :, 128*j:128*j+128],
                            pb[:, 0:512].bitcast(BF16).rearrange(
                                "p (c o) -> p c o", o=128)),
                            ins=(pbkey,), outs=("wfT",) + extra)
                    else:
                        a(lambda e, j=j, pb=pb: e.activation(
                            wT["f"].ap().rearrange("p (c o) -> p c o", o=D)
                            [:, :, 128*j:128*j+128],
                            pb[:, 0:512].bitcast(BF16).rearrange(
                                "p (c o) -> p c o", o=128),
                            AF.Copy),
                            ins=(pbkey,), outs=("wfT",) + extra)

        prep_wf()

        p.dma(lambda e: e.dma_start(out=beta_row[:], in_=beta_in), outs=("beta_row",))
        a(lambda e: e.activation(rows[0:1, 144:144+H], beta_row[0:1, :], AF.Exp, scale=-1.0),
          ins=("beta_row",), outs=("bexp",))
        v(lambda e: e.tensor_scalar(rows[0:1, 144:144+H], rows[0:1, 144:144+H], 1.0, None, OP.add),
          ins=("bexp",), outs=("bexp",))
        v(lambda e: e.reciprocal(rows[0:1, 160:160+H], rows[0:1, 144:144+H]),
          ins=("bexp",), outs=("beta_v",))
        v(lambda e: e.tensor_scalar(rows[0:1, 176:176+H], rows[0:1, 160:160+H], -1.0, 1.0,
                                    OP.mult, OP.add),
          ins=("beta_v",), outs=("bomb",))
        t(lambda e: e.matmul(ps_w1[0:4, 0:H], ones1f[0:1, 0:4], rows[0:1, 176:176+H],
                             start=True, stop=True, **MM),
          ins=("bomb", "ones1f"), outs=("psw1",))
        v(lambda e: e.tensor_copy(b4[:], ps_w1[0:4, 0:H]), ins=("psw1",), outs=("b4",))
        v(lambda e: e.tensor_copy(
            betaE[:].rearrange("p (j tt) -> p j tt", tt=64),
            b4[0:2, 0:16].rearrange("p (j s) -> p j s", s=2)[:, :, 0:1]
            .broadcast_to([2, 8, 64])),
          ins=("b4",), outs=("betaE",))
        v(lambda e: e.tensor_copy(
            betaO[:].rearrange("p (j tt) -> p j tt", tt=64),
            b4[0:2, 0:16].rearrange("p (j s) -> p j s", s=2)[:, :, 1:2]
            .broadcast_to([2, 8, 64])),
          ins=("b4",), outs=("betaO",))
        t(lambda e: e.matmul(ps_w2[:, 0:H], ones1f[:], rows[0:1, 160:160+H],
                             start=True, stop=True, **MM),
          ins=("beta_v", "ones1f"), outs=("psw2",))
        v(lambda e: e.tensor_copy(betac[:, 0:16], ps_w2[:, 0:H]),
          ins=("psw2",), outs=("betac",))

        def ln_stats(tt):
            xs = x_tok[:, tt*D:(tt+1)*D]
            v(lambda e, xs=xs, tt=tt: e.tensor_reduce(stats[:, tt:tt+1], xs, AX, OP.add),
              ins=(xq(tt),), outs=(f"sum{tt}",))
            a(lambda e, xs=xs, tt=tt: e.activation(
                scr1[:, 0:D], xs, AF.Square, accum_out=stats[:, 16+tt:17+tt]),
              ins=(xq(tt),), outs=("scr1", "scr1b", f"sq{tt}"))
            v(lambda e, tt=tt: e.tensor_scalar(stats[:, tt:tt+1], stats[:, tt:tt+1],
                                               1.0/D, None, OP.mult),
              ins=(f"sum{tt}",), outs=(f"m{tt}",))
            v(lambda e, tt=tt: e.tensor_scalar(stats[:, 16+tt:17+tt],
                                               stats[:, 16+tt:17+tt], 1.0/D, None,
                                               OP.mult),
              ins=(f"sq{tt}",), outs=(f"sq{tt}",))
            v(lambda e, tt=tt: e.tensor_tensor(stats[:, 48+tt:49+tt], stats[:, tt:tt+1],
                                               stats[:, tt:tt+1], OP.mult),
              ins=(f"m{tt}",), outs=(f"msq{tt}",))
            v(lambda e, tt=tt: e.tensor_sub(stats[:, 16+tt:17+tt],
                                            stats[:, 16+tt:17+tt],
                                            stats[:, 48+tt:49+tt]),
              ins=(f"sq{tt}", f"msq{tt}"), outs=(f"var{tt}",))
            a(lambda e, tt=tt: e.activation(stats[:, 32+tt:33+tt],
                                            stats[:, 16+tt:17+tt], AF.Ln, bias=1e-5),
              ins=(f"var{tt}",), outs=(f"lnv{tt}",))
            a(lambda e, tt=tt: e.activation(stats[:, 32+tt:33+tt],
                                            stats[:, 32+tt:33+tt], AF.Exp, scale=-0.5),
              ins=(f"lnv{tt}",), outs=(f"rstd{tt}",))
            v(lambda e, tt=tt: e.tensor_scalar(
                stats[:, 64+tt:65+tt], stats[:, tt:tt+1],
                stats[:, 32+tt:33+tt], -1.0, OP.mult, OP.mult),
              ins=(f"m{tt}", f"rstd{tt}"), outs=(f"nmr{tt}",))

        def ln_tile(tt, j):
            a(lambda e, tt=tt, j=j: e.activation(
                ln_tok[:, j*D:(j+1)*D], x_tok[:, tt*D:(tt+1)*D], AF.Identity,
                bias=stats[:, 64+tt:65+tt], scale=stats[:, 32+tt:33+tt]),
              ins=(xq(tt), f"rstd{tt}", f"nmr{tt}"),
              outs=("lnbuf" if j == 0 else "lnbufB",))

        def ffn_stats(tiles, wf_pairs=False, qkvo=()):
            qk = list(qkvo)
            for tt in tiles:
                ln_stats(tt)
                if wf_pairs and tt % 4 == 3:
                    wf_pair(tt // 4)
                for nn, jj in qk[tt*2:tt*2+2]:
                    emit_prep_block(nn, jj)
            if wf_pairs:
                for jp in range((max(tiles) + 1) // 4, 4):
                    wf_pair(jp)
            for nn, jj in qk[2*(max(tiles)+1 if tiles else 0):]:
                emit_prep_block(nn, jj)

        def ffn_amax(tiles):
            for tt in tiles:
                v(lambda e, tt=tt: e.tensor_reduce(
                    rowred[:, 16+tt:17+tt], x_tok[:, tt*D:(tt+1)*D], AX, OP.max),
                  ins=(xq(tt),), outs=(f"amx{tt}",))
                v(lambda e, tt=tt: e.tensor_reduce(
                    rowred[:, 32+tt:33+tt], x_tok[:, tt*D:(tt+1)*D], AX, OP.min),
                  ins=(xq(tt),), outs=(f"ami{tt}",))
                v(lambda e, tt=tt: e.tensor_sub(
                    rowred[:, 16+tt:17+tt], rowred[:, 16+tt:17+tt],
                    stats[:, tt:tt+1]),
                  ins=(f"amx{tt}", f"m{tt}"), outs=(f"amx{tt}",))
                v(lambda e, tt=tt: e.tensor_sub(
                    rowred[:, 32+tt:33+tt], stats[:, tt:tt+1],
                    rowred[:, 32+tt:33+tt]),
                  ins=(f"ami{tt}", f"m{tt}"), outs=(f"ami{tt}",))
                v(lambda e, tt=tt: e.tensor_max(
                    rowred[:, 16+tt:17+tt], rowred[:, 16+tt:17+tt],
                    rowred[:, 32+tt:33+tt]),
                  ins=(f"amx{tt}", f"ami{tt}"), outs=(f"amx{tt}",))
                v(lambda e, tt=tt: e.tensor_mul(
                    rowred[:, 16+tt:17+tt], rowred[:, 16+tt:17+tt],
                    stats[:, 32+tt:33+tt]),
                  ins=(f"amx{tt}", f"rstd{tt}"), outs=(f"am{tt}",))

        def ffn(tag, prep_blocks=(), store=False, stats_done=False):
            if not stats_done:
                ffn_stats(range(NT))
                ffn_amax(range(NT))
            v(lambda e: e.tensor_reduce(rowred[:, 0:1], rowred[:, 16:16+NT], AX,
                                        OP.max),
              ins=tuple(f"am{tt}" for tt in range(NT)), outs=("amax_col",))
            cross_part_max("amax_col", rowred[:, 0:1], 5, "amax")
            v(lambda e: e.tensor_scalar(scal[0:1, 6:7], scal[0:1, 5:6], 1e-12, None,
                                        OP.max), ins=("amax",), outs=("am1",))
            v(lambda e: e.reciprocal(scal[0:1, 7:8], scal[0:1, 6:7]),
              ins=("am1",), outs=("am2",))
            v(lambda e: e.tensor_scalar(scal[0:1, 8:9], scal[0:1, 7:8], 448.0, 448.0,
                                        OP.mult, OP.min), ins=("am2",), outs=("s_x",))
            v(lambda e: e.reciprocal(scal[0:1, 9:10], scal[0:1, 8:9]),
              ins=("s_x",), outs=("rsx",))
            v(lambda e: e.tensor_tensor(scal[0:1, 10:11], scal[0:1, 9:10],
                                        scal[0:1, 4:5], OP.mult),
              ins=("rsx", "rsw"), outs=("deq",))
            bcast_scalar(8, "s_x", 1)
            bcast_scalar(10, "deq", 2)
            v(lambda e: e.reciprocal(scal[0:1, 11:12], scal[0:1, 10:11]),
              ins=("deq",), outs=("ideq",))
            p.dma(lambda e: e.dma_start(out=stage[0:1, :], in_=bias_in),
                  outs=("stage",))
            v(lambda e: e.tensor_scalar(bias_bc[0:1, :], stage[0:1, :],
                                        scal[0:1, 11:12], 0.5, OP.mult,
                                        OP.mult),
              ins=("stage", "ideq"), outs=("bias_bc",))
            for tt in range(NT):
                v(lambda e, tt=tt: e.tensor_tensor(
                    stats[:, 48+tt:49+tt], stats[:, 32+tt:33+tt], scol[:, 1:2],
                    OP.mult),
                  ins=(f"rstd{tt}", "scol1"), outs=(f"rsts{tt}",))
                v(lambda e, tt=tt: e.tensor_tensor(
                    stats[:, 96+tt:97+tt], stats[:, 64+tt:65+tt], scol[:, 1:2],
                    OP.mult),
                  ins=(f"nmr{tt}", "scol1"), outs=(f"nms{tt}",))
            W2 = TPC * D

            def lanes(c):
                par = c % 2
                Y, Ykey = (v_tok, "v_tok") if par == 0 else (sk_tok, "sk_tok")
                Bq = a_s if par == 0 else ln_tok
                BKEYS = ("a_s",) if par == 0 else ("lnbuf", "lnbufB")
                Td, Tkey = (lnT, "lnT") if par == 0 else (qT, "qT")
                if par == 0:
                    pslots = [[(ps_sc, "ps_scA", 0), (ps_sc, "ps_scB", 512)],
                              [(ps_a, "ps_aA", 0), (ps_a, "ps_aB", 512)]]
                else:
                    pslots = [[(ps_m, "ps_mA", 0), (ps_m, "ps_mB", 512)],
                              [(ps_w1, "psw1", 0), (ps_w2, "psw2", 0)]]
                return Y, Ykey, Bq, BKEYS, Td, Tkey, pslots

            def q_phase(c):
                Y, Ykey, Bq, BKEYS, Td, Tkey, pslots = lanes(c)
                for j2 in range(TPC):
                    tt = c * TPC + j2
                    a(lambda e, tt=tt, j2=j2, Y=Y: e.activation(
                        Y[:, j2*D:(j2+1)*D], x_tok[:, tt*D:(tt+1)*D], AF.Identity,
                        bias=stats[:, 96+tt:97+tt], scale=stats[:, 48+tt:49+tt]),
                      ins=(xq(tt), f"rsts{tt}", f"nms{tt}"), outs=(Ykey,))
                v(lambda e, Y=Y, Bq=Bq: e.tensor_scalar(
                    Bq[:, 0:W2].bitcast(U16), Y[:, 0:W2].bitcast(U16),
                    0x7F80, None, OP.bitwise_and),
                  ins=(Ykey,), outs=BKEYS)
                v(lambda e, Bq=Bq: e.tensor_scalar(
                    Bq[:, 0:W2], Bq[:, 0:W2], 24.0, 0.375, OP.mult, OP.max),
                  ins=BKEYS, outs=BKEYS)
                v(lambda e, Y=Y, Bq=Bq: e.tensor_add(Y[:, 0:W2], Y[:, 0:W2],
                                                     Bq[:, 0:W2]),
                  ins=(Ykey,) + BKEYS, outs=(Ykey,))
                v(lambda e, Y=Y, Bq=Bq: e.tensor_sub(Y[:, 0:W2], Y[:, 0:W2],
                                                     Bq[:, 0:W2]),
                  ins=(Ykey,) + BKEYS, outs=(Ykey,))
                for j2 in range(TPC):
                    p.dma(lambda e, Y=Y, Td=Td, j2=j2: e.dma_start_transpose(
                        out=Td.ap().rearrange("p (m tt) -> p m tt",
                                              tt=CH)[:, :, 128*j2:128*j2+128],
                        in_=Y[:, j2*D:(j2+1)*D]),
                        ins=(Ykey,), outs=(Tkey,), xpose=True)

            def m_phase(c):
                Y, Ykey, Bq, BKEYS, Td, Tkey, pslots = lanes(c)
                for j2 in range(TPC):
                    tt = c * TPC + j2
                    scrX, xk0, xk1 = ((scr1, "scr1", "scr1b") if j2 % 2 == 0
                                      else (scr2, "scr2", "scr2b"))
                    for oh in range(2):
                        pst, pkey, off = pslots[j2][oh]
                        for k in range(8):
                            t(lambda e, j2=j2, oh=oh, k=k, pst=pst, off=off,
                              Td=Td: e.matmul(
                                pst[:, off:off+512],
                                Td[:, k*CH+128*j2:k*CH+128*j2+128],
                                wT["f"][:, k*D+oh*512:k*D+(oh+1)*512],
                                start=(k == 0), stop=False, **MM),
                              ins=(Tkey, "wfT"), outs=(pkey,))
                        t(lambda e, oh=oh, pst=pst, off=off: e.matmul(
                            pst[:, off:off+512], ones1[:],
                            bias_bc[0:1, oh*512:(oh+1)*512],
                            start=False, stop=True, **MM),
                          ins=("ones1", "bias_bc"), outs=(pkey,))
                        a(lambda e, oh=oh, pst=pst, off=off, scrX=scrX:
                          e.activation(
                            scrX[:, oh*512:(oh+1)*512], pst[:, off:off+512],
                            AF.Identity, scale=scol[:, 2:3]),
                          ins=(pkey, "scol2"),
                          outs=(xk0 if oh == 0 else xk1,))
                    g(lambda e, tt=tt, scrX=scrX: e.tensor_add(
                        x_tok[:, tt*D:(tt+1)*D], x_tok[:, tt*D:(tt+1)*D],
                        scrX[:, 0:D]),
                      ins=(xk0, xk1, xq(tt)), outs=(xq(tt),))

            pb_iter = list(prep_blocks)
            per = -(-len(pb_iter) // NCH) if pb_iter else 0

            def emit_some(c):
                for nn, jj in pb_iter[c*per:(c+1)*per]:
                    emit_prep_block(nn, jj)

            def store_chunk(c):
                lo, hi = c * TPC, (c + 1) * TPC
                p.dma(lambda e, lo=lo, hi=hi: e.dma_start(
                    out=yv[:, lo:hi, :],
                    in_=x_tok[:, lo*D:hi*D].rearrange("p (tt d) -> p tt d", d=D)),
                    ins=(xq(lo), xq(lo + 1)), outs=(f"y{lo}",))

            depth = min(2, NCH)
            for c in range(depth):
                q_phase(c)
            for c in range(depth, NCH):
                m_phase(c - depth)
                emit_some(c - depth)
                if store:
                    store_chunk(c - depth)
                q_phase(c)
            for c in range(NCH - depth, NCH):
                m_phase(c)
                emit_some(c)
                if store:
                    store_chunk(c)

        wfb = wT["f"]

        def quartet(c):
            if c % 2 == 0:
                return (sqT.ap(), "sqT", skT.ap(), "skT", sk_tok.ap(),
                        "sk_tok", v_tok.ap(), "v_tok")
            return (wfb[:, 0:2048], "sqTo", wfb[:, 2048:4096], "skTo",
                    wfb[:, 4096:6144], "sk_toko", wfb[:, 6144:8192], "v_toko")

        def acol(h):
            return (h % 2) * 512 + 64 * (h // 2)

        def attn_front_parts(c):
            sq, sqk, sk, skk, skt, sktk, vt, vtk = quartet(c)
            pwi = [0]

            def grp():
                r = [(ps_w1, "psw1"), (ps_w2, "psw2")][pwi[0] % 2]
                pwi[0] += 1
                return r

            def sigma(srcT, skey, dst, dkey):
                for hh in range(2):
                    sl = (slice(None), slice(hh*1024, (hh+1)*1024))
                    xb = ln_tok[:, hh*1024:(hh+1)*1024]
                    xbk = "lnbuf" if hh == 0 else "lnbufB"
                    v(lambda e, sl=sl, srcT=srcT, xb=xb: e.tensor_scalar(
                        xb, srcT[sl], 0.0, None, OP.min),
                      ins=(skey,), outs=(xbk,))
                    a(lambda e, xb=xb: e.activation(xb, xb, AF.Exp),
                      ins=(xbk,), outs=(xbk,))
                    v(lambda e, sl=sl, srcT=srcT, dst=dst, xb=xb:
                      e.scalar_tensor_tensor(
                        dst[sl], srcT[sl], 1.0, xb, OP.add, OP.max),
                      ins=(skey, xbk), outs=(dkey,))

            def proj_fm(wtag, wkey, dst, dkey):
                for oj in range(8):
                    pb, pbk = grp()
                    for k in range(8):
                        t(lambda e, oj=oj, k=k, pb=pb, wtag=wtag: e.matmul(
                            pb[:, 0:CH], wT[wtag][:, k*D+128*oj:k*D+128*oj+128],
                            lnT[:, k*CH:(k+1)*CH], start=(k == 0),
                            stop=(k == 7), **MM),
                          ins=("lnT", wkey), outs=(pbk,))
                    if oj % 2 == 0:
                        a(lambda e, oj=oj, pb=pb, dst=dst: e.activation(
                            dst[:, oj*CH:(oj+1)*CH], pb[:, 0:CH], AF.Copy),
                          ins=(pbk,), outs=(dkey,))
                    else:
                        v(lambda e, oj=oj, pb=pb, dst=dst: e.tensor_copy(
                            dst[:, oj*CH:(oj+1)*CH], pb[:, 0:CH]),
                          ins=(pbk,), outs=(dkey,))

            def p0():
                for j in range(TPC):
                    tt = c * TPC + j
                    ln_stats(tt)
                    ln_tile(tt, j)
                for j in range(TPC):
                    p.dma(lambda e, j=j: e.dma_start_transpose(
                        out=lnT.ap().rearrange("p (m tt) -> p m tt",
                                               tt=CH)[:, :, 128*j:128*j+128],
                        in_=ln_tok[:, j*D:(j+1)*D]),
                        ins=("lnbuf" if j == 0 else "lnbufB",), outs=("lnT",),
                        xpose=True)

            def p1():
                proj_fm("q", "wqT", qT, "qT")
                sigma(qT, "qT", sq, sqk)

            def p2():
                proj_fm("k", "wkT", kT, "kT")
                sigma(kT, "kT", sk, skk)

            def p3():
                for j in range(TPC):
                    for oh in range(2):
                        pb, pbk = grp()
                        for k in range(8):
                            t(lambda e, j=j, oh=oh, k=k, pb=pb: e.matmul(
                                pb[:, 0:512],
                                lnT[:, k*CH+128*j:k*CH+128*j+128],
                                wT["v"][:, k*D+oh*512:k*D+(oh+1)*512],
                                start=(k == 0), stop=(k == 7), **MM),
                              ins=("lnT", "wvT"), outs=(pbk,))
                        a(lambda e, j=j, oh=oh, pb=pb, vt=vt: e.activation(
                            vt[:, j*D+oh*512:j*D+(oh+1)*512], pb[:, 0:512],
                            AF.Copy),
                          ins=(pbk,), outs=(vtk,))
                for k in range(8):
                    p.dma(lambda e, k=k, sk=sk, skt=skt: e.dma_start_transpose(
                        out=skt[:, 0:TPC*D].rearrange(
                            "p (j dd) -> p j dd", dd=D)[:, :, 128*k:128*k+128],
                        in_=sk[:, k*CH:(k+1)*CH]),
                        ins=(skk,), outs=(sktk,), xpose=True)
                pb, pbk = grp()
                for j in range(TPC):
                    for k in range(8):
                        t(lambda e, j=j, k=k, pb=pb, skt=skt: e.matmul(
                            pb[:, j*16+2*k:j*16+2*k+2],
                            skt[:, j*D+128*k:j*D+128*k+128], oneh2[:],
                            start=True, stop=True, **MM),
                          ins=(sktk, "oneh2"), outs=(pbk,))
                v(lambda e, pb=pb: e.tensor_copy(csum_f[:, 0:TPC*16],
                                                 pb[:, 0:TPC*16]),
                  ins=(pbk,), outs=("csum_f",))
                for s in range(2 * TPC):
                    j, sg = divmod(s, 2)
                    e0 = j * 4 + sg * 2
                    v(lambda e, e0=e0: e.tensor_copy(
                        z2[0:64, :]
                        .rearrange("p (kk eight) -> p kk eight",
                                   eight=8)[:, :, e0:e0+1],
                        zfm[0:64, :].rearrange("p (kk o) -> p kk o", o=1)),
                      ins=("zfm",), outs=("z2",))
                    v(lambda e, e0=e0: e.tensor_copy(
                        z2[64:128, :]
                        .rearrange("p (kk eight) -> p kk eight",
                                   eight=8)[:, :, e0+1:e0+2],
                        zfm[64:128, :].rearrange("p (kk o) -> p kk o", o=1)),
                      ins=("zfm",), outs=("z2",))
                    v(lambda e, j=j, sg=sg: e.tensor_add(
                        zfm[:].rearrange("p (kk o) -> p kk o", o=1),
                        zfm[:].rearrange("p (kk o) -> p kk o", o=1),
                        csum_f[:, j*16:(j+1)*16]
                        .rearrange("p (kk two) -> p kk two", two=2)[:, :, sg:sg+1]),
                      ins=("csum_f", "zfm"), outs=("zfm",))
                pb, pbk = grp()
                for j in range(TPC):
                    for k in range(8):
                        t(lambda e, j=j, k=k, pb=pb, sq=sq: e.matmul(
                            pb[:, k*8+j*4:k*8+j*4+4],
                            sq[:, k*CH+j*128:k*CH+j*128+128],
                            z2[:, k*8+j*4:k*8+j*4+4],
                            start=True, stop=True, **MM),
                          ins=(sqk, "z2"), outs=(pbk,))
                        t(lambda e, j=j, k=k, pb=pb, sk=sk: e.matmul(
                            pb[:, 64+k*8+j*4:64+k*8+j*4+4],
                            sk[:, k*CH+j*128:k*CH+j*128+128],
                            z2[:, k*8+j*4:k*8+j*4+4],
                            start=True, stop=True, **MM),
                          ins=(skk, "z2"), outs=(pbk,))
                v(lambda e, pb=pb: e.tensor_copy(dq_sb[:, 0:128],
                                                 pb[:, 0:128]),
                  ins=(pbk,), outs=("dq_sb",))
                v(lambda e: e.tensor_scalar(dq_sb[:, 0:128],
                                            dq_sb[:, 0:128], 1e-6,
                                            None, OP.add),
                  ins=("dq_sb",), outs=("dq_sb",))
                v(lambda e: e.reciprocal(rdq[:, 0:128], dq_sb[:, 0:128]),
                  ins=("dq_sb",), outs=("rdq",))
            return [p0, p1, p2, p3]

        def attn_mid(c):
            sq, sqk, sk, skk, skt, sktk, vt, vtk = quartet(c)
            for j in range(TPC):
                cp, cpk = (c1p, "c1p") if j == 0 else (c1pB, "c1pB")
                for sg, lo in ((0, 0), (1, 64)):
                    e0 = j * 4 + sg * 2
                    v(lambda e, cp=cp, lo=lo, e0=e0: e.tensor_tensor(
                        cp[lo:lo+64, :].rearrange("p (i two) -> p i two", two=2),
                        rdq[lo:lo+64, 0:64]
                        .rearrange("p (kk eight) -> p kk eight",
                                   eight=8)[:, :, e0:e0+2],
                        betac[lo:lo+64, :].rearrange("p (i two) -> p i two",
                                                     two=2),
                        OP.mult),
                      ins=("rdq", "betac"), outs=(cpk,))
                    v(lambda e, j=j, lo=lo, e0=e0: e.tensor_copy(
                        c3_sb[lo:lo+64, j*16:(j+1)*16]
                        .rearrange("p (i two) -> p i two", two=2),
                        rdq[lo:lo+64, 64:128]
                        .rearrange("p (kk eight) -> p kk eight",
                                   eight=8)[:, :, e0:e0+2]),
                      ins=("rdq",), outs=("c3_sb",))
            for j in range(TPC):
                g(lambda e, j=j, sk=sk, skt=skt: e.tensor_tensor(
                    sk[:, j*D:(j+1)*D].rearrange("p (hh kk) -> p hh kk", kk=64),
                    skt[:, j*D:(j+1)*D].rearrange("p (hh kk) -> p hh kk", kk=64),
                    c3_sb[:, j*16:(j+1)*16].rearrange("p (hh o) -> p hh o", o=1)
                    .broadcast_to([128, 16, 64]), OP.mult),
                  ins=(sktk, "c3_sb", skk), outs=(skk,))
            for j in range(TPC):
                pt, ptk = (pt_un, "pt_un") if j == 0 else (scr1.ap(), "scr1")
                pc, pck = (pt_c2, "pt_c2") if j == 0 else (scr2.ap(), "scr2")
                ptks = (ptk,) if j == 0 else ("scr1", "scr1b")
                pcks = (pck,) if j == 0 else ("scr2", "scr2b")
                for sg in range(2):
                    for par in range(2):
                        for h in range(par, H, 2):
                            t(lambda e, h=h, sg=sg, j=j: e.matmul(
                                ps_sc[sg*64:sg*64+64, acol(h):acol(h)+64],
                                kT[(h % 2)*64:(h % 2)*64+64,
                                   (h//2)*CH+j*128+sg*64:(h//2)*CH+j*128+sg*64+64],
                                qT[(h % 2)*64:(h % 2)*64+64,
                                   (h//2)*CH+j*128+sg*64:(h//2)*CH+j*128+sg*64+64],
                                start=True, stop=True,
                                tile_position=((h % 2)*64, sg*64), **MM),
                              ins=("kT", "qT"),
                              outs=("ps_scA" if par == 0 else "ps_scB",))
                for par in range(2):
                    a(lambda e, par=par, pt=pt: e.activation(
                        pt[:, par*512:(par+1)*512], ps_sc[:, par*512:(par+1)*512],
                        AF.Exp, bias=-40.0, scale=0.125),
                      ins=("ps_scA" if par == 0 else "ps_scB",), outs=ptks)
                for lo in (0, 64):
                    g(lambda e, pt=pt, lo=lo: e.affine_select(
                        pt[lo:lo+64, :].rearrange("p (hh tt) -> p hh tt", tt=64),
                        pt[lo:lo+64, :].rearrange("p (hh tt) -> p hh tt", tt=64),
                        pattern=[[0, 16], [1, 64]], compare_op=OP.is_ge, fill=0.0,
                        base=0, channel_multiplier=-1),
                      ins=ptks, outs=ptks)
                t(lambda e, pt=pt: e.matmul(ps_w1[0:2, 0:512], oneh2[:],
                                            pt[:, 0:512], start=True, stop=True,
                                            **MM),
                  ins=("oneh2",) + ptks, outs=("psw1",))
                t(lambda e, pt=pt: e.matmul(ps_w2[0:2, 0:512], oneh2[:],
                                            pt[:, 512:1024], start=True,
                                            stop=True, **MM),
                  ins=("oneh2",) + ptks, outs=("psw2",))
                for pw, pwk in ((ps_w1, "psw1"), (ps_w2, "psw2")):
                    a(lambda e, pw=pw: e.activation(pw[0:2, 0:512],
                                                    pw[0:2, 0:512], AF.Ln),
                      ins=(pwk,), outs=(pwk,))
                    a(lambda e, pw=pw: e.activation(pw[0:2, 0:512],
                                                    pw[0:2, 0:512], AF.Exp,
                                                    scale=-1.0),
                      ins=(pwk,), outs=(pwk,))
                v(lambda e: e.tensor_tensor(c2bE[:], ps_w1[0:2, 0:512], betaE[:],
                                            OP.mult),
                  ins=("psw1", "betaE"), outs=("c2pool",))
                v(lambda e: e.tensor_tensor(c2bO[:], ps_w2[0:2, 0:512], betaO[:],
                                            OP.mult),
                  ins=("psw2", "betaO"), outs=("c2pool",))
                t(lambda e: e.matmul(ps_w1[:, 0:512], bt2[:], c2bE[:],
                                     start=True, stop=True, **MM),
                  ins=("bt2", "c2pool"), outs=("psw1",))
                t(lambda e: e.matmul(ps_w2[:, 0:512], bt2[:], c2bO[:],
                                     start=True, stop=True, **MM),
                  ins=("bt2", "c2pool"), outs=("psw2",))
                v(lambda e, pt=pt, pc=pc: e.tensor_tensor(
                    pc[:, 0:512], pt[:, 0:512], ps_w1[:, 0:512], OP.mult),
                  ins=ptks + ("psw1",), outs=pcks)
                v(lambda e, pt=pt, pc=pc: e.tensor_tensor(
                    pc[:, 512:1024], pt[:, 512:1024], ps_w2[:, 0:512], OP.mult),
                  ins=ptks + ("psw2",), outs=pcks)

        def attn_back_seg(c, s):
            sq, sqk, sk, skk, skt, sktk, vt, vtk = quartet(c)
            j, sg = divmod(s, 2)
            pc, pcks = (pt_c2, ("pt_c2",)) if j == 0 \
                else (scr2.ap(), ("scr2", "scr2b"))
            cp, cpk = (c1p, "c1p") if j == 0 else (c1pB, "c1pB")
            ib, ibk = (ib_sb.ap(), "ib_sb") if s % 2 == 0 \
                else (bias_bc[:, 0:512], "bias_bc")
            for par in range(2):
                for h in range(par, H, 2):
                    t(lambda e, h=h, sg=sg, j=j, sk=sk, skt=skt: e.matmul(
                        ps_sc[(h % 2)*64:(h % 2)*64+64, 64*(h//2):64*(h//2)+64],
                        sk[sg*64:sg*64+64, j*D+64*h:j*D+64*h+64],
                        skt[sg*64:sg*64+64, j*D+64*h:j*D+64*h+64],
                        start=True, stop=True,
                        tile_position=(sg*64, (h % 2)*64), **MM),
                      ins=(skk, sktk), outs=("ps_scA",))
            v(lambda e, ib=ib: e.scalar_tensor_tensor(
                ib[:, 0:512].rearrange("p (i kk) -> p i kk", kk=64),
                ps_sc[:, 0:512].rearrange("p (i kk) -> p i kk", kk=64), -1.0,
                irep[:].rearrange("p (o kk) -> p o kk", o=1)
                .broadcast_to([128, 8, 64]),
                OP.mult, OP.add),
              ins=("ps_scA", "irep"), outs=(ibk,))
            for par in range(2):
                for h in range(par, H, 2):
                    t(lambda e, h=h, sg=sg, j=j, sq=sq: e.matmul(
                        ps_a[sg*64:sg*64+64, acol(h):acol(h)+64],
                        sq[(h % 2)*64:(h % 2)*64+64,
                           (h//2)*CH+j*128+sg*64:(h//2)*CH+j*128+sg*64+64],
                        m_sb[(h % 2)*64:(h % 2)*64+64, 64*(h//2):64*(h//2)+64],
                        start=True, stop=True,
                        tile_position=((h % 2)*64, sg*64), **MM),
                      ins=(sqk, "m_sbA" if h % 2 == 0 else "m_sbB"),
                      outs=("ps_aA" if h % 2 == 0 else "ps_aB",))
            for par in range(2):
                for h in range(par, H, 2):
                    pw = ps_w1 if h % 2 == 0 else ps_w2
                    t(lambda e, h=h, sg=sg, j=j, pc=pc, vt=vt, pw=pw: e.matmul(
                        pw[sg*64:sg*64+64, 64*(h//2):64*(h//2)+64],
                        pc[sg*64:sg*64+64, acol(h):acol(h)+64],
                        vt[sg*64:sg*64+64, j*D+64*h:j*D+64*h+64],
                        start=True, stop=True,
                        tile_position=(sg*64, sg*64), **MM),
                      ins=pcks + (vtk,),
                      outs=("psw1" if h % 2 == 0 else "psw2",))
            for par in range(2):
                asl = a_s[:, j*D:(j+1)*D].rearrange(
                    "p (i f) -> p i f", f=128)[sg*64:sg*64+64, :,
                                               par*64:par*64+64]
                v(lambda e, asl=asl, sg=sg, par=par, cp=cp: e.tensor_tensor(
                    asl,
                    ps_a[sg*64:sg*64+64, par*512:par*512+512]
                    .rearrange("p (i kk) -> p i kk", kk=64),
                    cp[sg*64:sg*64+64, :]
                    .rearrange("p (i two) -> p i two", two=2)[:, :, par:par+1]
                    .broadcast_to([64, 8, 64]),
                    OP.mult),
                  ins=("ps_aA" if par == 0 else "ps_aB", cpk), outs=("a_s",))
                pw = ps_w1 if par == 0 else ps_w2
                v(lambda e, asl=asl, sg=sg, par=par, pw=pw: e.tensor_add(
                    asl, asl,
                    pw[sg*64:sg*64+64, 0:512]
                    .rearrange("p (i kk) -> p i kk", kk=64)),
                  ins=("a_s", "psw1" if par == 0 else "psw2"),
                  outs=("a_s",))
            for par in range(2):
                for h in range(par, H, 2):
                    t(lambda e, h=h, par=par, ib=ib: e.matmul(
                        ps_m[(h % 2)*64:(h % 2)*64+64, acol(h):acol(h)+64],
                        ib[(h % 2)*64:(h % 2)*64+64, 64*(h//2):64*(h//2)+64],
                        m_sb[(h % 2)*64:(h % 2)*64+64, 64*(h//2):64*(h//2)+64],
                        start=(h == par), stop=False,
                        tile_position=((h % 2)*64, (h % 2)*64), **MM),
                      ins=(ibk, "m_sbA" if h % 2 == 0 else "m_sbB"),
                      outs=("ps_mA" if h % 2 == 0 else "ps_mB",))
            for par in range(2):
                for h in range(par, H, 2):
                    t(lambda e, h=h, sg=sg, j=j, skt=skt, vt=vt: e.matmul(
                        ps_m[(h % 2)*64:(h % 2)*64+64, acol(h):acol(h)+64],
                        skt[sg*64:sg*64+64, j*D+64*h:j*D+64*h+64],
                        vt[sg*64:sg*64+64, j*D+64*h:j*D+64*h+64],
                        start=False, stop=True,
                        tile_position=(sg*64, (h % 2)*64), **MM),
                      ins=(sktk, vtk),
                      outs=("ps_mA" if h % 2 == 0 else "ps_mB",))
            a(lambda e: e.activation(m_sb[0:64, :], ps_m[0:64, 0:512],
                                     AF.Copy),
              ins=("ps_mA",), outs=("m_sbA",))
            a(lambda e: e.activation(m_sb[64:128, :], ps_m[64:128, 512:1024],
                                     AF.Copy),
              ins=("ps_mB",), outs=("m_sbB",))

        def attn_back_tail(c):
            for j in range(TPC):
                p.dma(lambda e, j=j: e.dma_start_transpose(
                    out=asT.ap().rearrange("p (m tt) -> p m tt",
                                           tt=CH)[:, :, 128*j:128*j+128],
                    in_=a_s[:, j*D:(j+1)*D]),
                    ins=("a_s",), outs=("pt_un", "pt_c2"), xpose=True)
            for j in range(TPC):
                tt = c * TPC + j
                for oh in range(2):
                    for k in range(8):
                        t(lambda e, j=j, oh=oh, k=k: e.matmul(
                            ps_m[:, oh*512:(oh+1)*512],
                            asT[:, k*CH+128*j:k*CH+128*j+128],
                            wT["o"][:, k*D+oh*512:k*D+(oh+1)*512],
                            start=(k == 0), stop=(k == 7), **MM),
                          ins=("pt_un", "pt_c2", "woT"),
                          outs=("ps_mA" if oh == 0 else "ps_mB",))
                    v(lambda e, tt=tt, oh=oh: e.tensor_add(
                        x_tok[:, tt*D+oh*512:tt*D+(oh+1)*512],
                        x_tok[:, tt*D+oh*512:tt*D+(oh+1)*512],
                        ps_m[:, oh*512:(oh+1)*512]),
                      ins=("ps_mA" if oh == 0 else "ps_mB", xq(tt)),
                      outs=(xq(tt),))

        def rebuild_for_ffn2():
            for jp in range(4):
                wf_pair(jp, extra=("sqTo", "skTo", "sk_toko", "v_toko"))

        yv = y_out.rearrange("(tt p) d -> p tt d", p=128)
        qkvo_blocks = [(nn, jj) for nn in ("q", "k", "v", "o") for jj in range(8)]
        ffn_stats(range(NT), wf_pairs=True, qkvo=qkvo_blocks)
        ffn_amax(range(NT))
        ffn("f1", stats_done=True)
        for f in attn_front_parts(0):
            f()
        for c in range(NCH):
            attn_mid(c)
            nxt = attn_front_parts(c + 1) if c + 1 < NCH else None
            for s in range(2 * TPC):
                attn_back_seg(c, s)
                if nxt is not None and s < 4:
                    nxt[s]()
            if c == NCH - 1:
                rebuild_for_ffn2()
            attn_back_tail(c)
            ffn_stats([c * TPC + j for j in range(TPC)])
            ffn_amax([c * TPC + j for j in range(TPC)])
        ffn("f2", store=True, stats_done=True)
        p.dma_fence()

        @blk.tensor
        def _(e):
            for gi, fn in p.body["te"]:
                if TRUNC is None or gi <= TRUNC:
                    fn(e)

        @blk.vector
        def _(e):
            for gi, fn in p.body["dv"]:
                if TRUNC is None or gi <= TRUNC:
                    fn(e)

        @blk.scalar
        def _(e):
            for gi, fn in p.body["ac"]:
                if TRUNC is None or gi <= TRUNC:
                    fn(e)

        @blk.gpsimd
        def _(e):
            for gi, fn in p.body["gp"]:
                if TRUNC is None or gi <= TRUNC:
                    fn(e)

        @blk.sync
        def _(e):
            for gi, fn in p.body["sy"]:
                if TRUNC is None or gi <= TRUNC:
                    fn(e)

    return nc


def kernel(L0_attn_x, L0_attn_W_q, L0_attn_W_k, L0_attn_W_v, L0_attn_W_o,
           L0_attn_beta_param, L0_ffn_weight, L0_ffn_bias):
    x = np.ascontiguousarray(np.asarray(L0_attn_x, dtype=np.float32))
    B, T, _ = x.shape
    nc = build_program(T=T)
    wmap = {
        "wq": np.ascontiguousarray(np.asarray(L0_attn_W_q, np.float32)),
        "wk": np.ascontiguousarray(np.asarray(L0_attn_W_k, np.float32)),
        "wv": np.ascontiguousarray(np.asarray(L0_attn_W_v, np.float32)),
        "wo": np.ascontiguousarray(np.asarray(L0_attn_W_o, np.float32)),
        "wf": np.ascontiguousarray(np.asarray(L0_ffn_weight, np.float32)),
        "fbias": np.ascontiguousarray(np.asarray(L0_ffn_bias, np.float32)).reshape(1, -1),
        "beta": np.ascontiguousarray(np.asarray(L0_attn_beta_param, np.float32)).reshape(1, -1),
    }
    in_maps = [dict(wmap, x=np.ascontiguousarray(x[i])) for i in range(B)]
    res = run_bass_kernel_spmd(nc, in_maps, core_ids=list(range(B)))
    out = np.stack([np.asarray(res.results[i]["y"]) for i in range(B)], axis=0)
    return out.astype(np.float32)


# version 36
# speedup vs baseline: 1.0036x; 1.0036x over previous
from contextlib import ExitStack

import numpy as np

import concourse.bass as bass
import concourse.mybir as mybir
from concourse.bass_utils import run_bass_kernel_spmd
from concourse.alu_op_type import AluOpType as OP

F32 = mybir.dt.float32
BF16 = mybir.dt.bfloat16
U16 = mybir.dt.uint16
FP8 = mybir.dt.float8e4
AF = mybir.ActivationFunctionType
AX = mybir.AxisListType.X

D = 1024
H = 16
DK = 64
NCORES = 8
AMAX_MODE = "core"
DBG_SEGS = None
TRUNC = None
_last_counts = {}

ACT_CONSTS = [-40.0, 0.125, 1e-5, -0.5, 1.0, 0.0, -1.0, 448.0, 0.5]


class Prog:

    ENGS = ("te", "dv", "ac", "gp", "sy")

    NQ = 8
    NGQ = 4

    def __init__(self, nc):
        self.nc = nc
        self.gidx = 0
        self.body = {e: [] for e in self.ENGS}
        self.sem = {}
        self.cnt = {e: 0 for e in self.ENGS}
        for i in range(self.NQ):
            self.cnt[f"syq{i}"] = 0
        for i in range(self.NGQ):
            self.cnt[f"gpq{i}"] = 0
        self._gprot = 0
        self.waited = {e: {} for e in self.ENGS}
        self.lastw = {}
        self.readers = {}
        self._xfence = False
        self._dmarot = 0

    def alloc_sems(self, ctx):
        for s in (list(self.ENGS) + [f"syq{i}" for i in range(self.NQ)]
                  + [f"gpq{i}" for i in range(self.NGQ)]):
            self.sem[s] = ctx.enter_context(self.nc.semaphore(s))

    def _dep(self, eng, deps):
        merged = {}
        for d in deps:
            if d is None:
                continue
            s, vv = d
            if eng == "te" and s == "te":
                continue
            if merged.get(s, 0) < vv:
                merged[s] = vv
        for s, vv in sorted(merged.items()):
            if self.waited[eng].get(s, 0) >= vv:
                continue
            self.waited[eng][s] = vv
            sem = self.sem[s]
            self.body[eng].append((self.gidx, lambda e, sem=sem, vv=vv: e.wait_ge(sem, vv)))

    def op(self, eng, fn, ins=(), outs=(), sem=None, inc=1):
        deps = []
        for k in ins:
            deps.append(self.lastw.get(k))
        for k in outs:
            deps.append(self.lastw.get(k))
            for s, vv in self.readers.get(k, {}).items():
                deps.append((s, vv))
        self._dep(eng, deps)
        semname = sem or eng
        self.cnt[semname] += inc
        val = self.cnt[semname]
        semh = self.sem[semname]

        def emit(e, fn=fn, semh=semh, inc=inc):
            fn(e).then_inc(semh, inc)
        self.gidx += 1
        self.body[eng].append((self.gidx, emit))
        for k in outs:
            self.lastw[k] = (semname, val)
            self.readers[k] = {}
        for k in ins:
            r = self.readers.setdefault(k, {})
            if r.get(semname, 0) < val:
                r[semname] = val
        return (semname, val)

    def dma_fence(self):
        self._dep("sy", [(f"syq{i}", self.cnt[f"syq{i}"]) for i in range(self.NQ)]
                  + [(f"gpq{i}", self.cnt[f"gpq{i}"]) for i in range(self.NGQ)])

    def dma(self, fn, ins=(), outs=(), xpose=False, eng="sy"):
        if xpose or self._xfence:
            self.dma_fence()
            self._xfence = xpose
        if eng == "gp":
            r = self._gprot
            self._gprot = (r + 1) % self.NGQ
            sname = f"gpq{r}"
        else:
            r = self._dmarot
            self._dmarot = (r + 1) % self.NQ
            sname = f"syq{r}"
        deps = [(sname, self.cnt[sname])]
        if getattr(self, "_last_xpose", None) is not None:
            deps.append(self._last_xpose)
        self._dep(eng, deps)
        res = self.op(eng, fn, ins=ins, outs=outs, sem=sname, inc=16)
        if xpose:
            self._last_xpose = res
        return res


def build_program(T=2048, CH=256):
    NT = T // 128
    NCH = T // CH
    TPC = CH // 128
    assert NT <= 16
    nc = bass.Bass("TRN2", target_bir_lowering=False)
    p = Prog(nc)

    x_in = nc.dram_tensor("x", [T, D], F32, kind="ExternalInput").ap()
    win = {
        "q": nc.dram_tensor("wq", [D, D], F32, kind="ExternalInput").ap(),
        "k": nc.dram_tensor("wk", [D, D], F32, kind="ExternalInput").ap(),
        "v": nc.dram_tensor("wv", [D, D], F32, kind="ExternalInput").ap(),
        "o": nc.dram_tensor("wo", [D, D], F32, kind="ExternalInput").ap(),
        "f": nc.dram_tensor("wf", [D, D], F32, kind="ExternalInput").ap(),
    }
    bias_in = nc.dram_tensor("fbias", [1, D], F32, kind="ExternalInput").ap()
    beta_in = nc.dram_tensor("beta", [1, H], F32, kind="ExternalInput").ap()
    y_out = nc.dram_tensor("y", [T, D], F32, kind="ExternalOutput").ap()

    ctx = ExitStack()
    sb = lambda name, shape, dt: ctx.enter_context(nc.sbuf_tensor(name, shape, dt))
    psum = lambda name, shape: ctx.enter_context(nc.psum_tensor(name, shape, F32))

    with ctx:
        p.alloc_sems(ctx)

        x_tok = sb("x_tok", [128, NT * D], F32)

        def xq(tt):
            return f"x_t{tt}"
        wT = {n: sb(f"w{n}T", [128, 8 * D], BF16) for n in ("q", "k", "v", "o", "f")}
        stage = sb("stage", [128, D], F32)
        lnT = sb("lnT", [128, 8 * CH], BF16)
        qT = sb("qT", [128, 8 * CH], BF16)
        kT = sb("kT", [128, 8 * CH], BF16)
        sqT = sb("sqT", [128, 8 * CH], BF16)
        skT = sb("skT", [128, 8 * CH], BF16)
        sk_tok = sb("sk_tok", [128, TPC * D], BF16)
        v_tok = sb("v_tok", [128, TPC * D], BF16)
        a_s = sb("a_s", [128, TPC * D], BF16)
        ln_tok = sb("ln_tok", [128, TPC * D], BF16)
        ptbuf = sb("ptbuf", [128, 2 * D], BF16)
        pt_un = ptbuf[:, 0:D]
        pt_c2 = ptbuf[:, D:2*D]
        scr1 = sb("scr1", [128, D], BF16)
        scr2 = sb("scr2", [128, D], BF16)
        ib_sb = sb("ib_sb", [128, 512], BF16)
        m_sb = sb("m_sb", [128, 512], BF16)
        stats = sb("stats", [128, 112], F32)
        zfm = sb("zfm", [128, 8], F32)
        z2 = sb("z2", [128, 64], BF16)
        csum_f = sb("csum_f", [128, 64], F32)
        dq_sb = sb("dq_sb", [128, 128], F32)
        rdq = sb("rdq", [128, 128], F32)
        c1p = sb("c1p", [128, 16], BF16)
        c1pB = sb("c1pB", [128, 16], BF16)
        c3_sb = sb("c3_sb", [128, 64], BF16)
        c2pool = sb("c2pool", [2, 1024], BF16)
        rowred = sb("rowred", [128, 48], F32)
        rows = sb("rows", [1, 200], F32)
        scal = sb("scal", [1, 16], F32)
        scol = sb("scol", [128, 4], F32)
        betaE = sb("betaE", [2, 512], BF16)
        betaO = sb("betaO", [2, 512], BF16)
        betac = sb("betac", [128, 16], BF16)
        b4 = sb("b4", [4, 16], BF16)
        bias_bc = sb("bias_bc", [128, D], BF16)
        beta_row = sb("beta_row", [1, H], F32)
        cpool = sb("cpool", [128, 12], F32)
        ones1 = sb("ones1", [1, 128], BF16)
        ones1f = sb("ones1f", [1, 128], F32)
        idf = sb("idf", [128, 128], F32)
        idb = sb("idb", [128, 128], BF16)
        bt2 = sb("bt2", [2, 128], BF16)
        oneh2 = sb("oneh2", [128, 2], BF16)
        irep = sb("irep", [128, 64], BF16)

        stage_bf = pt_c2
        c1fm = ln_tok
        sk3_tok = skT
        asT = ptbuf
        scr3 = pt_un
        c2bE = c2pool[:, 0:512]
        c2bO = c2pool[:, 512:1024]

        ps_sc = psum("ps_sc", [128, 1024])
        ps_a = psum("ps_a", [128, 1024])
        ps_m = psum("ps_m", [128, 1024])
        ps_w1 = psum("ps_w1", [128, 512])
        ps_w2 = psum("ps_w2", [128, 512])

        for i, vv in enumerate(ACT_CONSTS):
            nc.const_aps.register(F32, float(vv), cpool[:, i:i+1])

        import sys
        print(f"[build] sbuf bytes remaining/partition: {nc.sbuf_bytes_remaining}",
              file=sys.stderr)
        blk = ctx.enter_context(nc.Block())

        def t(fn, ins=(), outs=()):
            return p.op("te", fn, ins=ins, outs=outs)

        def v(fn, ins=(), outs=()):
            return p.op("dv", fn, ins=ins, outs=outs)

        def a(fn, ins=(), outs=()):
            return p.op("ac", fn, ins=ins, outs=outs)

        def g(fn, ins=(), outs=()):
            return p.op("gp", fn, ins=ins, outs=outs)

        MM = dict(skip_group_check=True)

        for i, vv in enumerate(ACT_CONSTS):
            g(lambda e, i=i, vv=vv: e.memset(cpool[:, i:i+1], float(vv)),
              outs=(f"cp{i}",))
        g(lambda e: e.memset(ones1[:], 1.0), outs=("ones1",))
        g(lambda e: e.memset(ones1f[:], 1.0), outs=("ones1f",))
        for nm, tile in (("idf", idf), ("idb", idb)):
            g(lambda e, tile=tile: e.memset(tile[:], 1.0), outs=(nm,))
            g(lambda e, tile=tile: e.affine_select(
                tile[:], tile[:], pattern=[[-1, 128]], compare_op=OP.is_equal,
                fill=0.0, base=0, channel_multiplier=1), ins=(nm,), outs=(nm,))
        g(lambda e: e.memset(irep[:], 1.0), outs=("irep",))
        g(lambda e: e.affine_select(irep[0:64, :], irep[0:64, :], pattern=[[-1, 64]],
                                    compare_op=OP.is_equal, fill=0.0, base=0,
                                    channel_multiplier=1), ins=("irep",), outs=("irep",))
        g(lambda e: e.affine_select(irep[64:128, :], irep[64:128, :], pattern=[[-1, 64]],
                                    compare_op=OP.is_equal, fill=0.0, base=0,
                                    channel_multiplier=1), ins=("irep",), outs=("irep",))
        g(lambda e: e.memset(oneh2[:], 1.0), outs=("oneh2",))
        g(lambda e: e.affine_select(oneh2[:, 0:1], oneh2[:, 0:1], pattern=[[0, 1]],
                                    compare_op=OP.is_ge, fill=0.0, base=63,
                                    channel_multiplier=-1), ins=("oneh2",), outs=("oneh2",))
        g(lambda e: e.affine_select(oneh2[:, 1:2], oneh2[:, 1:2], pattern=[[0, 1]],
                                    compare_op=OP.is_ge, fill=0.0, base=-64,
                                    channel_multiplier=1), ins=("oneh2",), outs=("oneh2",))
        g(lambda e: e.memset(bt2[:], 1.0), outs=("bt2",))
        g(lambda e: e.affine_select(bt2[:], bt2[:], pattern=[[1, 128]],
                                    compare_op=OP.is_ge, fill=0.0, base=0,
                                    channel_multiplier=-64),
          ins=("bt2",), outs=("bt2",))
        g(lambda e: e.affine_select(bt2[:], bt2[:], pattern=[[-1, 128]],
                                    compare_op=OP.is_ge, fill=0.0, base=63,
                                    channel_multiplier=64),
          ins=("bt2",), outs=("bt2",))
        g(lambda e: e.memset(zfm[:], 0.0), outs=("zfm",))
        g(lambda e: e.memset(z2[:], 0.0), outs=("z2",))
        g(lambda e: e.memset(m_sb[:], 0.0), outs=("m_sbA", "m_sbB"))
        if DBG_SEGS is not None:
            g(lambda e: e.memset(a_s[:], 0.0), outs=("a_s",))

        xv = x_in.rearrange("(tt p) d -> p tt d", p=128)
        nsplit = min(4, NT)
        for q4 in range(nsplit):
            lo, hi = q4 * NT // nsplit, (q4 + 1) * NT // nsplit
            p.dma(lambda e, lo=lo, hi=hi: e.dma_start(
                out=x_tok[:, lo*D:hi*D].rearrange("p (tt d) -> p tt d", d=D),
                in_=xv[:, lo:hi, :]),
                outs=tuple({xq(tt) for tt in range(lo, hi)}), eng="gp")

        def quant_buf(src_ap, src_key, dst_ap, dst_key, scol_idx):
            SC1 = ("scr1", "scr1b")
            SC2 = ("scr2", "scr2b")
            a(lambda e: e.activation(
                scr1[:, 0:D], src_ap, AF.Identity, scale=scol[:, scol_idx:scol_idx+1]),
              ins=(src_key, f"scol{scol_idx}"), outs=SC1)
            v(lambda e: e.tensor_scalar(scr3[:, 0:D].bitcast(U16), scr1[:, 0:D].bitcast(U16),
                                        0x7F80, None, OP.bitwise_and),
              ins=SC1, outs=("pt_un",))
            v(lambda e: e.tensor_scalar(scr2[:, 0:D], scr3[:, 0:D],
                                        24.0, 0.375, OP.mult, OP.max),
              ins=("pt_un",), outs=SC2)
            v(lambda e: e.tensor_add(scr1[:, 0:D], scr1[:, 0:D], scr2[:, 0:D]),
              ins=SC1 + SC2, outs=SC1)
            v(lambda e: e.tensor_sub(dst_ap, scr1[:, 0:D], scr2[:, 0:D]),
              ins=SC1 + SC2, outs=(dst_key,))

        def cross_part_max(colkey, col_ap, out_idx, out_key):
            t(lambda e: e.transpose(ps_w1[0:1, 0:128], col_ap, idf[:]),
              ins=(colkey, "idf"), outs=("psw1",))
            a(lambda e: e.activation(rows[0:1, 0:128], ps_w1[0:1, 0:128], AF.Copy),
              ins=("psw1",), outs=("rows128",))
            v(lambda e: e.tensor_reduce(scal[0:1, out_idx:out_idx+1],
                                        rows[0:1, 0:128], AX, OP.max),
              ins=("rows128",), outs=(out_key,))

        def bcast_scalar(in_idx, in_key, scol_idx):
            t(lambda e: e.matmul(ps_w2[:, 0:1], ones1f[:],
                                 scal[0:1, in_idx:in_idx+1], start=True, stop=True,
                                 **MM),
              ins=(in_key, "ones1f"), outs=("psw2",))
            a(lambda e: e.activation(scol[:, scol_idx:scol_idx+1], ps_w2[:, 0:1],
                                     AF.Copy),
              ins=("psw2",), outs=(f"scol{scol_idx}",))

        kTf = kT.ap().bitcast(F32)
        lanes_f32 = [(stage[:], "stage"), (kTf, "kT")]
        lanes_bf = [(sqT[:, 0:1024], "sqT"), (sqT[:, 1024:2048], "sqT")]

        def emit_prep_block(n, j):
            wsrc = win[n].rearrange("(j p) i -> p j i", p=128)
            ldst, lkey = lanes_f32[j % 2]
            conv, ckey = lanes_bf[j % 2]
            p.dma(lambda e, j=j, wsrc=wsrc, ldst=ldst: e.dma_start(
                out=ldst, in_=wsrc[:, j, :]), outs=(lkey,), eng="gp")
            if n == "f":
                quant_buf(ldst, lkey, conv, ckey, 0)
            else:
                v(lambda e, ldst=ldst, conv=conv: e.tensor_copy(conv, ldst),
                  ins=(lkey,), outs=(ckey,))
            pb, pbkey = (ps_w1, "psw1") if j % 2 == 0 else (ps_w2, "psw2")
            for cidx in range(8):
                t(lambda e, cidx=cidx, pb=pb, conv=conv: e.transpose(
                    pb[:, cidx*64:(cidx+1)*64].bitcast(BF16),
                    conv[:, cidx*128:(cidx+1)*128], idb[:]),
                  ins=(ckey, "idb"), outs=(pbkey,))
            if j % 2 == 0:
                v(lambda e, n=n, j=j, pb=pb: e.tensor_copy(
                    wT[n].ap().rearrange("p (c o) -> p c o", o=D)
                    [:, :, 128*j:128*j+128],
                    pb[:, 0:512].bitcast(BF16).rearrange(
                        "p (c o) -> p c o", o=128)),
                    ins=(pbkey,), outs=(f"w{n}T",))
            else:
                a(lambda e, n=n, j=j, pb=pb: e.activation(
                    wT[n].ap().rearrange("p (c o) -> p c o", o=D)
                    [:, :, 128*j:128*j+128],
                    pb[:, 0:512].bitcast(BF16).rearrange(
                        "p (c o) -> p c o", o=128),
                    AF.Copy),
                    ins=(pbkey,), outs=(f"w{n}T",))

        def prep_wf():
            wsrc = win["f"].rearrange("(j p) i -> p j i", p=128)
            for j in range(8):
                ldst, lkey = lanes_f32[j % 2]
                p.dma(lambda e, j=j, ldst=ldst: e.dma_start(
                    out=ldst, in_=wsrc[:, j, :]), outs=(lkey,), eng="gp")
                v(lambda e, j=j, ldst=ldst: e.tensor_reduce(
                    rowred[:, j:j+1], ldst, AX, OP.max,
                    apply_absolute_value=True), ins=(lkey,), outs=(f"wr{j}",))
            v(lambda e: e.tensor_reduce(rowred[:, 8:9], rowred[:, 0:8], AX, OP.max),
              ins=tuple(f"wr{j}" for j in range(8)), outs=("wamax_col",))
            cross_part_max("wamax_col", rowred[:, 8:9], 0, "wamax")
            v(lambda e: e.tensor_scalar(scal[0:1, 1:2], scal[0:1, 0:1], 1e-12,
                                        None, OP.max), ins=("wamax",), outs=("wt1",))
            v(lambda e: e.reciprocal(scal[0:1, 2:3], scal[0:1, 1:2]),
              ins=("wt1",), outs=("wt2",))
            v(lambda e: e.tensor_scalar(scal[0:1, 3:4], scal[0:1, 2:3], 448.0,
                                        448.0, OP.mult, OP.min),
              ins=("wt2",), outs=("s_w",))
            v(lambda e: e.reciprocal(scal[0:1, 4:5], scal[0:1, 3:4]),
              ins=("s_w",), outs=("rsw",))
            v(lambda e: e.tensor_scalar(scal[0:1, 4:5], scal[0:1, 4:5],
                                        0.5, None, OP.mult),
              ins=("rsw",), outs=("rsw",))
            bcast_scalar(3, "s_w", 0)

        def wf_pair(jp, extra=()):
            wsrc = win["f"].rearrange("(j p) i -> p j i", p=128)
            if True:
                j0 = 2 * jp
                p.dma(lambda e, j0=j0: e.dma_start(
                    out=stage[:], in_=wsrc[:, j0, :]), outs=("stage",),
                    eng="gp")
                p.dma(lambda e, j0=j0: e.dma_start(
                    out=kTf, in_=wsrc[:, j0 + 1, :]), outs=("kT",), eng="gp")
                a(lambda e: e.activation(sqT[:, 0:1024], stage[:], AF.Identity,
                                         scale=scol[:, 0:1]),
                  ins=("stage", "scol0"), outs=("sqT",))
                a(lambda e: e.activation(sqT[:, 1024:2048], kTf, AF.Identity,
                                         scale=scol[:, 0:1]),
                  ins=("kT", "scol0"), outs=("sqT",))
                v(lambda e: e.tensor_scalar(
                    skT[:, 0:2048].bitcast(U16), sqT[:, 0:2048].bitcast(U16),
                    0x7F80, None, OP.bitwise_and),
                  ins=("sqT",), outs=("skT",))
                v(lambda e: e.tensor_scalar(skT[:, 0:2048], skT[:, 0:2048],
                                            24.0, 0.375, OP.mult, OP.max),
                  ins=("skT",), outs=("skT",))
                v(lambda e: e.tensor_add(sqT[:, 0:2048], sqT[:, 0:2048],
                                         skT[:, 0:2048]),
                  ins=("sqT", "skT"), outs=("sqT",))
                v(lambda e: e.tensor_sub(sqT[:, 0:2048], sqT[:, 0:2048],
                                         skT[:, 0:2048]),
                  ins=("sqT", "skT"), outs=("sqT",))
                for dj in range(2):
                    j = j0 + dj
                    pb, pbkey = (ps_w1, "psw1") if dj == 0 else (ps_w2, "psw2")
                    for cidx in range(8):
                        t(lambda e, dj=dj, cidx=cidx, pb=pb: e.transpose(
                            pb[:, cidx*64:(cidx+1)*64].bitcast(BF16),
                            sqT[:, dj*1024+cidx*128:dj*1024+(cidx+1)*128],
                            idb[:]),
                          ins=("sqT", "idb"), outs=(pbkey,))
                    if dj == 0:
                        v(lambda e, j=j, pb=pb: e.tensor_copy(
                            wT["f"].ap().rearrange("p (c o) -> p c o", o=D)
                            [:, :, 128*j:128*j+128],
                            pb[:, 0:512].bitcast(BF16).rearrange(
                                "p (c o) -> p c o", o=128)),
                            ins=(pbkey,), outs=("wfT",) + extra)
                    else:
                        a(lambda e, j=j, pb=pb: e.activation(
                            wT["f"].ap().rearrange("p (c o) -> p c o", o=D)
                            [:, :, 128*j:128*j+128],
                            pb[:, 0:512].bitcast(BF16).rearrange(
                                "p (c o) -> p c o", o=128),
                            AF.Copy),
                            ins=(pbkey,), outs=("wfT",) + extra)

        prep_wf()

        p.dma(lambda e: e.dma_start(out=beta_row[:], in_=beta_in), outs=("beta_row",))
        a(lambda e: e.activation(rows[0:1, 144:144+H], beta_row[0:1, :], AF.Exp, scale=-1.0),
          ins=("beta_row",), outs=("bexp",))
        v(lambda e: e.tensor_scalar(rows[0:1, 144:144+H], rows[0:1, 144:144+H], 1.0, None, OP.add),
          ins=("bexp",), outs=("bexp",))
        v(lambda e: e.reciprocal(rows[0:1, 160:160+H], rows[0:1, 144:144+H]),
          ins=("bexp",), outs=("beta_v",))
        v(lambda e: e.tensor_scalar(rows[0:1, 176:176+H], rows[0:1, 160:160+H], -1.0, 1.0,
                                    OP.mult, OP.add),
          ins=("beta_v",), outs=("bomb",))
        t(lambda e: e.matmul(ps_w1[0:4, 0:H], ones1f[0:1, 0:4], rows[0:1, 176:176+H],
                             start=True, stop=True, **MM),
          ins=("bomb", "ones1f"), outs=("psw1",))
        v(lambda e: e.tensor_copy(b4[:], ps_w1[0:4, 0:H]), ins=("psw1",), outs=("b4",))
        v(lambda e: e.tensor_copy(
            betaE[:].rearrange("p (j tt) -> p j tt", tt=64),
            b4[0:2, 0:16].rearrange("p (j s) -> p j s", s=2)[:, :, 0:1]
            .broadcast_to([2, 8, 64])),
          ins=("b4",), outs=("betaE",))
        v(lambda e: e.tensor_copy(
            betaO[:].rearrange("p (j tt) -> p j tt", tt=64),
            b4[0:2, 0:16].rearrange("p (j s) -> p j s", s=2)[:, :, 1:2]
            .broadcast_to([2, 8, 64])),
          ins=("b4",), outs=("betaO",))
        t(lambda e: e.matmul(ps_w2[:, 0:H], ones1f[:], rows[0:1, 160:160+H],
                             start=True, stop=True, **MM),
          ins=("beta_v", "ones1f"), outs=("psw2",))
        v(lambda e: e.tensor_copy(betac[:, 0:16], ps_w2[:, 0:H]),
          ins=("psw2",), outs=("betac",))

        def ln_stats(tt):
            xs = x_tok[:, tt*D:(tt+1)*D]
            v(lambda e, xs=xs, tt=tt: e.tensor_reduce(stats[:, tt:tt+1], xs, AX, OP.add),
              ins=(xq(tt),), outs=(f"sum{tt}",))
            a(lambda e, xs=xs, tt=tt: e.activation(
                scr1[:, 0:D], xs, AF.Square, accum_out=stats[:, 16+tt:17+tt]),
              ins=(xq(tt),), outs=("scr1", "scr1b", f"sq{tt}"))
            v(lambda e, tt=tt: e.tensor_scalar(stats[:, tt:tt+1], stats[:, tt:tt+1],
                                               1.0/D, None, OP.mult),
              ins=(f"sum{tt}",), outs=(f"m{tt}",))
            v(lambda e, tt=tt: e.tensor_scalar(stats[:, 16+tt:17+tt],
                                               stats[:, 16+tt:17+tt], 1.0/D, None,
                                               OP.mult),
              ins=(f"sq{tt}",), outs=(f"sq{tt}",))
            v(lambda e, tt=tt: e.tensor_tensor(stats[:, 48+tt:49+tt], stats[:, tt:tt+1],
                                               stats[:, tt:tt+1], OP.mult),
              ins=(f"m{tt}",), outs=(f"msq{tt}",))
            v(lambda e, tt=tt: e.tensor_sub(stats[:, 16+tt:17+tt],
                                            stats[:, 16+tt:17+tt],
                                            stats[:, 48+tt:49+tt]),
              ins=(f"sq{tt}", f"msq{tt}"), outs=(f"var{tt}",))
            a(lambda e, tt=tt: e.activation(stats[:, 32+tt:33+tt],
                                            stats[:, 16+tt:17+tt], AF.Ln, bias=1e-5),
              ins=(f"var{tt}",), outs=(f"lnv{tt}",))
            a(lambda e, tt=tt: e.activation(stats[:, 32+tt:33+tt],
                                            stats[:, 32+tt:33+tt], AF.Exp, scale=-0.5),
              ins=(f"lnv{tt}",), outs=(f"rstd{tt}",))
            v(lambda e, tt=tt: e.tensor_scalar(
                stats[:, 64+tt:65+tt], stats[:, tt:tt+1],
                stats[:, 32+tt:33+tt], -1.0, OP.mult, OP.mult),
              ins=(f"m{tt}", f"rstd{tt}"), outs=(f"nmr{tt}",))

        def ln_tile(tt, j):
            a(lambda e, tt=tt, j=j: e.activation(
                ln_tok[:, j*D:(j+1)*D], x_tok[:, tt*D:(tt+1)*D], AF.Identity,
                bias=stats[:, 64+tt:65+tt], scale=stats[:, 32+tt:33+tt]),
              ins=(xq(tt), f"rstd{tt}", f"nmr{tt}"),
              outs=("lnbuf" if j == 0 else "lnbufB",))

        def ffn_stats(tiles, wf_pairs=False, qkvo=()):
            qk = list(qkvo)
            for tt in tiles:
                ln_stats(tt)
                if wf_pairs and tt % 4 == 3:
                    wf_pair(tt // 4)
                for nn, jj in qk[tt*2:tt*2+2]:
                    emit_prep_block(nn, jj)
            if wf_pairs:
                for jp in range((max(tiles) + 1) // 4, 4):
                    wf_pair(jp)
            for nn, jj in qk[2*(max(tiles)+1 if tiles else 0):]:
                emit_prep_block(nn, jj)

        def ffn_amax(tiles):
            for tt in tiles:
                v(lambda e, tt=tt: e.tensor_reduce(
                    rowred[:, 16+tt:17+tt], x_tok[:, tt*D:(tt+1)*D], AX, OP.max),
                  ins=(xq(tt),), outs=(f"amx{tt}",))
                v(lambda e, tt=tt: e.tensor_reduce(
                    rowred[:, 32+tt:33+tt], x_tok[:, tt*D:(tt+1)*D], AX, OP.min),
                  ins=(xq(tt),), outs=(f"ami{tt}",))
                v(lambda e, tt=tt: e.tensor_sub(
                    rowred[:, 16+tt:17+tt], rowred[:, 16+tt:17+tt],
                    stats[:, tt:tt+1]),
                  ins=(f"amx{tt}", f"m{tt}"), outs=(f"amx{tt}",))
                v(lambda e, tt=tt: e.tensor_sub(
                    rowred[:, 32+tt:33+tt], stats[:, tt:tt+1],
                    rowred[:, 32+tt:33+tt]),
                  ins=(f"ami{tt}", f"m{tt}"), outs=(f"ami{tt}",))
                v(lambda e, tt=tt: e.tensor_max(
                    rowred[:, 16+tt:17+tt], rowred[:, 16+tt:17+tt],
                    rowred[:, 32+tt:33+tt]),
                  ins=(f"amx{tt}", f"ami{tt}"), outs=(f"amx{tt}",))
                v(lambda e, tt=tt: e.tensor_mul(
                    rowred[:, 16+tt:17+tt], rowred[:, 16+tt:17+tt],
                    stats[:, 32+tt:33+tt]),
                  ins=(f"amx{tt}", f"rstd{tt}"), outs=(f"am{tt}",))

        def ffn(tag, prep_blocks=(), store=False, stats_done=False):
            if not stats_done:
                ffn_stats(range(NT))
                ffn_amax(range(NT))
            v(lambda e: e.tensor_reduce(rowred[:, 0:1], rowred[:, 16:16+NT], AX,
                                        OP.max),
              ins=tuple(f"am{tt}" for tt in range(NT)), outs=("amax_col",))
            cross_part_max("amax_col", rowred[:, 0:1], 5, "amax")
            v(lambda e: e.tensor_scalar(scal[0:1, 6:7], scal[0:1, 5:6], 1e-12, None,
                                        OP.max), ins=("amax",), outs=("am1",))
            v(lambda e: e.reciprocal(scal[0:1, 7:8], scal[0:1, 6:7]),
              ins=("am1",), outs=("am2",))
            v(lambda e: e.tensor_scalar(scal[0:1, 8:9], scal[0:1, 7:8], 448.0, 448.0,
                                        OP.mult, OP.min), ins=("am2",), outs=("s_x",))
            v(lambda e: e.reciprocal(scal[0:1, 9:10], scal[0:1, 8:9]),
              ins=("s_x",), outs=("rsx",))
            v(lambda e: e.tensor_tensor(scal[0:1, 10:11], scal[0:1, 9:10],
                                        scal[0:1, 4:5], OP.mult),
              ins=("rsx", "rsw"), outs=("deq",))
            bcast_scalar(8, "s_x", 1)
            bcast_scalar(10, "deq", 2)
            v(lambda e: e.reciprocal(scal[0:1, 11:12], scal[0:1, 10:11]),
              ins=("deq",), outs=("ideq",))
            p.dma(lambda e: e.dma_start(out=stage[0:1, :], in_=bias_in),
                  outs=("stage",))
            v(lambda e: e.tensor_scalar(bias_bc[0:1, :], stage[0:1, :],
                                        scal[0:1, 11:12], 0.5, OP.mult,
                                        OP.mult),
              ins=("stage", "ideq"), outs=("bias_bc",))
            for tt in range(NT):
                v(lambda e, tt=tt: e.tensor_tensor(
                    stats[:, 48+tt:49+tt], stats[:, 32+tt:33+tt], scol[:, 1:2],
                    OP.mult),
                  ins=(f"rstd{tt}", "scol1"), outs=(f"rsts{tt}",))
                v(lambda e, tt=tt: e.tensor_tensor(
                    stats[:, 96+tt:97+tt], stats[:, 64+tt:65+tt], scol[:, 1:2],
                    OP.mult),
                  ins=(f"nmr{tt}", "scol1"), outs=(f"nms{tt}",))
            W2 = TPC * D

            def lanes(c):
                par = c % 2
                Y, Ykey = (v_tok, "v_tok") if par == 0 else (sk_tok, "sk_tok")
                Bq = a_s if par == 0 else ln_tok
                BKEYS = ("a_s",) if par == 0 else ("lnbuf", "lnbufB")
                Td, Tkey = (lnT, "lnT") if par == 0 else (qT, "qT")
                if par == 0:
                    pslots = [[(ps_sc, "ps_scA", 0), (ps_sc, "ps_scB", 512)],
                              [(ps_a, "ps_aA", 0), (ps_a, "ps_aB", 512)]]
                else:
                    pslots = [[(ps_m, "ps_mA", 0), (ps_m, "ps_mB", 512)],
                              [(ps_w1, "psw1", 0), (ps_w2, "psw2", 0)]]
                return Y, Ykey, Bq, BKEYS, Td, Tkey, pslots

            def q_phase(c):
                Y, Ykey, Bq, BKEYS, Td, Tkey, pslots = lanes(c)
                for j2 in range(TPC):
                    tt = c * TPC + j2
                    a(lambda e, tt=tt, j2=j2, Y=Y: e.activation(
                        Y[:, j2*D:(j2+1)*D], x_tok[:, tt*D:(tt+1)*D], AF.Identity,
                        bias=stats[:, 96+tt:97+tt], scale=stats[:, 48+tt:49+tt]),
                      ins=(xq(tt), f"rsts{tt}", f"nms{tt}"), outs=(Ykey,))
                v(lambda e, Y=Y, Bq=Bq: e.tensor_scalar(
                    Bq[:, 0:W2].bitcast(U16), Y[:, 0:W2].bitcast(U16),
                    0x7F80, None, OP.bitwise_and),
                  ins=(Ykey,), outs=BKEYS)
                v(lambda e, Bq=Bq: e.tensor_scalar(
                    Bq[:, 0:W2], Bq[:, 0:W2], 24.0, 0.375, OP.mult, OP.max),
                  ins=BKEYS, outs=BKEYS)
                v(lambda e, Y=Y, Bq=Bq: e.tensor_add(Y[:, 0:W2], Y[:, 0:W2],
                                                     Bq[:, 0:W2]),
                  ins=(Ykey,) + BKEYS, outs=(Ykey,))
                v(lambda e, Y=Y, Bq=Bq: e.tensor_sub(Y[:, 0:W2], Y[:, 0:W2],
                                                     Bq[:, 0:W2]),
                  ins=(Ykey,) + BKEYS, outs=(Ykey,))
                for j2 in range(TPC):
                    p.dma(lambda e, Y=Y, Td=Td, j2=j2: e.dma_start_transpose(
                        out=Td.ap().rearrange("p (m tt) -> p m tt",
                                              tt=CH)[:, :, 128*j2:128*j2+128],
                        in_=Y[:, j2*D:(j2+1)*D]),
                        ins=(Ykey,), outs=(Tkey,), xpose=True)

            def m_phase(c):
                Y, Ykey, Bq, BKEYS, Td, Tkey, pslots = lanes(c)
                for j2 in range(TPC):
                    tt = c * TPC + j2
                    scrX, xk0, xk1 = ((scr1, "scr1", "scr1b") if j2 % 2 == 0
                                      else (scr2, "scr2", "scr2b"))
                    for oh in range(2):
                        pst, pkey, off = pslots[j2][oh]
                        for k in range(8):
                            t(lambda e, j2=j2, oh=oh, k=k, pst=pst, off=off,
                              Td=Td: e.matmul(
                                pst[:, off:off+512],
                                Td[:, k*CH+128*j2:k*CH+128*j2+128],
                                wT["f"][:, k*D+oh*512:k*D+(oh+1)*512],
                                start=(k == 0), stop=False, **MM),
                              ins=(Tkey, "wfT"), outs=(pkey,))
                        t(lambda e, oh=oh, pst=pst, off=off: e.matmul(
                            pst[:, off:off+512], ones1[:],
                            bias_bc[0:1, oh*512:(oh+1)*512],
                            start=False, stop=True, **MM),
                          ins=("ones1", "bias_bc"), outs=(pkey,))
                        a(lambda e, oh=oh, pst=pst, off=off, scrX=scrX:
                          e.activation(
                            scrX[:, oh*512:(oh+1)*512], pst[:, off:off+512],
                            AF.Identity, scale=scol[:, 2:3]),
                          ins=(pkey, "scol2"),
                          outs=(xk0 if oh == 0 else xk1,))
                    g(lambda e, tt=tt, scrX=scrX: e.tensor_add(
                        x_tok[:, tt*D:(tt+1)*D], x_tok[:, tt*D:(tt+1)*D],
                        scrX[:, 0:D]),
                      ins=(xk0, xk1, xq(tt)), outs=(xq(tt),))

            pb_iter = list(prep_blocks)
            per = -(-len(pb_iter) // NCH) if pb_iter else 0

            def emit_some(c):
                for nn, jj in pb_iter[c*per:(c+1)*per]:
                    emit_prep_block(nn, jj)

            def store_chunk(c):
                lo, hi = c * TPC, (c + 1) * TPC
                p.dma(lambda e, lo=lo, hi=hi: e.dma_start(
                    out=yv[:, lo:hi, :],
                    in_=x_tok[:, lo*D:hi*D].rearrange("p (tt d) -> p tt d", d=D)),
                    ins=(xq(lo), xq(lo + 1)), outs=(f"y{lo}",))

            depth = min(2, NCH)
            for c in range(depth):
                q_phase(c)
            for c in range(depth, NCH):
                m_phase(c - depth)
                emit_some(c - depth)
                if store:
                    store_chunk(c - depth)
                q_phase(c)
            for c in range(NCH - depth, NCH):
                m_phase(c)
                emit_some(c)
                if store:
                    store_chunk(c)

        wfb = wT["f"]

        def quartet(c):
            if c % 2 == 0:
                return (sqT.ap(), "sqT", skT.ap(), "skT", sk_tok.ap(),
                        "sk_tok", v_tok.ap(), "v_tok")
            return (wfb[:, 0:2048], "sqTo", wfb[:, 2048:4096], "skTo",
                    wfb[:, 4096:6144], "sk_toko", wfb[:, 6144:8192], "v_toko")

        def acol(h):
            return (h % 2) * 512 + 64 * (h // 2)

        def attn_front_parts(c):
            sq, sqk, sk, skk, skt, sktk, vt, vtk = quartet(c)
            pwi = [0]

            def grp():
                r = [(ps_w1, "psw1"), (ps_w2, "psw2")][pwi[0] % 2]
                pwi[0] += 1
                return r

            def sigma(srcT, skey, dst, dkey):
                for hh in range(2):
                    sl = (slice(None), slice(hh*1024, (hh+1)*1024))
                    xb = ln_tok[:, hh*1024:(hh+1)*1024]
                    xbk = "lnbuf" if hh == 0 else "lnbufB"
                    v(lambda e, sl=sl, srcT=srcT, xb=xb: e.tensor_scalar(
                        xb, srcT[sl], 0.0, None, OP.min),
                      ins=(skey,), outs=(xbk,))
                    a(lambda e, xb=xb: e.activation(xb, xb, AF.Exp),
                      ins=(xbk,), outs=(xbk,))
                    v(lambda e, sl=sl, srcT=srcT, dst=dst, xb=xb:
                      e.scalar_tensor_tensor(
                        dst[sl], srcT[sl], 1.0, xb, OP.add, OP.max),
                      ins=(skey, xbk), outs=(dkey,))

            def proj_fm(wtag, wkey, dst, dkey):
                for oj in range(8):
                    pb, pbk = grp()
                    for k in range(8):
                        t(lambda e, oj=oj, k=k, pb=pb, wtag=wtag: e.matmul(
                            pb[:, 0:CH], wT[wtag][:, k*D+128*oj:k*D+128*oj+128],
                            lnT[:, k*CH:(k+1)*CH], start=(k == 0),
                            stop=(k == 7), **MM),
                          ins=("lnT", wkey), outs=(pbk,))
                    if oj % 2 == 0:
                        a(lambda e, oj=oj, pb=pb, dst=dst: e.activation(
                            dst[:, oj*CH:(oj+1)*CH], pb[:, 0:CH], AF.Copy),
                          ins=(pbk,), outs=(dkey,))
                    else:
                        v(lambda e, oj=oj, pb=pb, dst=dst: e.tensor_copy(
                            dst[:, oj*CH:(oj+1)*CH], pb[:, 0:CH]),
                          ins=(pbk,), outs=(dkey,))

            def p0():
                for j in range(TPC):
                    tt = c * TPC + j
                    ln_stats(tt)
                    ln_tile(tt, j)
                for j in range(TPC):
                    p.dma(lambda e, j=j: e.dma_start_transpose(
                        out=lnT.ap().rearrange("p (m tt) -> p m tt",
                                               tt=CH)[:, :, 128*j:128*j+128],
                        in_=ln_tok[:, j*D:(j+1)*D]),
                        ins=("lnbuf" if j == 0 else "lnbufB",), outs=("lnT",),
                        xpose=True)

            def p1():
                proj_fm("q", "wqT", qT, "qT")
                sigma(qT, "qT", sq, sqk)

            def p2():
                proj_fm("k", "wkT", kT, "kT")
                sigma(kT, "kT", sk, skk)

            def p3():
                for j in range(TPC):
                    for oh in range(2):
                        pb, pbk = grp()
                        for k in range(8):
                            t(lambda e, j=j, oh=oh, k=k, pb=pb: e.matmul(
                                pb[:, 0:512],
                                lnT[:, k*CH+128*j:k*CH+128*j+128],
                                wT["v"][:, k*D+oh*512:k*D+(oh+1)*512],
                                start=(k == 0), stop=(k == 7), **MM),
                              ins=("lnT", "wvT"), outs=(pbk,))
                        a(lambda e, j=j, oh=oh, pb=pb, vt=vt: e.activation(
                            vt[:, j*D+oh*512:j*D+(oh+1)*512], pb[:, 0:512],
                            AF.Copy),
                          ins=(pbk,), outs=(vtk,))
                for k in range(8):
                    p.dma(lambda e, k=k, sk=sk, skt=skt: e.dma_start_transpose(
                        out=skt[:, 0:TPC*D].rearrange(
                            "p (j dd) -> p j dd", dd=D)[:, :, 128*k:128*k+128],
                        in_=sk[:, k*CH:(k+1)*CH]),
                        ins=(skk,), outs=(sktk,), xpose=True)
                pb, pbk = grp()
                for j in range(TPC):
                    for k in range(8):
                        t(lambda e, j=j, k=k, pb=pb, skt=skt: e.matmul(
                            pb[:, j*16+2*k:j*16+2*k+2],
                            skt[:, j*D+128*k:j*D+128*k+128], oneh2[:],
                            start=True, stop=True, **MM),
                          ins=(sktk, "oneh2"), outs=(pbk,))
                v(lambda e, pb=pb: e.tensor_copy(csum_f[:, 0:TPC*16],
                                                 pb[:, 0:TPC*16]),
                  ins=(pbk,), outs=("csum_f",))
                for s in range(2 * TPC):
                    j, sg = divmod(s, 2)
                    e0 = j * 4 + sg * 2
                    v(lambda e, e0=e0: e.tensor_copy(
                        z2[0:64, :]
                        .rearrange("p (kk eight) -> p kk eight",
                                   eight=8)[:, :, e0:e0+1],
                        zfm[0:64, :].rearrange("p (kk o) -> p kk o", o=1)),
                      ins=("zfm",), outs=("z2",))
                    v(lambda e, e0=e0: e.tensor_copy(
                        z2[64:128, :]
                        .rearrange("p (kk eight) -> p kk eight",
                                   eight=8)[:, :, e0+1:e0+2],
                        zfm[64:128, :].rearrange("p (kk o) -> p kk o", o=1)),
                      ins=("zfm",), outs=("z2",))
                    v(lambda e, j=j, sg=sg: e.tensor_add(
                        zfm[:].rearrange("p (kk o) -> p kk o", o=1),
                        zfm[:].rearrange("p (kk o) -> p kk o", o=1),
                        csum_f[:, j*16:(j+1)*16]
                        .rearrange("p (kk two) -> p kk two", two=2)[:, :, sg:sg+1]),
                      ins=("csum_f", "zfm"), outs=("zfm",))
                pb, pbk = grp()
                for j in range(TPC):
                    for k in range(8):
                        t(lambda e, j=j, k=k, pb=pb, sq=sq: e.matmul(
                            pb[:, k*8+j*4:k*8+j*4+4],
                            sq[:, k*CH+j*128:k*CH+j*128+128],
                            z2[:, k*8+j*4:k*8+j*4+4],
                            start=True, stop=True, **MM),
                          ins=(sqk, "z2"), outs=(pbk,))
                        t(lambda e, j=j, k=k, pb=pb, sk=sk: e.matmul(
                            pb[:, 64+k*8+j*4:64+k*8+j*4+4],
                            sk[:, k*CH+j*128:k*CH+j*128+128],
                            z2[:, k*8+j*4:k*8+j*4+4],
                            start=True, stop=True, **MM),
                          ins=(skk, "z2"), outs=(pbk,))
                v(lambda e, pb=pb: e.tensor_copy(dq_sb[:, 0:128],
                                                 pb[:, 0:128]),
                  ins=(pbk,), outs=("dq_sb",))
                v(lambda e: e.tensor_scalar(dq_sb[:, 0:128],
                                            dq_sb[:, 0:128], 1e-6,
                                            None, OP.add),
                  ins=("dq_sb",), outs=("dq_sb",))
                v(lambda e: e.reciprocal(rdq[:, 0:128], dq_sb[:, 0:128]),
                  ins=("dq_sb",), outs=("rdq",))
            return [p0, p1, p2, p3]

        def attn_mid(c):
            sq, sqk, sk, skk, skt, sktk, vt, vtk = quartet(c)
            for j in range(TPC):
                cp, cpk = (c1p, "c1p") if j == 0 else (c1pB, "c1pB")
                for sg, lo in ((0, 0), (1, 64)):
                    e0 = j * 4 + sg * 2
                    v(lambda e, cp=cp, lo=lo, e0=e0: e.tensor_tensor(
                        cp[lo:lo+64, :].rearrange("p (i two) -> p i two", two=2),
                        rdq[lo:lo+64, 0:64]
                        .rearrange("p (kk eight) -> p kk eight",
                                   eight=8)[:, :, e0:e0+2],
                        betac[lo:lo+64, :].rearrange("p (i two) -> p i two",
                                                     two=2),
                        OP.mult),
                      ins=("rdq", "betac"), outs=(cpk,))
                    v(lambda e, j=j, lo=lo, e0=e0: e.tensor_copy(
                        c3_sb[lo:lo+64, j*16:(j+1)*16]
                        .rearrange("p (i two) -> p i two", two=2),
                        rdq[lo:lo+64, 64:128]
                        .rearrange("p (kk eight) -> p kk eight",
                                   eight=8)[:, :, e0:e0+2]),
                      ins=("rdq",), outs=("c3_sb",))
            for j in range(TPC):
                g(lambda e, j=j, sk=sk, skt=skt: e.tensor_tensor(
                    sk[:, j*D:(j+1)*D].rearrange("p (hh kk) -> p hh kk", kk=64),
                    skt[:, j*D:(j+1)*D].rearrange("p (hh kk) -> p hh kk", kk=64),
                    c3_sb[:, j*16:(j+1)*16].rearrange("p (hh o) -> p hh o", o=1)
                    .broadcast_to([128, 16, 64]), OP.mult),
                  ins=(sktk, "c3_sb", skk), outs=(skk,))
            for j in range(TPC):
                pt, ptk = (pt_un, "pt_un") if j == 0 else (scr1.ap(), "scr1")
                pc, pck = (pt_c2, "pt_c2") if j == 0 else (scr2.ap(), "scr2")
                ptks = (ptk,) if j == 0 else ("scr1", "scr1b")
                pcks = (pck,) if j == 0 else ("scr2", "scr2b")
                for sg in range(2):
                    for par in range(2):
                        for h in range(par, H, 2):
                            t(lambda e, h=h, sg=sg, j=j: e.matmul(
                                ps_sc[sg*64:sg*64+64, acol(h):acol(h)+64],
                                kT[(h % 2)*64:(h % 2)*64+64,
                                   (h//2)*CH+j*128+sg*64:(h//2)*CH+j*128+sg*64+64],
                                qT[(h % 2)*64:(h % 2)*64+64,
                                   (h//2)*CH+j*128+sg*64:(h//2)*CH+j*128+sg*64+64],
                                start=True, stop=True,
                                tile_position=((h % 2)*64, sg*64), **MM),
                              ins=("kT", "qT"),
                              outs=("ps_scA" if par == 0 else "ps_scB",))
                for par in range(2):
                    a(lambda e, par=par, pt=pt: e.activation(
                        pt[:, par*512:(par+1)*512], ps_sc[:, par*512:(par+1)*512],
                        AF.Exp, bias=-40.0, scale=0.125),
                      ins=("ps_scA" if par == 0 else "ps_scB",), outs=ptks)
                for lo in (0, 64):
                    g(lambda e, pt=pt, lo=lo: e.affine_select(
                        pt[lo:lo+64, :].rearrange("p (hh tt) -> p hh tt", tt=64),
                        pt[lo:lo+64, :].rearrange("p (hh tt) -> p hh tt", tt=64),
                        pattern=[[0, 16], [1, 64]], compare_op=OP.is_ge, fill=0.0,
                        base=0, channel_multiplier=-1),
                      ins=ptks, outs=ptks)
                t(lambda e, pt=pt: e.matmul(ps_w1[0:2, 0:512], oneh2[:],
                                            pt[:, 0:512], start=True, stop=True,
                                            **MM),
                  ins=("oneh2",) + ptks, outs=("psw1",))
                t(lambda e, pt=pt: e.matmul(ps_w2[0:2, 0:512], oneh2[:],
                                            pt[:, 512:1024], start=True,
                                            stop=True, **MM),
                  ins=("oneh2",) + ptks, outs=("psw2",))
                for pw, pwk in ((ps_w1, "psw1"), (ps_w2, "psw2")):
                    a(lambda e, pw=pw: e.activation(pw[0:2, 0:512],
                                                    pw[0:2, 0:512], AF.Ln),
                      ins=(pwk,), outs=(pwk,))
                    a(lambda e, pw=pw: e.activation(pw[0:2, 0:512],
                                                    pw[0:2, 0:512], AF.Exp,
                                                    scale=-1.0),
                      ins=(pwk,), outs=(pwk,))
                v(lambda e: e.tensor_tensor(c2bE[:], ps_w1[0:2, 0:512], betaE[:],
                                            OP.mult),
                  ins=("psw1", "betaE"), outs=("c2pool",))
                v(lambda e: e.tensor_tensor(c2bO[:], ps_w2[0:2, 0:512], betaO[:],
                                            OP.mult),
                  ins=("psw2", "betaO"), outs=("c2pool",))
                t(lambda e: e.matmul(ps_w1[:, 0:512], bt2[:], c2bE[:],
                                     start=True, stop=True, **MM),
                  ins=("bt2", "c2pool"), outs=("psw1",))
                t(lambda e: e.matmul(ps_w2[:, 0:512], bt2[:], c2bO[:],
                                     start=True, stop=True, **MM),
                  ins=("bt2", "c2pool"), outs=("psw2",))
                v(lambda e, pt=pt, pc=pc: e.tensor_tensor(
                    pc[:, 0:512], pt[:, 0:512], ps_w1[:, 0:512], OP.mult),
                  ins=ptks + ("psw1",), outs=pcks)
                v(lambda e, pt=pt, pc=pc: e.tensor_tensor(
                    pc[:, 512:1024], pt[:, 512:1024], ps_w2[:, 0:512], OP.mult),
                  ins=ptks + ("psw2",), outs=pcks)

        def attn_back_seg(c, s):
            sq, sqk, sk, skk, skt, sktk, vt, vtk = quartet(c)
            j, sg = divmod(s, 2)
            pc, pcks = (pt_c2, ("pt_c2",)) if j == 0 \
                else (scr2.ap(), ("scr2", "scr2b"))
            cp, cpk = (c1p, "c1p") if j == 0 else (c1pB, "c1pB")
            ib, ibk = (ib_sb.ap(), "ib_sb") if s % 2 == 0 \
                else (bias_bc[:, 0:512], "bias_bc")
            pboff = 0 if s % 2 == 0 else 512
            pbk_sc = "ps_scA" if s % 2 == 0 else "ps_scB"
            for par in range(2):
                for h in range(par, H, 2):
                    t(lambda e, h=h, sg=sg, j=j, sk=sk, skt=skt,
                      pboff=pboff: e.matmul(
                        ps_sc[(h % 2)*64:(h % 2)*64+64,
                              pboff+64*(h//2):pboff+64*(h//2)+64],
                        sk[sg*64:sg*64+64, j*D+64*h:j*D+64*h+64],
                        skt[sg*64:sg*64+64, j*D+64*h:j*D+64*h+64],
                        start=True, stop=True,
                        tile_position=(sg*64, (h % 2)*64), **MM),
                      ins=(skk, sktk), outs=(pbk_sc,))
            v(lambda e, ib=ib, pboff=pboff: e.scalar_tensor_tensor(
                ib[:, 0:512].rearrange("p (i kk) -> p i kk", kk=64),
                ps_sc[:, pboff:pboff+512].rearrange("p (i kk) -> p i kk", kk=64),
                -1.0,
                irep[:].rearrange("p (o kk) -> p o kk", o=1)
                .broadcast_to([128, 8, 64]),
                OP.mult, OP.add),
              ins=(pbk_sc, "irep"), outs=(ibk,))
            for par in range(2):
                for h in range(par, H, 2):
                    t(lambda e, h=h, sg=sg, j=j, sq=sq: e.matmul(
                        ps_a[sg*64:sg*64+64, acol(h):acol(h)+64],
                        sq[(h % 2)*64:(h % 2)*64+64,
                           (h//2)*CH+j*128+sg*64:(h//2)*CH+j*128+sg*64+64],
                        m_sb[(h % 2)*64:(h % 2)*64+64, 64*(h//2):64*(h//2)+64],
                        start=True, stop=True,
                        tile_position=((h % 2)*64, sg*64), **MM),
                      ins=(sqk, "m_sbA" if h % 2 == 0 else "m_sbB"),
                      outs=("ps_aA" if h % 2 == 0 else "ps_aB",))
            for par in range(2):
                for h in range(par, H, 2):
                    pw = ps_w1 if h % 2 == 0 else ps_w2
                    t(lambda e, h=h, sg=sg, j=j, pc=pc, vt=vt, pw=pw: e.matmul(
                        pw[sg*64:sg*64+64, 64*(h//2):64*(h//2)+64],
                        pc[sg*64:sg*64+64, acol(h):acol(h)+64],
                        vt[sg*64:sg*64+64, j*D+64*h:j*D+64*h+64],
                        start=True, stop=True,
                        tile_position=(sg*64, sg*64), **MM),
                      ins=pcks + (vtk,),
                      outs=("psw1" if h % 2 == 0 else "psw2",))
            for par in range(2):
                asl = a_s[:, j*D:(j+1)*D].rearrange(
                    "p (i f) -> p i f", f=128)[sg*64:sg*64+64, :,
                                               par*64:par*64+64]
                v(lambda e, asl=asl, sg=sg, par=par, cp=cp: e.tensor_tensor(
                    asl,
                    ps_a[sg*64:sg*64+64, par*512:par*512+512]
                    .rearrange("p (i kk) -> p i kk", kk=64),
                    cp[sg*64:sg*64+64, :]
                    .rearrange("p (i two) -> p i two", two=2)[:, :, par:par+1]
                    .broadcast_to([64, 8, 64]),
                    OP.mult),
                  ins=("ps_aA" if par == 0 else "ps_aB", cpk), outs=("a_s",))
                pw = ps_w1 if par == 0 else ps_w2
                v(lambda e, asl=asl, sg=sg, par=par, pw=pw: e.tensor_add(
                    asl, asl,
                    pw[sg*64:sg*64+64, 0:512]
                    .rearrange("p (i kk) -> p i kk", kk=64)),
                  ins=("a_s", "psw1" if par == 0 else "psw2"),
                  outs=("a_s",))
            for par in range(2):
                for h in range(par, H, 2):
                    t(lambda e, h=h, par=par, ib=ib: e.matmul(
                        ps_m[(h % 2)*64:(h % 2)*64+64, acol(h):acol(h)+64],
                        ib[(h % 2)*64:(h % 2)*64+64, 64*(h//2):64*(h//2)+64],
                        m_sb[(h % 2)*64:(h % 2)*64+64, 64*(h//2):64*(h//2)+64],
                        start=(h == par), stop=False,
                        tile_position=((h % 2)*64, (h % 2)*64), **MM),
                      ins=(ibk, "m_sbA" if h % 2 == 0 else "m_sbB"),
                      outs=("ps_mA" if h % 2 == 0 else "ps_mB",))
            for par in range(2):
                for h in range(par, H, 2):
                    t(lambda e, h=h, sg=sg, j=j, skt=skt, vt=vt: e.matmul(
                        ps_m[(h % 2)*64:(h % 2)*64+64, acol(h):acol(h)+64],
                        skt[sg*64:sg*64+64, j*D+64*h:j*D+64*h+64],
                        vt[sg*64:sg*64+64, j*D+64*h:j*D+64*h+64],
                        start=False, stop=True,
                        tile_position=(sg*64, (h % 2)*64), **MM),
                      ins=(sktk, vtk),
                      outs=("ps_mA" if h % 2 == 0 else "ps_mB",))
            a(lambda e: e.activation(m_sb[0:64, :], ps_m[0:64, 0:512],
                                     AF.Copy),
              ins=("ps_mA",), outs=("m_sbA",))
            a(lambda e: e.activation(m_sb[64:128, :], ps_m[64:128, 512:1024],
                                     AF.Copy),
              ins=("ps_mB",), outs=("m_sbB",))

        def attn_back_tail(c):
            for j in range(TPC):
                p.dma(lambda e, j=j: e.dma_start_transpose(
                    out=asT.ap().rearrange("p (m tt) -> p m tt",
                                           tt=CH)[:, :, 128*j:128*j+128],
                    in_=a_s[:, j*D:(j+1)*D]),
                    ins=("a_s",), outs=("pt_un", "pt_c2"), xpose=True)
            for j in range(TPC):
                tt = c * TPC + j
                for oh in range(2):
                    for k in range(8):
                        t(lambda e, j=j, oh=oh, k=k: e.matmul(
                            ps_m[:, oh*512:(oh+1)*512],
                            asT[:, k*CH+128*j:k*CH+128*j+128],
                            wT["o"][:, k*D+oh*512:k*D+(oh+1)*512],
                            start=(k == 0), stop=(k == 7), **MM),
                          ins=("pt_un", "pt_c2", "woT"),
                          outs=("ps_mA" if oh == 0 else "ps_mB",))
                    v(lambda e, tt=tt, oh=oh: e.tensor_add(
                        x_tok[:, tt*D+oh*512:tt*D+(oh+1)*512],
                        x_tok[:, tt*D+oh*512:tt*D+(oh+1)*512],
                        ps_m[:, oh*512:(oh+1)*512]),
                      ins=("ps_mA" if oh == 0 else "ps_mB", xq(tt)),
                      outs=(xq(tt),))

        def rebuild_for_ffn2():
            for jp in range(4):
                wf_pair(jp, extra=("sqTo", "skTo", "sk_toko", "v_toko"))

        yv = y_out.rearrange("(tt p) d -> p tt d", p=128)
        qkvo_blocks = [(nn, jj) for nn in ("q", "k", "v", "o") for jj in range(8)]
        ffn_stats(range(NT), wf_pairs=True, qkvo=qkvo_blocks)
        ffn_amax(range(NT))
        ffn("f1", stats_done=True)
        for f in attn_front_parts(0):
            f()
        for c in range(NCH):
            attn_mid(c)
            nxt = attn_front_parts(c + 1) if c + 1 < NCH else None
            for s in range(2 * TPC):
                attn_back_seg(c, s)
                if nxt is not None and s < 4:
                    nxt[s]()
            if c == NCH - 1:
                rebuild_for_ffn2()
            attn_back_tail(c)
            ffn_stats([c * TPC + j for j in range(TPC)])
            ffn_amax([c * TPC + j for j in range(TPC)])
        ffn("f2", store=True, stats_done=True)
        p.dma_fence()

        @blk.tensor
        def _(e):
            for gi, fn in p.body["te"]:
                if TRUNC is None or gi <= TRUNC:
                    fn(e)

        @blk.vector
        def _(e):
            for gi, fn in p.body["dv"]:
                if TRUNC is None or gi <= TRUNC:
                    fn(e)

        @blk.scalar
        def _(e):
            for gi, fn in p.body["ac"]:
                if TRUNC is None or gi <= TRUNC:
                    fn(e)

        @blk.gpsimd
        def _(e):
            for gi, fn in p.body["gp"]:
                if TRUNC is None or gi <= TRUNC:
                    fn(e)

        @blk.sync
        def _(e):
            for gi, fn in p.body["sy"]:
                if TRUNC is None or gi <= TRUNC:
                    fn(e)

    return nc


def kernel(L0_attn_x, L0_attn_W_q, L0_attn_W_k, L0_attn_W_v, L0_attn_W_o,
           L0_attn_beta_param, L0_ffn_weight, L0_ffn_bias):
    x = np.ascontiguousarray(np.asarray(L0_attn_x, dtype=np.float32))
    B, T, _ = x.shape
    nc = build_program(T=T)
    wmap = {
        "wq": np.ascontiguousarray(np.asarray(L0_attn_W_q, np.float32)),
        "wk": np.ascontiguousarray(np.asarray(L0_attn_W_k, np.float32)),
        "wv": np.ascontiguousarray(np.asarray(L0_attn_W_v, np.float32)),
        "wo": np.ascontiguousarray(np.asarray(L0_attn_W_o, np.float32)),
        "wf": np.ascontiguousarray(np.asarray(L0_ffn_weight, np.float32)),
        "fbias": np.ascontiguousarray(np.asarray(L0_ffn_bias, np.float32)).reshape(1, -1),
        "beta": np.ascontiguousarray(np.asarray(L0_attn_beta_param, np.float32)).reshape(1, -1),
    }
    in_maps = [dict(wmap, x=np.ascontiguousarray(x[i])) for i in range(B)]
    res = run_bass_kernel_spmd(nc, in_maps, core_ids=list(range(B)))
    out = np.stack([np.asarray(res.results[i]["y"]) for i in range(B)], axis=0)
    return out.astype(np.float32)


# version 37
# speedup vs baseline: 1.0123x; 1.0086x over previous
from contextlib import ExitStack

import numpy as np

import concourse.bass as bass
import concourse.mybir as mybir
from concourse.bass_utils import run_bass_kernel_spmd
from concourse.alu_op_type import AluOpType as OP

F32 = mybir.dt.float32
BF16 = mybir.dt.bfloat16
U16 = mybir.dt.uint16
FP8 = mybir.dt.float8e4
AF = mybir.ActivationFunctionType
AX = mybir.AxisListType.X

D = 1024
H = 16
DK = 64
NCORES = 8
AMAX_MODE = "core"
DBG_SEGS = None
TRUNC = None
_last_counts = {}

ACT_CONSTS = [-40.0, 0.125, 1e-5, -0.5, 1.0, 0.0, -1.0, 448.0, 0.5]


class Prog:

    ENGS = ("te", "dv", "ac", "gp", "sy")

    NQ = 8
    NGQ = 4

    def __init__(self, nc):
        self.nc = nc
        self.gidx = 0
        self.body = {e: [] for e in self.ENGS}
        self.sem = {}
        self.cnt = {e: 0 for e in self.ENGS}
        for i in range(self.NQ):
            self.cnt[f"syq{i}"] = 0
        for i in range(self.NGQ):
            self.cnt[f"gpq{i}"] = 0
        self._gprot = 0
        self.waited = {e: {} for e in self.ENGS}
        self.lastw = {}
        self.readers = {}
        self._xfence = False
        self._dmarot = 0

    def alloc_sems(self, ctx):
        for s in (list(self.ENGS) + [f"syq{i}" for i in range(self.NQ)]
                  + [f"gpq{i}" for i in range(self.NGQ)]):
            self.sem[s] = ctx.enter_context(self.nc.semaphore(s))

    def _dep(self, eng, deps):
        merged = {}
        for d in deps:
            if d is None:
                continue
            s, vv = d
            if eng == "te" and s == "te":
                continue
            if merged.get(s, 0) < vv:
                merged[s] = vv
        for s, vv in sorted(merged.items()):
            if self.waited[eng].get(s, 0) >= vv:
                continue
            self.waited[eng][s] = vv
            sem = self.sem[s]
            self.body[eng].append((self.gidx, lambda e, sem=sem, vv=vv: e.wait_ge(sem, vv)))

    def op(self, eng, fn, ins=(), outs=(), sem=None, inc=1):
        deps = []
        for k in ins:
            deps.append(self.lastw.get(k))
        for k in outs:
            deps.append(self.lastw.get(k))
            for s, vv in self.readers.get(k, {}).items():
                deps.append((s, vv))
        self._dep(eng, deps)
        semname = sem or eng
        self.cnt[semname] += inc
        val = self.cnt[semname]
        semh = self.sem[semname]

        def emit(e, fn=fn, semh=semh, inc=inc):
            fn(e).then_inc(semh, inc)
        self.gidx += 1
        self.body[eng].append((self.gidx, emit))
        for k in outs:
            self.lastw[k] = (semname, val)
            self.readers[k] = {}
        for k in ins:
            r = self.readers.setdefault(k, {})
            if r.get(semname, 0) < val:
                r[semname] = val
        return (semname, val)

    def dma_fence(self):
        self._dep("sy", [(f"syq{i}", self.cnt[f"syq{i}"]) for i in range(self.NQ)]
                  + [(f"gpq{i}", self.cnt[f"gpq{i}"]) for i in range(self.NGQ)])

    def dma(self, fn, ins=(), outs=(), xpose=False, eng="sy"):
        if xpose or self._xfence:
            self.dma_fence()
            self._xfence = xpose
        if eng == "gp":
            r = self._gprot
            self._gprot = (r + 1) % self.NGQ
            sname = f"gpq{r}"
        else:
            r = self._dmarot
            self._dmarot = (r + 1) % self.NQ
            sname = f"syq{r}"
        deps = [(sname, self.cnt[sname])]
        if getattr(self, "_last_xpose", None) is not None:
            deps.append(self._last_xpose)
        self._dep(eng, deps)
        res = self.op(eng, fn, ins=ins, outs=outs, sem=sname, inc=16)
        if xpose:
            self._last_xpose = res
        return res


def build_program(T=2048, CH=256):
    NT = T // 128
    NCH = T // CH
    TPC = CH // 128
    assert NT <= 16
    nc = bass.Bass("TRN2", target_bir_lowering=False)
    p = Prog(nc)

    x_in = nc.dram_tensor("x", [T, D], F32, kind="ExternalInput").ap()
    win = {
        "q": nc.dram_tensor("wq", [D, D], F32, kind="ExternalInput").ap(),
        "k": nc.dram_tensor("wk", [D, D], F32, kind="ExternalInput").ap(),
        "v": nc.dram_tensor("wv", [D, D], F32, kind="ExternalInput").ap(),
        "o": nc.dram_tensor("wo", [D, D], F32, kind="ExternalInput").ap(),
        "f": nc.dram_tensor("wf", [D, D], F32, kind="ExternalInput").ap(),
    }
    bias_in = nc.dram_tensor("fbias", [1, D], F32, kind="ExternalInput").ap()
    beta_in = nc.dram_tensor("beta", [1, H], F32, kind="ExternalInput").ap()
    y_out = nc.dram_tensor("y", [T, D], F32, kind="ExternalOutput").ap()

    ctx = ExitStack()
    sb = lambda name, shape, dt: ctx.enter_context(nc.sbuf_tensor(name, shape, dt))
    psum = lambda name, shape: ctx.enter_context(nc.psum_tensor(name, shape, F32))

    with ctx:
        p.alloc_sems(ctx)

        x_tok = sb("x_tok", [128, NT * D], F32)

        def xq(tt):
            return f"x_t{tt}"
        wT = {n: sb(f"w{n}T", [128, 8 * D], BF16) for n in ("q", "k", "v", "o", "f")}
        stage = sb("stage", [128, D], F32)
        lnT = sb("lnT", [128, 8 * CH], BF16)
        qT = sb("qT", [128, 8 * CH], BF16)
        kT = sb("kT", [128, 8 * CH], BF16)
        sqT = sb("sqT", [128, 8 * CH], BF16)
        skT = sb("skT", [128, 8 * CH], BF16)
        sk_tok = sb("sk_tok", [128, TPC * D], BF16)
        v_tok = sb("v_tok", [128, TPC * D], BF16)
        a_s = sb("a_s", [128, TPC * D], BF16)
        ln_tok = sb("ln_tok", [128, TPC * D], BF16)
        ptbuf = sb("ptbuf", [128, 2 * D], BF16)
        pt_un = ptbuf[:, 0:D]
        pt_c2 = ptbuf[:, D:2*D]
        scr1 = sb("scr1", [128, D], BF16)
        scr2 = sb("scr2", [128, D], BF16)
        ib_sb = sb("ib_sb", [128, 512], BF16)
        m_sb = sb("m_sb", [128, 512], BF16)
        stats = sb("stats", [128, 112], F32)
        zfm = sb("zfm", [128, 8], F32)
        z2 = sb("z2", [128, 64], BF16)
        csum_f = sb("csum_f", [128, 64], F32)
        dq_sb = sb("dq_sb", [128, 128], F32)
        rdq = sb("rdq", [128, 128], F32)
        c1p = sb("c1p", [128, 16], BF16)
        c1pB = sb("c1pB", [128, 16], BF16)
        c3_sb = sb("c3_sb", [128, 64], BF16)
        c2pool = sb("c2pool", [2, 1024], BF16)
        rowred = sb("rowred", [128, 48], F32)
        rows = sb("rows", [1, 200], F32)
        scal = sb("scal", [1, 16], F32)
        scol = sb("scol", [128, 4], F32)
        betaE = sb("betaE", [2, 512], BF16)
        betaO = sb("betaO", [2, 512], BF16)
        betac = sb("betac", [128, 16], BF16)
        b4 = sb("b4", [4, 16], BF16)
        bias_bc = sb("bias_bc", [128, D], BF16)
        beta_row = sb("beta_row", [1, H], F32)
        cpool = sb("cpool", [128, 12], F32)
        ones1 = sb("ones1", [1, 128], BF16)
        ones1f = sb("ones1f", [1, 128], F32)
        idf = sb("idf", [128, 128], F32)
        idb = sb("idb", [128, 128], BF16)
        bt2 = sb("bt2", [2, 128], BF16)
        oneh2 = sb("oneh2", [128, 2], BF16)
        irep = sb("irep", [128, 64], BF16)

        stage_bf = pt_c2
        c1fm = ln_tok
        sk3_tok = skT
        asT = ptbuf
        scr3 = pt_un
        c2bE = c2pool[:, 0:512]
        c2bO = c2pool[:, 512:1024]

        ps_sc = psum("ps_sc", [128, 1024])
        ps_a = psum("ps_a", [128, 1024])
        ps_m = psum("ps_m", [128, 1024])
        ps_w1 = psum("ps_w1", [128, 512])
        ps_w2 = psum("ps_w2", [128, 512])

        for i, vv in enumerate(ACT_CONSTS):
            nc.const_aps.register(F32, float(vv), cpool[:, i:i+1])

        import sys
        print(f"[build] sbuf bytes remaining/partition: {nc.sbuf_bytes_remaining}",
              file=sys.stderr)
        blk = ctx.enter_context(nc.Block())

        def t(fn, ins=(), outs=()):
            return p.op("te", fn, ins=ins, outs=outs)

        def v(fn, ins=(), outs=()):
            return p.op("dv", fn, ins=ins, outs=outs)

        def a(fn, ins=(), outs=()):
            return p.op("ac", fn, ins=ins, outs=outs)

        def g(fn, ins=(), outs=()):
            return p.op("gp", fn, ins=ins, outs=outs)

        MM = dict(skip_group_check=True)

        for i, vv in enumerate(ACT_CONSTS):
            g(lambda e, i=i, vv=vv: e.memset(cpool[:, i:i+1], float(vv)),
              outs=(f"cp{i}",))
        g(lambda e: e.memset(ones1[:], 1.0), outs=("ones1",))
        g(lambda e: e.memset(ones1f[:], 1.0), outs=("ones1f",))
        for nm, tile in (("idf", idf), ("idb", idb)):
            g(lambda e, tile=tile: e.memset(tile[:], 1.0), outs=(nm,))
            g(lambda e, tile=tile: e.affine_select(
                tile[:], tile[:], pattern=[[-1, 128]], compare_op=OP.is_equal,
                fill=0.0, base=0, channel_multiplier=1), ins=(nm,), outs=(nm,))
        g(lambda e: e.memset(irep[:], 1.0), outs=("irep",))
        g(lambda e: e.affine_select(irep[0:64, :], irep[0:64, :], pattern=[[-1, 64]],
                                    compare_op=OP.is_equal, fill=0.0, base=0,
                                    channel_multiplier=1), ins=("irep",), outs=("irep",))
        g(lambda e: e.affine_select(irep[64:128, :], irep[64:128, :], pattern=[[-1, 64]],
                                    compare_op=OP.is_equal, fill=0.0, base=0,
                                    channel_multiplier=1), ins=("irep",), outs=("irep",))
        g(lambda e: e.memset(oneh2[:], 1.0), outs=("oneh2",))
        g(lambda e: e.affine_select(oneh2[:, 0:1], oneh2[:, 0:1], pattern=[[0, 1]],
                                    compare_op=OP.is_ge, fill=0.0, base=63,
                                    channel_multiplier=-1), ins=("oneh2",), outs=("oneh2",))
        g(lambda e: e.affine_select(oneh2[:, 1:2], oneh2[:, 1:2], pattern=[[0, 1]],
                                    compare_op=OP.is_ge, fill=0.0, base=-64,
                                    channel_multiplier=1), ins=("oneh2",), outs=("oneh2",))
        g(lambda e: e.memset(bt2[:], 1.0), outs=("bt2",))
        g(lambda e: e.affine_select(bt2[:], bt2[:], pattern=[[1, 128]],
                                    compare_op=OP.is_ge, fill=0.0, base=0,
                                    channel_multiplier=-64),
          ins=("bt2",), outs=("bt2",))
        g(lambda e: e.affine_select(bt2[:], bt2[:], pattern=[[-1, 128]],
                                    compare_op=OP.is_ge, fill=0.0, base=63,
                                    channel_multiplier=64),
          ins=("bt2",), outs=("bt2",))
        g(lambda e: e.memset(zfm[:], 0.0), outs=("zfm",))
        g(lambda e: e.memset(z2[:], 0.0), outs=("z2",))
        g(lambda e: e.memset(m_sb[:], 0.0), outs=("m_sbA", "m_sbB"))
        if DBG_SEGS is not None:
            g(lambda e: e.memset(a_s[:], 0.0), outs=("a_s",))

        xv = x_in.rearrange("(tt p) d -> p tt d", p=128)
        nsplit = min(4, NT)
        for q4 in range(nsplit):
            lo, hi = q4 * NT // nsplit, (q4 + 1) * NT // nsplit
            p.dma(lambda e, lo=lo, hi=hi: e.dma_start(
                out=x_tok[:, lo*D:hi*D].rearrange("p (tt d) -> p tt d", d=D),
                in_=xv[:, lo:hi, :]),
                outs=tuple({xq(tt) for tt in range(lo, hi)}), eng="gp")

        def quant_buf(src_ap, src_key, dst_ap, dst_key, scol_idx):
            SC1 = ("scr1", "scr1b")
            SC2 = ("scr2", "scr2b")
            a(lambda e: e.activation(
                scr1[:, 0:D], src_ap, AF.Identity, scale=scol[:, scol_idx:scol_idx+1]),
              ins=(src_key, f"scol{scol_idx}"), outs=SC1)
            v(lambda e: e.tensor_scalar(scr3[:, 0:D].bitcast(U16), scr1[:, 0:D].bitcast(U16),
                                        0x7F80, None, OP.bitwise_and),
              ins=SC1, outs=("pt_un",))
            v(lambda e: e.tensor_scalar(scr2[:, 0:D], scr3[:, 0:D],
                                        24.0, 0.375, OP.mult, OP.max),
              ins=("pt_un",), outs=SC2)
            v(lambda e: e.tensor_add(scr1[:, 0:D], scr1[:, 0:D], scr2[:, 0:D]),
              ins=SC1 + SC2, outs=SC1)
            v(lambda e: e.tensor_sub(dst_ap, scr1[:, 0:D], scr2[:, 0:D]),
              ins=SC1 + SC2, outs=(dst_key,))

        def cross_part_max(colkey, col_ap, out_idx, out_key):
            t(lambda e: e.transpose(ps_w1[0:1, 0:128], col_ap, idf[:]),
              ins=(colkey, "idf"), outs=("psw1",))
            a(lambda e: e.activation(rows[0:1, 0:128], ps_w1[0:1, 0:128], AF.Copy),
              ins=("psw1",), outs=("rows128",))
            v(lambda e: e.tensor_reduce(scal[0:1, out_idx:out_idx+1],
                                        rows[0:1, 0:128], AX, OP.max),
              ins=("rows128",), outs=(out_key,))

        def bcast_scalar(in_idx, in_key, scol_idx):
            t(lambda e: e.matmul(ps_w2[:, 0:1], ones1f[:],
                                 scal[0:1, in_idx:in_idx+1], start=True, stop=True,
                                 **MM),
              ins=(in_key, "ones1f"), outs=("psw2",))
            a(lambda e: e.activation(scol[:, scol_idx:scol_idx+1], ps_w2[:, 0:1],
                                     AF.Copy),
              ins=("psw2",), outs=(f"scol{scol_idx}",))

        kTf = kT.ap().bitcast(F32)
        lanes_f32 = [(stage[:], "stage"), (kTf, "kT")]
        lanes_bf = [(sqT[:, 0:1024], "sqT"), (sqT[:, 1024:2048], "sqT")]

        def emit_prep_block(n, j):
            wsrc = win[n].rearrange("(j p) i -> p j i", p=128)
            ldst, lkey = lanes_f32[j % 2]
            conv, ckey = lanes_bf[j % 2]
            p.dma(lambda e, j=j, wsrc=wsrc, ldst=ldst: e.dma_start(
                out=ldst, in_=wsrc[:, j, :]), outs=(lkey,), eng="gp")
            if n == "f":
                quant_buf(ldst, lkey, conv, ckey, 0)
            else:
                v(lambda e, ldst=ldst, conv=conv: e.tensor_copy(conv, ldst),
                  ins=(lkey,), outs=(ckey,))
            pb, pbkey = (ps_w1, "psw1") if j % 2 == 0 else (ps_w2, "psw2")
            for cidx in range(8):
                t(lambda e, cidx=cidx, pb=pb, conv=conv: e.transpose(
                    pb[:, cidx*64:(cidx+1)*64].bitcast(BF16),
                    conv[:, cidx*128:(cidx+1)*128], idb[:]),
                  ins=(ckey, "idb"), outs=(pbkey,))
            if j % 2 == 0:
                v(lambda e, n=n, j=j, pb=pb: e.tensor_copy(
                    wT[n].ap().rearrange("p (c o) -> p c o", o=D)
                    [:, :, 128*j:128*j+128],
                    pb[:, 0:512].bitcast(BF16).rearrange(
                        "p (c o) -> p c o", o=128)),
                    ins=(pbkey,), outs=(f"w{n}T",))
            else:
                a(lambda e, n=n, j=j, pb=pb: e.activation(
                    wT[n].ap().rearrange("p (c o) -> p c o", o=D)
                    [:, :, 128*j:128*j+128],
                    pb[:, 0:512].bitcast(BF16).rearrange(
                        "p (c o) -> p c o", o=128),
                    AF.Copy),
                    ins=(pbkey,), outs=(f"w{n}T",))

        def prep_wf():
            wsrc = win["f"].rearrange("(j p) i -> p j i", p=128)
            for j in range(8):
                ldst, lkey = lanes_f32[j % 2]
                p.dma(lambda e, j=j, ldst=ldst: e.dma_start(
                    out=ldst, in_=wsrc[:, j, :]), outs=(lkey,), eng="gp")
                v(lambda e, j=j, ldst=ldst: e.tensor_reduce(
                    rowred[:, j:j+1], ldst, AX, OP.max,
                    apply_absolute_value=True), ins=(lkey,), outs=(f"wr{j}",))
            v(lambda e: e.tensor_reduce(rowred[:, 8:9], rowred[:, 0:8], AX, OP.max),
              ins=tuple(f"wr{j}" for j in range(8)), outs=("wamax_col",))
            cross_part_max("wamax_col", rowred[:, 8:9], 0, "wamax")
            v(lambda e: e.tensor_scalar(scal[0:1, 1:2], scal[0:1, 0:1], 1e-12,
                                        None, OP.max), ins=("wamax",), outs=("wt1",))
            v(lambda e: e.reciprocal(scal[0:1, 2:3], scal[0:1, 1:2]),
              ins=("wt1",), outs=("wt2",))
            v(lambda e: e.tensor_scalar(scal[0:1, 3:4], scal[0:1, 2:3], 448.0,
                                        448.0, OP.mult, OP.min),
              ins=("wt2",), outs=("s_w",))
            v(lambda e: e.reciprocal(scal[0:1, 4:5], scal[0:1, 3:4]),
              ins=("s_w",), outs=("rsw",))
            v(lambda e: e.tensor_scalar(scal[0:1, 4:5], scal[0:1, 4:5],
                                        0.5, None, OP.mult),
              ins=("rsw",), outs=("rsw",))
            bcast_scalar(3, "s_w", 0)

        def wf_pair(jp, extra=()):
            wsrc = win["f"].rearrange("(j p) i -> p j i", p=128)
            if True:
                j0 = 2 * jp
                p.dma(lambda e, j0=j0: e.dma_start(
                    out=stage[:], in_=wsrc[:, j0, :]), outs=("stage",),
                    eng="gp")
                p.dma(lambda e, j0=j0: e.dma_start(
                    out=kTf, in_=wsrc[:, j0 + 1, :]), outs=("kT",), eng="gp")
                a(lambda e: e.activation(sqT[:, 0:1024], stage[:], AF.Identity,
                                         scale=scol[:, 0:1]),
                  ins=("stage", "scol0"), outs=("sqT",))
                a(lambda e: e.activation(sqT[:, 1024:2048], kTf, AF.Identity,
                                         scale=scol[:, 0:1]),
                  ins=("kT", "scol0"), outs=("sqT",))
                v(lambda e: e.tensor_scalar(
                    skT[:, 0:2048].bitcast(U16), sqT[:, 0:2048].bitcast(U16),
                    0x7F80, None, OP.bitwise_and),
                  ins=("sqT",), outs=("skT",))
                v(lambda e: e.tensor_scalar(skT[:, 0:2048], skT[:, 0:2048],
                                            24.0, 0.375, OP.mult, OP.max),
                  ins=("skT",), outs=("skT",))
                v(lambda e: e.tensor_add(sqT[:, 0:2048], sqT[:, 0:2048],
                                         skT[:, 0:2048]),
                  ins=("sqT", "skT"), outs=("sqT",))
                v(lambda e: e.tensor_sub(sqT[:, 0:2048], sqT[:, 0:2048],
                                         skT[:, 0:2048]),
                  ins=("sqT", "skT"), outs=("sqT",))
                for dj in range(2):
                    j = j0 + dj
                    pb, pbkey = (ps_w1, "psw1") if dj == 0 else (ps_w2, "psw2")
                    for cidx in range(8):
                        t(lambda e, dj=dj, cidx=cidx, pb=pb: e.transpose(
                            pb[:, cidx*64:(cidx+1)*64].bitcast(BF16),
                            sqT[:, dj*1024+cidx*128:dj*1024+(cidx+1)*128],
                            idb[:]),
                          ins=("sqT", "idb"), outs=(pbkey,))
                    if dj == 0:
                        v(lambda e, j=j, pb=pb: e.tensor_copy(
                            wT["f"].ap().rearrange("p (c o) -> p c o", o=D)
                            [:, :, 128*j:128*j+128],
                            pb[:, 0:512].bitcast(BF16).rearrange(
                                "p (c o) -> p c o", o=128)),
                            ins=(pbkey,), outs=("wfT",) + extra)
                    else:
                        a(lambda e, j=j, pb=pb: e.activation(
                            wT["f"].ap().rearrange("p (c o) -> p c o", o=D)
                            [:, :, 128*j:128*j+128],
                            pb[:, 0:512].bitcast(BF16).rearrange(
                                "p (c o) -> p c o", o=128),
                            AF.Copy),
                            ins=(pbkey,), outs=("wfT",) + extra)

        prep_wf()

        p.dma(lambda e: e.dma_start(out=beta_row[:], in_=beta_in), outs=("beta_row",))
        a(lambda e: e.activation(rows[0:1, 144:144+H], beta_row[0:1, :], AF.Exp, scale=-1.0),
          ins=("beta_row",), outs=("bexp",))
        v(lambda e: e.tensor_scalar(rows[0:1, 144:144+H], rows[0:1, 144:144+H], 1.0, None, OP.add),
          ins=("bexp",), outs=("bexp",))
        v(lambda e: e.reciprocal(rows[0:1, 160:160+H], rows[0:1, 144:144+H]),
          ins=("bexp",), outs=("beta_v",))
        v(lambda e: e.tensor_scalar(rows[0:1, 176:176+H], rows[0:1, 160:160+H], -1.0, 1.0,
                                    OP.mult, OP.add),
          ins=("beta_v",), outs=("bomb",))
        t(lambda e: e.matmul(ps_w1[0:4, 0:H], ones1f[0:1, 0:4], rows[0:1, 176:176+H],
                             start=True, stop=True, **MM),
          ins=("bomb", "ones1f"), outs=("psw1",))
        v(lambda e: e.tensor_copy(b4[:], ps_w1[0:4, 0:H]), ins=("psw1",), outs=("b4",))
        v(lambda e: e.tensor_copy(
            betaE[:].rearrange("p (j tt) -> p j tt", tt=64),
            b4[0:2, 0:16].rearrange("p (j s) -> p j s", s=2)[:, :, 0:1]
            .broadcast_to([2, 8, 64])),
          ins=("b4",), outs=("betaE",))
        v(lambda e: e.tensor_copy(
            betaO[:].rearrange("p (j tt) -> p j tt", tt=64),
            b4[0:2, 0:16].rearrange("p (j s) -> p j s", s=2)[:, :, 1:2]
            .broadcast_to([2, 8, 64])),
          ins=("b4",), outs=("betaO",))
        t(lambda e: e.matmul(ps_w2[:, 0:H], ones1f[:], rows[0:1, 160:160+H],
                             start=True, stop=True, **MM),
          ins=("beta_v", "ones1f"), outs=("psw2",))
        v(lambda e: e.tensor_copy(betac[:, 0:16], ps_w2[:, 0:H]),
          ins=("psw2",), outs=("betac",))

        def ln_stats(tt):
            xs = x_tok[:, tt*D:(tt+1)*D]
            v(lambda e, xs=xs, tt=tt: e.tensor_reduce(stats[:, tt:tt+1], xs, AX, OP.add),
              ins=(xq(tt),), outs=(f"sum{tt}",))
            a(lambda e, xs=xs, tt=tt: e.activation(
                scr1[:, 0:D], xs, AF.Square, accum_out=stats[:, 16+tt:17+tt]),
              ins=(xq(tt),), outs=("scr1", "scr1b", f"sq{tt}"))
            v(lambda e, tt=tt: e.tensor_scalar(stats[:, tt:tt+1], stats[:, tt:tt+1],
                                               1.0/D, None, OP.mult),
              ins=(f"sum{tt}",), outs=(f"m{tt}",))
            v(lambda e, tt=tt: e.tensor_scalar(stats[:, 16+tt:17+tt],
                                               stats[:, 16+tt:17+tt], 1.0/D, None,
                                               OP.mult),
              ins=(f"sq{tt}",), outs=(f"sq{tt}",))
            v(lambda e, tt=tt: e.tensor_tensor(stats[:, 48+tt:49+tt], stats[:, tt:tt+1],
                                               stats[:, tt:tt+1], OP.mult),
              ins=(f"m{tt}",), outs=(f"msq{tt}",))
            v(lambda e, tt=tt: e.tensor_sub(stats[:, 16+tt:17+tt],
                                            stats[:, 16+tt:17+tt],
                                            stats[:, 48+tt:49+tt]),
              ins=(f"sq{tt}", f"msq{tt}"), outs=(f"var{tt}",))
            a(lambda e, tt=tt: e.activation(stats[:, 32+tt:33+tt],
                                            stats[:, 16+tt:17+tt], AF.Ln, bias=1e-5),
              ins=(f"var{tt}",), outs=(f"lnv{tt}",))
            a(lambda e, tt=tt: e.activation(stats[:, 32+tt:33+tt],
                                            stats[:, 32+tt:33+tt], AF.Exp, scale=-0.5),
              ins=(f"lnv{tt}",), outs=(f"rstd{tt}",))
            v(lambda e, tt=tt: e.tensor_scalar(
                stats[:, 64+tt:65+tt], stats[:, tt:tt+1],
                stats[:, 32+tt:33+tt], -1.0, OP.mult, OP.mult),
              ins=(f"m{tt}", f"rstd{tt}"), outs=(f"nmr{tt}",))

        def ln_tile(tt, j):
            a(lambda e, tt=tt, j=j: e.activation(
                ln_tok[:, j*D:(j+1)*D], x_tok[:, tt*D:(tt+1)*D], AF.Identity,
                bias=stats[:, 64+tt:65+tt], scale=stats[:, 32+tt:33+tt]),
              ins=(xq(tt), f"rstd{tt}", f"nmr{tt}"),
              outs=("lnbuf" if j == 0 else "lnbufB",))

        def ffn_stats(tiles, wf_pairs=False, qkvo=()):
            qk = list(qkvo)
            for tt in tiles:
                ln_stats(tt)
                if wf_pairs and tt % 4 == 3:
                    wf_pair(tt // 4)
                for nn, jj in qk[tt*2:tt*2+2]:
                    emit_prep_block(nn, jj)
            if wf_pairs:
                for jp in range((max(tiles) + 1) // 4, 4):
                    wf_pair(jp)
            for nn, jj in qk[2*(max(tiles)+1 if tiles else 0):]:
                emit_prep_block(nn, jj)

        def ffn_amax(tiles):
            assert len(tiles) % 2 == 0 and tiles[1] == tiles[0] + 1
            for i in range(0, len(tiles), 2):
                tt = tiles[i]
                ik = (xq(tt), xq(tt + 1))
                mk = (f"m{tt}", f"m{tt+1}")
                rk = (f"rstd{tt}", f"rstd{tt+1}")
                ak = (f"am{tt}", f"am{tt+1}")
                xv2 = x_tok[:, tt*D:(tt+2)*D].rearrange(
                    "p (two d) -> p two d", d=D)
                v(lambda e, tt=tt, xv2=xv2: e.tensor_reduce(
                    rowred[:, 16+tt:18+tt], xv2, AX, OP.max),
                  ins=ik, outs=(f"amx{tt}",))
                v(lambda e, tt=tt, xv2=xv2: e.tensor_reduce(
                    rowred[:, 32+tt:34+tt], xv2, AX, OP.min),
                  ins=ik, outs=(f"ami{tt}",))
                v(lambda e, tt=tt: e.tensor_sub(
                    rowred[:, 16+tt:18+tt], rowred[:, 16+tt:18+tt],
                    stats[:, tt:tt+2]),
                  ins=(f"amx{tt}",) + mk, outs=(f"amx{tt}",))
                v(lambda e, tt=tt: e.tensor_sub(
                    rowred[:, 32+tt:34+tt], stats[:, tt:tt+2],
                    rowred[:, 32+tt:34+tt]),
                  ins=(f"ami{tt}",) + mk, outs=(f"ami{tt}",))
                v(lambda e, tt=tt: e.tensor_max(
                    rowred[:, 16+tt:18+tt], rowred[:, 16+tt:18+tt],
                    rowred[:, 32+tt:34+tt]),
                  ins=(f"amx{tt}", f"ami{tt}"), outs=(f"amx{tt}",))
                v(lambda e, tt=tt: e.tensor_mul(
                    rowred[:, 16+tt:18+tt], rowred[:, 16+tt:18+tt],
                    stats[:, 32+tt:34+tt]),
                  ins=(f"amx{tt}",) + rk, outs=ak)

        def ffn(tag, prep_blocks=(), store=False, stats_done=False):
            if not stats_done:
                ffn_stats(range(NT))
                ffn_amax(range(NT))
            v(lambda e: e.tensor_reduce(rowred[:, 0:1], rowred[:, 16:16+NT], AX,
                                        OP.max),
              ins=tuple(f"am{tt}" for tt in range(NT)), outs=("amax_col",))
            cross_part_max("amax_col", rowred[:, 0:1], 5, "amax")
            v(lambda e: e.tensor_scalar(scal[0:1, 6:7], scal[0:1, 5:6], 1e-12, None,
                                        OP.max), ins=("amax",), outs=("am1",))
            v(lambda e: e.reciprocal(scal[0:1, 7:8], scal[0:1, 6:7]),
              ins=("am1",), outs=("am2",))
            v(lambda e: e.tensor_scalar(scal[0:1, 8:9], scal[0:1, 7:8], 448.0, 448.0,
                                        OP.mult, OP.min), ins=("am2",), outs=("s_x",))
            v(lambda e: e.reciprocal(scal[0:1, 9:10], scal[0:1, 8:9]),
              ins=("s_x",), outs=("rsx",))
            v(lambda e: e.tensor_tensor(scal[0:1, 10:11], scal[0:1, 9:10],
                                        scal[0:1, 4:5], OP.mult),
              ins=("rsx", "rsw"), outs=("deq",))
            bcast_scalar(8, "s_x", 1)
            bcast_scalar(10, "deq", 2)
            v(lambda e: e.reciprocal(scal[0:1, 11:12], scal[0:1, 10:11]),
              ins=("deq",), outs=("ideq",))
            p.dma(lambda e: e.dma_start(out=stage[0:1, :], in_=bias_in),
                  outs=("stage",))
            v(lambda e: e.tensor_scalar(bias_bc[0:1, :], stage[0:1, :],
                                        scal[0:1, 11:12], 0.5, OP.mult,
                                        OP.mult),
              ins=("stage", "ideq"), outs=("bias_bc",))
            for tt in range(NT):
                v(lambda e, tt=tt: e.tensor_tensor(
                    stats[:, 48+tt:49+tt], stats[:, 32+tt:33+tt], scol[:, 1:2],
                    OP.mult),
                  ins=(f"rstd{tt}", "scol1"), outs=(f"rsts{tt}",))
                v(lambda e, tt=tt: e.tensor_tensor(
                    stats[:, 96+tt:97+tt], stats[:, 64+tt:65+tt], scol[:, 1:2],
                    OP.mult),
                  ins=(f"nmr{tt}", "scol1"), outs=(f"nms{tt}",))
            W2 = TPC * D

            def lanes(c):
                par = c % 2
                Y, Ykey = (v_tok, "v_tok") if par == 0 else (sk_tok, "sk_tok")
                Bq = a_s if par == 0 else ln_tok
                BKEYS = ("a_s",) if par == 0 else ("lnbuf", "lnbufB")
                Td, Tkey = (lnT, "lnT") if par == 0 else (qT, "qT")
                if par == 0:
                    pslots = [[(ps_sc, "ps_scA", 0), (ps_sc, "ps_scB", 512)],
                              [(ps_a, "ps_aA", 0), (ps_a, "ps_aB", 512)]]
                else:
                    pslots = [[(ps_m, "ps_mA", 0), (ps_m, "ps_mB", 512)],
                              [(ps_w1, "psw1", 0), (ps_w2, "psw2", 0)]]
                return Y, Ykey, Bq, BKEYS, Td, Tkey, pslots

            def q_phase(c):
                Y, Ykey, Bq, BKEYS, Td, Tkey, pslots = lanes(c)
                for j2 in range(TPC):
                    tt = c * TPC + j2
                    a(lambda e, tt=tt, j2=j2, Y=Y: e.activation(
                        Y[:, j2*D:(j2+1)*D], x_tok[:, tt*D:(tt+1)*D], AF.Identity,
                        bias=stats[:, 96+tt:97+tt], scale=stats[:, 48+tt:49+tt]),
                      ins=(xq(tt), f"rsts{tt}", f"nms{tt}"), outs=(Ykey,))
                v(lambda e, Y=Y, Bq=Bq: e.tensor_scalar(
                    Bq[:, 0:W2].bitcast(U16), Y[:, 0:W2].bitcast(U16),
                    0x7F80, None, OP.bitwise_and),
                  ins=(Ykey,), outs=BKEYS)
                v(lambda e, Bq=Bq: e.tensor_scalar(
                    Bq[:, 0:W2], Bq[:, 0:W2], 24.0, 0.375, OP.mult, OP.max),
                  ins=BKEYS, outs=BKEYS)
                v(lambda e, Y=Y, Bq=Bq: e.tensor_add(Y[:, 0:W2], Y[:, 0:W2],
                                                     Bq[:, 0:W2]),
                  ins=(Ykey,) + BKEYS, outs=(Ykey,))
                v(lambda e, Y=Y, Bq=Bq: e.tensor_sub(Y[:, 0:W2], Y[:, 0:W2],
                                                     Bq[:, 0:W2]),
                  ins=(Ykey,) + BKEYS, outs=(Ykey,))
                for j2 in range(TPC):
                    p.dma(lambda e, Y=Y, Td=Td, j2=j2: e.dma_start_transpose(
                        out=Td.ap().rearrange("p (m tt) -> p m tt",
                                              tt=CH)[:, :, 128*j2:128*j2+128],
                        in_=Y[:, j2*D:(j2+1)*D]),
                        ins=(Ykey,), outs=(Tkey,), xpose=True)

            def m_phase(c):
                Y, Ykey, Bq, BKEYS, Td, Tkey, pslots = lanes(c)
                for j2 in range(TPC):
                    tt = c * TPC + j2
                    scrX, xk0, xk1 = ((scr1, "scr1", "scr1b") if j2 % 2 == 0
                                      else (scr2, "scr2", "scr2b"))
                    for oh in range(2):
                        pst, pkey, off = pslots[j2][oh]
                        for k in range(8):
                            t(lambda e, j2=j2, oh=oh, k=k, pst=pst, off=off,
                              Td=Td: e.matmul(
                                pst[:, off:off+512],
                                Td[:, k*CH+128*j2:k*CH+128*j2+128],
                                wT["f"][:, k*D+oh*512:k*D+(oh+1)*512],
                                start=(k == 0), stop=False, **MM),
                              ins=(Tkey, "wfT"), outs=(pkey,))
                        t(lambda e, oh=oh, pst=pst, off=off: e.matmul(
                            pst[:, off:off+512], ones1[:],
                            bias_bc[0:1, oh*512:(oh+1)*512],
                            start=False, stop=True, **MM),
                          ins=("ones1", "bias_bc"), outs=(pkey,))
                        a(lambda e, oh=oh, pst=pst, off=off, scrX=scrX:
                          e.activation(
                            scrX[:, oh*512:(oh+1)*512], pst[:, off:off+512],
                            AF.Identity, scale=scol[:, 2:3]),
                          ins=(pkey, "scol2"),
                          outs=(xk0 if oh == 0 else xk1,))
                    g(lambda e, tt=tt, scrX=scrX: e.tensor_add(
                        x_tok[:, tt*D:(tt+1)*D], x_tok[:, tt*D:(tt+1)*D],
                        scrX[:, 0:D]),
                      ins=(xk0, xk1, xq(tt)), outs=(xq(tt),))

            pb_iter = list(prep_blocks)
            per = -(-len(pb_iter) // NCH) if pb_iter else 0

            def emit_some(c):
                for nn, jj in pb_iter[c*per:(c+1)*per]:
                    emit_prep_block(nn, jj)

            def store_chunk(c):
                lo, hi = c * TPC, (c + 1) * TPC
                p.dma(lambda e, lo=lo, hi=hi: e.dma_start(
                    out=yv[:, lo:hi, :],
                    in_=x_tok[:, lo*D:hi*D].rearrange("p (tt d) -> p tt d", d=D)),
                    ins=(xq(lo), xq(lo + 1)), outs=(f"y{lo}",))

            depth = min(2, NCH)
            for c in range(depth):
                q_phase(c)
            for c in range(depth, NCH):
                m_phase(c - depth)
                emit_some(c - depth)
                if store:
                    store_chunk(c - depth)
                q_phase(c)
            for c in range(NCH - depth, NCH):
                m_phase(c)
                emit_some(c)
                if store:
                    store_chunk(c)

        wfb = wT["f"]

        def quartet(c):
            if c % 2 == 0:
                return (sqT.ap(), "sqT", skT.ap(), "skT", sk_tok.ap(),
                        "sk_tok", v_tok.ap(), "v_tok")
            return (wfb[:, 0:2048], "sqTo", wfb[:, 2048:4096], "skTo",
                    wfb[:, 4096:6144], "sk_toko", wfb[:, 6144:8192], "v_toko")

        def acol(h):
            return (h % 2) * 512 + 64 * (h // 2)

        def attn_front_parts(c):
            sq, sqk, sk, skk, skt, sktk, vt, vtk = quartet(c)
            pwi = [0]

            def grp():
                r = [(ps_w1, "psw1"), (ps_w2, "psw2")][pwi[0] % 2]
                pwi[0] += 1
                return r

            def sigma(srcT, skey, dst, dkey):
                for hh in range(2):
                    sl = (slice(None), slice(hh*1024, (hh+1)*1024))
                    xb = ln_tok[:, hh*1024:(hh+1)*1024]
                    xbk = "lnbuf" if hh == 0 else "lnbufB"
                    v(lambda e, sl=sl, srcT=srcT, xb=xb: e.tensor_scalar(
                        xb, srcT[sl], 0.0, None, OP.min),
                      ins=(skey,), outs=(xbk,))
                    a(lambda e, xb=xb: e.activation(xb, xb, AF.Exp),
                      ins=(xbk,), outs=(xbk,))
                    v(lambda e, sl=sl, srcT=srcT, dst=dst, xb=xb:
                      e.scalar_tensor_tensor(
                        dst[sl], srcT[sl], 1.0, xb, OP.add, OP.max),
                      ins=(skey, xbk), outs=(dkey,))

            def proj_fm(wtag, wkey, dst, dkey):
                for oj in range(8):
                    pb, pbk = grp()
                    for k in range(8):
                        t(lambda e, oj=oj, k=k, pb=pb, wtag=wtag: e.matmul(
                            pb[:, 0:CH], wT[wtag][:, k*D+128*oj:k*D+128*oj+128],
                            lnT[:, k*CH:(k+1)*CH], start=(k == 0),
                            stop=(k == 7), **MM),
                          ins=("lnT", wkey), outs=(pbk,))
                    if oj % 2 == 0:
                        a(lambda e, oj=oj, pb=pb, dst=dst: e.activation(
                            dst[:, oj*CH:(oj+1)*CH], pb[:, 0:CH], AF.Copy),
                          ins=(pbk,), outs=(dkey,))
                    else:
                        v(lambda e, oj=oj, pb=pb, dst=dst: e.tensor_copy(
                            dst[:, oj*CH:(oj+1)*CH], pb[:, 0:CH]),
                          ins=(pbk,), outs=(dkey,))

            def p0():
                for j in range(TPC):
                    tt = c * TPC + j
                    ln_stats(tt)
                    ln_tile(tt, j)
                for j in range(TPC):
                    p.dma(lambda e, j=j: e.dma_start_transpose(
                        out=lnT.ap().rearrange("p (m tt) -> p m tt",
                                               tt=CH)[:, :, 128*j:128*j+128],
                        in_=ln_tok[:, j*D:(j+1)*D]),
                        ins=("lnbuf" if j == 0 else "lnbufB",), outs=("lnT",),
                        xpose=True)

            def p1():
                proj_fm("q", "wqT", qT, "qT")
                sigma(qT, "qT", sq, sqk)

            def p2():
                proj_fm("k", "wkT", kT, "kT")
                sigma(kT, "kT", sk, skk)

            def p3():
                for j in range(TPC):
                    for oh in range(2):
                        pb, pbk = grp()
                        for k in range(8):
                            t(lambda e, j=j, oh=oh, k=k, pb=pb: e.matmul(
                                pb[:, 0:512],
                                lnT[:, k*CH+128*j:k*CH+128*j+128],
                                wT["v"][:, k*D+oh*512:k*D+(oh+1)*512],
                                start=(k == 0), stop=(k == 7), **MM),
                              ins=("lnT", "wvT"), outs=(pbk,))
                        a(lambda e, j=j, oh=oh, pb=pb, vt=vt: e.activation(
                            vt[:, j*D+oh*512:j*D+(oh+1)*512], pb[:, 0:512],
                            AF.Copy),
                          ins=(pbk,), outs=(vtk,))
                for k in range(8):
                    p.dma(lambda e, k=k, sk=sk, skt=skt: e.dma_start_transpose(
                        out=skt[:, 0:TPC*D].rearrange(
                            "p (j dd) -> p j dd", dd=D)[:, :, 128*k:128*k+128],
                        in_=sk[:, k*CH:(k+1)*CH]),
                        ins=(skk,), outs=(sktk,), xpose=True)
                pb, pbk = grp()
                for j in range(TPC):
                    for k in range(8):
                        t(lambda e, j=j, k=k, pb=pb, skt=skt: e.matmul(
                            pb[:, j*16+2*k:j*16+2*k+2],
                            skt[:, j*D+128*k:j*D+128*k+128], oneh2[:],
                            start=True, stop=True, **MM),
                          ins=(sktk, "oneh2"), outs=(pbk,))
                v(lambda e, pb=pb: e.tensor_copy(csum_f[:, 0:TPC*16],
                                                 pb[:, 0:TPC*16]),
                  ins=(pbk,), outs=("csum_f",))
                for s in range(2 * TPC):
                    j, sg = divmod(s, 2)
                    e0 = j * 4 + sg * 2
                    v(lambda e, e0=e0: e.tensor_copy(
                        z2[0:64, :]
                        .rearrange("p (kk eight) -> p kk eight",
                                   eight=8)[:, :, e0:e0+1],
                        zfm[0:64, :].rearrange("p (kk o) -> p kk o", o=1)),
                      ins=("zfm",), outs=("z2",))
                    v(lambda e, e0=e0: e.tensor_copy(
                        z2[64:128, :]
                        .rearrange("p (kk eight) -> p kk eight",
                                   eight=8)[:, :, e0+1:e0+2],
                        zfm[64:128, :].rearrange("p (kk o) -> p kk o", o=1)),
                      ins=("zfm",), outs=("z2",))
                    v(lambda e, j=j, sg=sg: e.tensor_add(
                        zfm[:].rearrange("p (kk o) -> p kk o", o=1),
                        zfm[:].rearrange("p (kk o) -> p kk o", o=1),
                        csum_f[:, j*16:(j+1)*16]
                        .rearrange("p (kk two) -> p kk two", two=2)[:, :, sg:sg+1]),
                      ins=("csum_f", "zfm"), outs=("zfm",))
                pb, pbk = grp()
                for j in range(TPC):
                    for k in range(8):
                        t(lambda e, j=j, k=k, pb=pb, sq=sq: e.matmul(
                            pb[:, k*8+j*4:k*8+j*4+4],
                            sq[:, k*CH+j*128:k*CH+j*128+128],
                            z2[:, k*8+j*4:k*8+j*4+4],
                            start=True, stop=True, **MM),
                          ins=(sqk, "z2"), outs=(pbk,))
                        t(lambda e, j=j, k=k, pb=pb, sk=sk: e.matmul(
                            pb[:, 64+k*8+j*4:64+k*8+j*4+4],
                            sk[:, k*CH+j*128:k*CH+j*128+128],
                            z2[:, k*8+j*4:k*8+j*4+4],
                            start=True, stop=True, **MM),
                          ins=(skk, "z2"), outs=(pbk,))
                v(lambda e, pb=pb: e.tensor_copy(dq_sb[:, 0:128],
                                                 pb[:, 0:128]),
                  ins=(pbk,), outs=("dq_sb",))
                v(lambda e: e.tensor_scalar(dq_sb[:, 0:128],
                                            dq_sb[:, 0:128], 1e-6,
                                            None, OP.add),
                  ins=("dq_sb",), outs=("dq_sb",))
                v(lambda e: e.reciprocal(rdq[:, 0:128], dq_sb[:, 0:128]),
                  ins=("dq_sb",), outs=("rdq",))
            return [p0, p1, p2, p3]

        def attn_mid(c):
            sq, sqk, sk, skk, skt, sktk, vt, vtk = quartet(c)
            for j in range(TPC):
                cp, cpk = (c1p, "c1p") if j == 0 else (c1pB, "c1pB")
                for sg, lo in ((0, 0), (1, 64)):
                    e0 = j * 4 + sg * 2
                    v(lambda e, cp=cp, lo=lo, e0=e0: e.tensor_tensor(
                        cp[lo:lo+64, :].rearrange("p (i two) -> p i two", two=2),
                        rdq[lo:lo+64, 0:64]
                        .rearrange("p (kk eight) -> p kk eight",
                                   eight=8)[:, :, e0:e0+2],
                        betac[lo:lo+64, :].rearrange("p (i two) -> p i two",
                                                     two=2),
                        OP.mult),
                      ins=("rdq", "betac"), outs=(cpk,))
                    v(lambda e, j=j, lo=lo, e0=e0: e.tensor_copy(
                        c3_sb[lo:lo+64, j*16:(j+1)*16]
                        .rearrange("p (i two) -> p i two", two=2),
                        rdq[lo:lo+64, 64:128]
                        .rearrange("p (kk eight) -> p kk eight",
                                   eight=8)[:, :, e0:e0+2]),
                      ins=("rdq",), outs=("c3_sb",))
            for j in range(TPC):
                g(lambda e, j=j, sk=sk, skt=skt: e.tensor_tensor(
                    sk[:, j*D:(j+1)*D].rearrange("p (hh kk) -> p hh kk", kk=64),
                    skt[:, j*D:(j+1)*D].rearrange("p (hh kk) -> p hh kk", kk=64),
                    c3_sb[:, j*16:(j+1)*16].rearrange("p (hh o) -> p hh o", o=1)
                    .broadcast_to([128, 16, 64]), OP.mult),
                  ins=(sktk, "c3_sb", skk), outs=(skk,))
            for j in range(TPC):
                pt, ptk = (pt_un, "pt_un") if j == 0 else (scr1.ap(), "scr1")
                pc, pck = (pt_c2, "pt_c2") if j == 0 else (scr2.ap(), "scr2")
                ptks = (ptk,) if j == 0 else ("scr1", "scr1b")
                pcks = (pck,) if j == 0 else ("scr2", "scr2b")
                for sg in range(2):
                    for par in range(2):
                        for h in range(par, H, 2):
                            t(lambda e, h=h, sg=sg, j=j: e.matmul(
                                ps_sc[sg*64:sg*64+64, acol(h):acol(h)+64],
                                kT[(h % 2)*64:(h % 2)*64+64,
                                   (h//2)*CH+j*128+sg*64:(h//2)*CH+j*128+sg*64+64],
                                qT[(h % 2)*64:(h % 2)*64+64,
                                   (h//2)*CH+j*128+sg*64:(h//2)*CH+j*128+sg*64+64],
                                start=True, stop=True,
                                tile_position=((h % 2)*64, sg*64), **MM),
                              ins=("kT", "qT"),
                              outs=("ps_scA" if par == 0 else "ps_scB",))
                for par in range(2):
                    a(lambda e, par=par, pt=pt: e.activation(
                        pt[:, par*512:(par+1)*512], ps_sc[:, par*512:(par+1)*512],
                        AF.Exp, bias=-40.0, scale=0.125),
                      ins=("ps_scA" if par == 0 else "ps_scB",), outs=ptks)
                for lo in (0, 64):
                    g(lambda e, pt=pt, lo=lo: e.affine_select(
                        pt[lo:lo+64, :].rearrange("p (hh tt) -> p hh tt", tt=64),
                        pt[lo:lo+64, :].rearrange("p (hh tt) -> p hh tt", tt=64),
                        pattern=[[0, 16], [1, 64]], compare_op=OP.is_ge, fill=0.0,
                        base=0, channel_multiplier=-1),
                      ins=ptks, outs=ptks)
                t(lambda e, pt=pt: e.matmul(ps_w1[0:2, 0:512], oneh2[:],
                                            pt[:, 0:512], start=True, stop=True,
                                            **MM),
                  ins=("oneh2",) + ptks, outs=("psw1",))
                t(lambda e, pt=pt: e.matmul(ps_w2[0:2, 0:512], oneh2[:],
                                            pt[:, 512:1024], start=True,
                                            stop=True, **MM),
                  ins=("oneh2",) + ptks, outs=("psw2",))
                for pw, pwk in ((ps_w1, "psw1"), (ps_w2, "psw2")):
                    a(lambda e, pw=pw: e.activation(pw[0:2, 0:512],
                                                    pw[0:2, 0:512], AF.Ln),
                      ins=(pwk,), outs=(pwk,))
                    a(lambda e, pw=pw: e.activation(pw[0:2, 0:512],
                                                    pw[0:2, 0:512], AF.Exp,
                                                    scale=-1.0),
                      ins=(pwk,), outs=(pwk,))
                v(lambda e: e.tensor_tensor(c2bE[:], ps_w1[0:2, 0:512], betaE[:],
                                            OP.mult),
                  ins=("psw1", "betaE"), outs=("c2pool",))
                v(lambda e: e.tensor_tensor(c2bO[:], ps_w2[0:2, 0:512], betaO[:],
                                            OP.mult),
                  ins=("psw2", "betaO"), outs=("c2pool",))
                t(lambda e: e.matmul(ps_w1[:, 0:512], bt2[:], c2bE[:],
                                     start=True, stop=True, **MM),
                  ins=("bt2", "c2pool"), outs=("psw1",))
                t(lambda e: e.matmul(ps_w2[:, 0:512], bt2[:], c2bO[:],
                                     start=True, stop=True, **MM),
                  ins=("bt2", "c2pool"), outs=("psw2",))
                v(lambda e, pt=pt, pc=pc: e.tensor_tensor(
                    pc[:, 0:512], pt[:, 0:512], ps_w1[:, 0:512], OP.mult),
                  ins=ptks + ("psw1",), outs=pcks)
                v(lambda e, pt=pt, pc=pc: e.tensor_tensor(
                    pc[:, 512:1024], pt[:, 512:1024], ps_w2[:, 0:512], OP.mult),
                  ins=ptks + ("psw2",), outs=pcks)

        def attn_back_seg(c, s):
            sq, sqk, sk, skk, skt, sktk, vt, vtk = quartet(c)
            j, sg = divmod(s, 2)
            pc, pcks = (pt_c2, ("pt_c2",)) if j == 0 \
                else (scr2.ap(), ("scr2", "scr2b"))
            cp, cpk = (c1p, "c1p") if j == 0 else (c1pB, "c1pB")
            ib, ibk = (ib_sb.ap(), "ib_sb") if s % 2 == 0 \
                else (bias_bc[:, 0:512], "bias_bc")
            pboff = 0 if s % 2 == 0 else 512
            pbk_sc = "ps_scA" if s % 2 == 0 else "ps_scB"
            for par in range(2):
                for h in range(par, H, 2):
                    t(lambda e, h=h, sg=sg, j=j, sk=sk, skt=skt,
                      pboff=pboff: e.matmul(
                        ps_sc[(h % 2)*64:(h % 2)*64+64,
                              pboff+64*(h//2):pboff+64*(h//2)+64],
                        sk[sg*64:sg*64+64, j*D+64*h:j*D+64*h+64],
                        skt[sg*64:sg*64+64, j*D+64*h:j*D+64*h+64],
                        start=True, stop=True,
                        tile_position=(sg*64, (h % 2)*64), **MM),
                      ins=(skk, sktk), outs=(pbk_sc,))
            v(lambda e, ib=ib, pboff=pboff: e.scalar_tensor_tensor(
                ib[:, 0:512].rearrange("p (i kk) -> p i kk", kk=64),
                ps_sc[:, pboff:pboff+512].rearrange("p (i kk) -> p i kk", kk=64),
                -1.0,
                irep[:].rearrange("p (o kk) -> p o kk", o=1)
                .broadcast_to([128, 8, 64]),
                OP.mult, OP.add),
              ins=(pbk_sc, "irep"), outs=(ibk,))
            for par in range(2):
                for h in range(par, H, 2):
                    t(lambda e, h=h, sg=sg, j=j, sq=sq: e.matmul(
                        ps_a[sg*64:sg*64+64, acol(h):acol(h)+64],
                        sq[(h % 2)*64:(h % 2)*64+64,
                           (h//2)*CH+j*128+sg*64:(h//2)*CH+j*128+sg*64+64],
                        m_sb[(h % 2)*64:(h % 2)*64+64, 64*(h//2):64*(h//2)+64],
                        start=True, stop=True,
                        tile_position=((h % 2)*64, sg*64), **MM),
                      ins=(sqk, "m_sbA" if h % 2 == 0 else "m_sbB"),
                      outs=("ps_aA" if h % 2 == 0 else "ps_aB",))
            for par in range(2):
                for h in range(par, H, 2):
                    pw = ps_w1 if h % 2 == 0 else ps_w2
                    t(lambda e, h=h, sg=sg, j=j, pc=pc, vt=vt, pw=pw: e.matmul(
                        pw[sg*64:sg*64+64, 64*(h//2):64*(h//2)+64],
                        pc[sg*64:sg*64+64, acol(h):acol(h)+64],
                        vt[sg*64:sg*64+64, j*D+64*h:j*D+64*h+64],
                        start=True, stop=True,
                        tile_position=(sg*64, sg*64), **MM),
                      ins=pcks + (vtk,),
                      outs=("psw1" if h % 2 == 0 else "psw2",))
            for par in range(2):
                asl = a_s[:, j*D:(j+1)*D].rearrange(
                    "p (i f) -> p i f", f=128)[sg*64:sg*64+64, :,
                                               par*64:par*64+64]
                v(lambda e, asl=asl, sg=sg, par=par, cp=cp: e.tensor_tensor(
                    asl,
                    ps_a[sg*64:sg*64+64, par*512:par*512+512]
                    .rearrange("p (i kk) -> p i kk", kk=64),
                    cp[sg*64:sg*64+64, :]
                    .rearrange("p (i two) -> p i two", two=2)[:, :, par:par+1]
                    .broadcast_to([64, 8, 64]),
                    OP.mult),
                  ins=("ps_aA" if par == 0 else "ps_aB", cpk), outs=("a_s",))
                pw = ps_w1 if par == 0 else ps_w2
                v(lambda e, asl=asl, sg=sg, par=par, pw=pw: e.tensor_add(
                    asl, asl,
                    pw[sg*64:sg*64+64, 0:512]
                    .rearrange("p (i kk) -> p i kk", kk=64)),
                  ins=("a_s", "psw1" if par == 0 else "psw2"),
                  outs=("a_s",))
            for par in range(2):
                for h in range(par, H, 2):
                    t(lambda e, h=h, par=par, ib=ib: e.matmul(
                        ps_m[(h % 2)*64:(h % 2)*64+64, acol(h):acol(h)+64],
                        ib[(h % 2)*64:(h % 2)*64+64, 64*(h//2):64*(h//2)+64],
                        m_sb[(h % 2)*64:(h % 2)*64+64, 64*(h//2):64*(h//2)+64],
                        start=(h == par), stop=False,
                        tile_position=((h % 2)*64, (h % 2)*64), **MM),
                      ins=(ibk, "m_sbA" if h % 2 == 0 else "m_sbB"),
                      outs=("ps_mA" if h % 2 == 0 else "ps_mB",))
            for par in range(2):
                for h in range(par, H, 2):
                    t(lambda e, h=h, sg=sg, j=j, skt=skt, vt=vt: e.matmul(
                        ps_m[(h % 2)*64:(h % 2)*64+64, acol(h):acol(h)+64],
                        skt[sg*64:sg*64+64, j*D+64*h:j*D+64*h+64],
                        vt[sg*64:sg*64+64, j*D+64*h:j*D+64*h+64],
                        start=False, stop=True,
                        tile_position=(sg*64, (h % 2)*64), **MM),
                      ins=(sktk, vtk),
                      outs=("ps_mA" if h % 2 == 0 else "ps_mB",))
            a(lambda e: e.activation(m_sb[0:64, :], ps_m[0:64, 0:512],
                                     AF.Copy),
              ins=("ps_mA",), outs=("m_sbA",))
            a(lambda e: e.activation(m_sb[64:128, :], ps_m[64:128, 512:1024],
                                     AF.Copy),
              ins=("ps_mB",), outs=("m_sbB",))

        def attn_back_tail(c):
            for j in range(TPC):
                p.dma(lambda e, j=j: e.dma_start_transpose(
                    out=asT.ap().rearrange("p (m tt) -> p m tt",
                                           tt=CH)[:, :, 128*j:128*j+128],
                    in_=a_s[:, j*D:(j+1)*D]),
                    ins=("a_s",), outs=("pt_un", "pt_c2"), xpose=True)
            for j in range(TPC):
                tt = c * TPC + j
                for oh in range(2):
                    for k in range(8):
                        t(lambda e, j=j, oh=oh, k=k: e.matmul(
                            ps_m[:, oh*512:(oh+1)*512],
                            asT[:, k*CH+128*j:k*CH+128*j+128],
                            wT["o"][:, k*D+oh*512:k*D+(oh+1)*512],
                            start=(k == 0), stop=(k == 7), **MM),
                          ins=("pt_un", "pt_c2", "woT"),
                          outs=("ps_mA" if oh == 0 else "ps_mB",))
                    v(lambda e, tt=tt, oh=oh: e.tensor_add(
                        x_tok[:, tt*D+oh*512:tt*D+(oh+1)*512],
                        x_tok[:, tt*D+oh*512:tt*D+(oh+1)*512],
                        ps_m[:, oh*512:(oh+1)*512]),
                      ins=("ps_mA" if oh == 0 else "ps_mB", xq(tt)),
                      outs=(xq(tt),))

        def rebuild_for_ffn2():
            for jp in range(4):
                wf_pair(jp, extra=("sqTo", "skTo", "sk_toko", "v_toko"))

        yv = y_out.rearrange("(tt p) d -> p tt d", p=128)
        qkvo_blocks = [(nn, jj) for nn in ("q", "k", "v", "o") for jj in range(8)]
        ffn_stats(range(NT), wf_pairs=True, qkvo=qkvo_blocks)
        ffn_amax(range(NT))
        ffn("f1", stats_done=True)
        for f in attn_front_parts(0):
            f()
        for c in range(NCH):
            attn_mid(c)
            nxt = attn_front_parts(c + 1) if c + 1 < NCH else None
            for s in range(2 * TPC):
                attn_back_seg(c, s)
                if nxt is not None and s < 4:
                    nxt[s]()
            if c == NCH - 1:
                rebuild_for_ffn2()
            attn_back_tail(c)
            ffn_stats([c * TPC + j for j in range(TPC)])
            ffn_amax([c * TPC + j for j in range(TPC)])
        ffn("f2", store=True, stats_done=True)
        p.dma_fence()

        @blk.tensor
        def _(e):
            for gi, fn in p.body["te"]:
                if TRUNC is None or gi <= TRUNC:
                    fn(e)

        @blk.vector
        def _(e):
            for gi, fn in p.body["dv"]:
                if TRUNC is None or gi <= TRUNC:
                    fn(e)

        @blk.scalar
        def _(e):
            for gi, fn in p.body["ac"]:
                if TRUNC is None or gi <= TRUNC:
                    fn(e)

        @blk.gpsimd
        def _(e):
            for gi, fn in p.body["gp"]:
                if TRUNC is None or gi <= TRUNC:
                    fn(e)

        @blk.sync
        def _(e):
            for gi, fn in p.body["sy"]:
                if TRUNC is None or gi <= TRUNC:
                    fn(e)

    return nc


def kernel(L0_attn_x, L0_attn_W_q, L0_attn_W_k, L0_attn_W_v, L0_attn_W_o,
           L0_attn_beta_param, L0_ffn_weight, L0_ffn_bias):
    x = np.ascontiguousarray(np.asarray(L0_attn_x, dtype=np.float32))
    B, T, _ = x.shape
    nc = build_program(T=T)
    wmap = {
        "wq": np.ascontiguousarray(np.asarray(L0_attn_W_q, np.float32)),
        "wk": np.ascontiguousarray(np.asarray(L0_attn_W_k, np.float32)),
        "wv": np.ascontiguousarray(np.asarray(L0_attn_W_v, np.float32)),
        "wo": np.ascontiguousarray(np.asarray(L0_attn_W_o, np.float32)),
        "wf": np.ascontiguousarray(np.asarray(L0_ffn_weight, np.float32)),
        "fbias": np.ascontiguousarray(np.asarray(L0_ffn_bias, np.float32)).reshape(1, -1),
        "beta": np.ascontiguousarray(np.asarray(L0_attn_beta_param, np.float32)).reshape(1, -1),
    }
    in_maps = [dict(wmap, x=np.ascontiguousarray(x[i])) for i in range(B)]
    res = run_bass_kernel_spmd(nc, in_maps, core_ids=list(range(B)))
    out = np.stack([np.asarray(res.results[i]["y"]) for i in range(B)], axis=0)
    return out.astype(np.float32)


# version 38
# speedup vs baseline: 1.0192x; 1.0068x over previous
from contextlib import ExitStack

import numpy as np

import concourse.bass as bass
import concourse.mybir as mybir
from concourse.bass_utils import run_bass_kernel_spmd
from concourse.alu_op_type import AluOpType as OP

F32 = mybir.dt.float32
BF16 = mybir.dt.bfloat16
U16 = mybir.dt.uint16
FP8 = mybir.dt.float8e4
AF = mybir.ActivationFunctionType
AX = mybir.AxisListType.X

D = 1024
H = 16
DK = 64
NCORES = 8
AMAX_MODE = "core"
DBG_SEGS = None
TRUNC = None
_last_counts = {}

ACT_CONSTS = [-40.0, 0.125, 1e-5, -0.5, 1.0, 0.0, -1.0, 448.0, 0.5]


class Prog:

    ENGS = ("te", "dv", "ac", "gp", "sy")

    NQ = 8

    def __init__(self, nc):
        self.nc = nc
        self.gidx = 0
        self.body = {e: [] for e in self.ENGS}
        self.sem = {}
        self.cnt = {e: 0 for e in self.ENGS}
        for i in range(self.NQ):
            self.cnt[f"syq{i}"] = 0
        self.waited = {e: {} for e in self.ENGS}
        self.lastw = {}
        self.readers = {}
        self._xfence = False
        self._dmarot = 0

    def alloc_sems(self, ctx):
        for s in list(self.ENGS) + [f"syq{i}" for i in range(self.NQ)]:
            self.sem[s] = ctx.enter_context(self.nc.semaphore(s))

    def _dep(self, eng, deps):
        merged = {}
        for d in deps:
            if d is None:
                continue
            s, vv = d
            if eng == "te" and s == "te":
                continue
            if merged.get(s, 0) < vv:
                merged[s] = vv
        for s, vv in sorted(merged.items()):
            if self.waited[eng].get(s, 0) >= vv:
                continue
            self.waited[eng][s] = vv
            sem = self.sem[s]
            self.body[eng].append((self.gidx, lambda e, sem=sem, vv=vv: e.wait_ge(sem, vv)))

    def op(self, eng, fn, ins=(), outs=(), sem=None, inc=1):
        deps = []
        for k in ins:
            deps.append(self.lastw.get(k))
        for k in outs:
            deps.append(self.lastw.get(k))
            for s, vv in self.readers.get(k, {}).items():
                deps.append((s, vv))
        self._dep(eng, deps)
        semname = sem or eng
        self.cnt[semname] += inc
        val = self.cnt[semname]
        semh = self.sem[semname]

        def emit(e, fn=fn, semh=semh, inc=inc):
            fn(e).then_inc(semh, inc)
        self.gidx += 1
        self.body[eng].append((self.gidx, emit))
        for k in outs:
            self.lastw[k] = (semname, val)
            self.readers[k] = {}
        for k in ins:
            r = self.readers.setdefault(k, {})
            if r.get(semname, 0) < val:
                r[semname] = val
        return (semname, val)

    def dma_fence(self):
        self._dep("sy", [(f"syq{i}", self.cnt[f"syq{i}"]) for i in range(self.NQ)])

    def dma(self, fn, ins=(), outs=(), xpose=False):
        if xpose or self._xfence:
            self.dma_fence()
            self._xfence = xpose
        r = self._dmarot
        self._dmarot = (r + 1) % self.NQ
        sname = f"syq{r}"
        self._dep("sy", [(sname, self.cnt[sname])])
        return self.op("sy", fn, ins=ins, outs=outs, sem=sname, inc=16)


def build_program(T=2048, CH=256):
    NT = T // 128
    NCH = T // CH
    TPC = CH // 128
    assert NT <= 16
    nc = bass.Bass("TRN2", target_bir_lowering=False)
    p = Prog(nc)

    x_in = nc.dram_tensor("x", [T, D], F32, kind="ExternalInput").ap()
    win = {
        "q": nc.dram_tensor("wq", [D, D], F32, kind="ExternalInput").ap(),
        "k": nc.dram_tensor("wk", [D, D], F32, kind="ExternalInput").ap(),
        "v": nc.dram_tensor("wv", [D, D], F32, kind="ExternalInput").ap(),
        "o": nc.dram_tensor("wo", [D, D], F32, kind="ExternalInput").ap(),
        "f": nc.dram_tensor("wf", [D, D], F32, kind="ExternalInput").ap(),
    }
    bias_in = nc.dram_tensor("fbias", [1, D], F32, kind="ExternalInput").ap()
    beta_in = nc.dram_tensor("beta", [1, H], F32, kind="ExternalInput").ap()
    y_out = nc.dram_tensor("y", [T, D], F32, kind="ExternalOutput").ap()

    ctx = ExitStack()
    sb = lambda name, shape, dt: ctx.enter_context(nc.sbuf_tensor(name, shape, dt))
    psum = lambda name, shape: ctx.enter_context(nc.psum_tensor(name, shape, F32))

    with ctx:
        p.alloc_sems(ctx)

        x_tok = sb("x_tok", [128, NT * D], F32)

        def xq(tt):
            return f"x_t{tt}"
        wT = {n: sb(f"w{n}T", [128, 8 * D], BF16) for n in ("q", "k", "v", "o", "f")}
        stage = sb("stage", [128, D], F32)
        lnT = sb("lnT", [128, 8 * CH], BF16)
        qT = sb("qT", [128, 8 * CH], BF16)
        kT = sb("kT", [128, 8 * CH], BF16)
        sqT = sb("sqT", [128, 8 * CH], BF16)
        skT = sb("skT", [128, 8 * CH], BF16)
        sk_tok = sb("sk_tok", [128, TPC * D], BF16)
        v_tok = sb("v_tok", [128, TPC * D], BF16)
        a_s = sb("a_s", [128, TPC * D], BF16)
        ln_tok = sb("ln_tok", [128, TPC * D], BF16)
        ptbuf = sb("ptbuf", [128, 2 * D], BF16)
        pt_un = ptbuf[:, 0:D]
        pt_c2 = ptbuf[:, D:2*D]
        scr1 = sb("scr1", [128, D], BF16)
        scr2 = sb("scr2", [128, D], BF16)
        ib_sb = sb("ib_sb", [128, 512], BF16)
        m_sb = sb("m_sb", [128, 512], BF16)
        stats = sb("stats", [128, 112], F32)
        zfm = sb("zfm", [128, 8], F32)
        z2 = sb("z2", [128, 64], BF16)
        csum_f = sb("csum_f", [128, 64], F32)
        dq_sb = sb("dq_sb", [128, 128], F32)
        rdq = sb("rdq", [128, 128], F32)
        c1p = sb("c1p", [128, 16], BF16)
        c1pB = sb("c1pB", [128, 16], BF16)
        c3_sb = sb("c3_sb", [128, 64], BF16)
        c2pool = sb("c2pool", [2, 1024], BF16)
        rowred = sb("rowred", [128, 48], F32)
        rows = sb("rows", [1, 200], F32)
        scal = sb("scal", [1, 16], F32)
        scol = sb("scol", [128, 4], F32)
        betaE = sb("betaE", [2, 512], BF16)
        betaO = sb("betaO", [2, 512], BF16)
        betac = sb("betac", [128, 16], BF16)
        b4 = sb("b4", [4, 16], BF16)
        bias_bc = sb("bias_bc", [128, D], BF16)
        beta_row = sb("beta_row", [1, H], F32)
        cpool = sb("cpool", [128, 12], F32)
        ones1 = sb("ones1", [1, 128], BF16)
        ones1f = sb("ones1f", [1, 128], F32)
        idf = sb("idf", [128, 128], F32)
        idb = sb("idb", [128, 128], BF16)
        bt2 = sb("bt2", [2, 128], BF16)
        oneh2 = sb("oneh2", [128, 2], BF16)
        irep = sb("irep", [128, 64], BF16)

        stage_bf = pt_c2
        c1fm = ln_tok
        sk3_tok = skT
        asT = ptbuf
        scr3 = pt_un
        c2bE = c2pool[:, 0:512]
        c2bO = c2pool[:, 512:1024]

        ps_sc = psum("ps_sc", [128, 1024])
        ps_a = psum("ps_a", [128, 1024])
        ps_m = psum("ps_m", [128, 1024])
        ps_w1 = psum("ps_w1", [128, 512])
        ps_w2 = psum("ps_w2", [128, 512])

        for i, vv in enumerate(ACT_CONSTS):
            nc.const_aps.register(F32, float(vv), cpool[:, i:i+1])

        import sys
        print(f"[build] sbuf bytes remaining/partition: {nc.sbuf_bytes_remaining}",
              file=sys.stderr)
        blk = ctx.enter_context(nc.Block())

        def t(fn, ins=(), outs=()):
            return p.op("te", fn, ins=ins, outs=outs)

        def v(fn, ins=(), outs=()):
            return p.op("dv", fn, ins=ins, outs=outs)

        def a(fn, ins=(), outs=()):
            return p.op("ac", fn, ins=ins, outs=outs)

        def g(fn, ins=(), outs=()):
            return p.op("gp", fn, ins=ins, outs=outs)

        MM = dict(skip_group_check=True)

        for i, vv in enumerate(ACT_CONSTS):
            g(lambda e, i=i, vv=vv: e.memset(cpool[:, i:i+1], float(vv)),
              outs=(f"cp{i}",))
        g(lambda e: e.memset(ones1[:], 1.0), outs=("ones1",))
        g(lambda e: e.memset(ones1f[:], 1.0), outs=("ones1f",))
        for nm, tile in (("idf", idf), ("idb", idb)):
            g(lambda e, tile=tile: e.memset(tile[:], 1.0), outs=(nm,))
            g(lambda e, tile=tile: e.affine_select(
                tile[:], tile[:], pattern=[[-1, 128]], compare_op=OP.is_equal,
                fill=0.0, base=0, channel_multiplier=1), ins=(nm,), outs=(nm,))
        g(lambda e: e.memset(irep[:], 1.0), outs=("irep",))
        g(lambda e: e.affine_select(irep[0:64, :], irep[0:64, :], pattern=[[-1, 64]],
                                    compare_op=OP.is_equal, fill=0.0, base=0,
                                    channel_multiplier=1), ins=("irep",), outs=("irep",))
        g(lambda e: e.affine_select(irep[64:128, :], irep[64:128, :], pattern=[[-1, 64]],
                                    compare_op=OP.is_equal, fill=0.0, base=0,
                                    channel_multiplier=1), ins=("irep",), outs=("irep",))
        g(lambda e: e.memset(oneh2[:], 1.0), outs=("oneh2",))
        g(lambda e: e.affine_select(oneh2[:, 0:1], oneh2[:, 0:1], pattern=[[0, 1]],
                                    compare_op=OP.is_ge, fill=0.0, base=63,
                                    channel_multiplier=-1), ins=("oneh2",), outs=("oneh2",))
        g(lambda e: e.affine_select(oneh2[:, 1:2], oneh2[:, 1:2], pattern=[[0, 1]],
                                    compare_op=OP.is_ge, fill=0.0, base=-64,
                                    channel_multiplier=1), ins=("oneh2",), outs=("oneh2",))
        g(lambda e: e.memset(bt2[:], 1.0), outs=("bt2",))
        g(lambda e: e.affine_select(bt2[:], bt2[:], pattern=[[1, 128]],
                                    compare_op=OP.is_ge, fill=0.0, base=0,
                                    channel_multiplier=-64),
          ins=("bt2",), outs=("bt2",))
        g(lambda e: e.affine_select(bt2[:], bt2[:], pattern=[[-1, 128]],
                                    compare_op=OP.is_ge, fill=0.0, base=63,
                                    channel_multiplier=64),
          ins=("bt2",), outs=("bt2",))
        g(lambda e: e.memset(zfm[:], 0.0), outs=("zfm",))
        g(lambda e: e.memset(z2[:], 0.0), outs=("z2",))
        g(lambda e: e.memset(m_sb[:], 0.0), outs=("m_sbA", "m_sbB"))
        if DBG_SEGS is not None:
            g(lambda e: e.memset(a_s[:], 0.0), outs=("a_s",))

        xv = x_in.rearrange("(tt p) d -> p tt d", p=128)
        nsplit = min(4, NT)
        for q4 in range(nsplit):
            lo, hi = q4 * NT // nsplit, (q4 + 1) * NT // nsplit
            p.dma(lambda e, lo=lo, hi=hi: e.dma_start(
                out=x_tok[:, lo*D:hi*D].rearrange("p (tt d) -> p tt d", d=D),
                in_=xv[:, lo:hi, :]),
                outs=tuple({xq(tt) for tt in range(lo, hi)}))

        def quant_buf(src_ap, src_key, dst_ap, dst_key, scol_idx):
            SC1 = ("scr1", "scr1b")
            SC2 = ("scr2", "scr2b")
            a(lambda e: e.activation(
                scr1[:, 0:D], src_ap, AF.Identity, scale=scol[:, scol_idx:scol_idx+1]),
              ins=(src_key, f"scol{scol_idx}"), outs=SC1)
            v(lambda e: e.tensor_scalar(scr3[:, 0:D].bitcast(U16), scr1[:, 0:D].bitcast(U16),
                                        0x7F80, None, OP.bitwise_and),
              ins=SC1, outs=("pt_un",))
            v(lambda e: e.tensor_scalar(scr2[:, 0:D], scr3[:, 0:D],
                                        24.0, 0.375, OP.mult, OP.max),
              ins=("pt_un",), outs=SC2)
            v(lambda e: e.tensor_add(scr1[:, 0:D], scr1[:, 0:D], scr2[:, 0:D]),
              ins=SC1 + SC2, outs=SC1)
            v(lambda e: e.tensor_sub(dst_ap, scr1[:, 0:D], scr2[:, 0:D]),
              ins=SC1 + SC2, outs=(dst_key,))

        def cross_part_max(colkey, col_ap, out_idx, out_key):
            t(lambda e: e.transpose(ps_w1[0:1, 0:128], col_ap, idf[:]),
              ins=(colkey, "idf"), outs=("psw1",))
            a(lambda e: e.activation(rows[0:1, 0:128], ps_w1[0:1, 0:128], AF.Copy),
              ins=("psw1",), outs=("rows128",))
            v(lambda e: e.tensor_reduce(scal[0:1, out_idx:out_idx+1],
                                        rows[0:1, 0:128], AX, OP.max),
              ins=("rows128",), outs=(out_key,))

        def bcast_scalar(in_idx, in_key, scol_idx):
            t(lambda e: e.matmul(ps_w2[:, 0:1], ones1f[:],
                                 scal[0:1, in_idx:in_idx+1], start=True, stop=True,
                                 **MM),
              ins=(in_key, "ones1f"), outs=("psw2",))
            a(lambda e: e.activation(scol[:, scol_idx:scol_idx+1], ps_w2[:, 0:1],
                                     AF.Copy),
              ins=("psw2",), outs=(f"scol{scol_idx}",))

        kTf = kT.ap().bitcast(F32)
        lanes_f32 = [(stage[:], "stage"), (kTf, "kT")]
        lanes_bf = [(sqT[:, 0:1024], "sqT"), (sqT[:, 1024:2048], "sqT")]

        def emit_prep_block(n, j):
            wsrc = win[n].rearrange("(j p) i -> p j i", p=128)
            ldst, lkey = lanes_f32[j % 2]
            conv, ckey = lanes_bf[j % 2]
            p.dma(lambda e, j=j, wsrc=wsrc, ldst=ldst: e.dma_start(
                out=ldst, in_=wsrc[:, j, :]), outs=(lkey,))
            if n == "f":
                quant_buf(ldst, lkey, conv, ckey, 0)
            else:
                v(lambda e, ldst=ldst, conv=conv: e.tensor_copy(conv, ldst),
                  ins=(lkey,), outs=(ckey,))
            pb, pbkey = (ps_w1, "psw1") if j % 2 == 0 else (ps_w2, "psw2")
            for cidx in range(8):
                t(lambda e, cidx=cidx, pb=pb, conv=conv: e.transpose(
                    pb[:, cidx*64:(cidx+1)*64].bitcast(BF16),
                    conv[:, cidx*128:(cidx+1)*128], idb[:]),
                  ins=(ckey, "idb"), outs=(pbkey,))
            if j % 2 == 0:
                v(lambda e, n=n, j=j, pb=pb: e.tensor_copy(
                    wT[n].ap().rearrange("p (c o) -> p c o", o=D)
                    [:, :, 128*j:128*j+128],
                    pb[:, 0:512].bitcast(BF16).rearrange(
                        "p (c o) -> p c o", o=128)),
                    ins=(pbkey,), outs=(f"w{n}T",))
            else:
                a(lambda e, n=n, j=j, pb=pb: e.activation(
                    wT[n].ap().rearrange("p (c o) -> p c o", o=D)
                    [:, :, 128*j:128*j+128],
                    pb[:, 0:512].bitcast(BF16).rearrange(
                        "p (c o) -> p c o", o=128),
                    AF.Copy),
                    ins=(pbkey,), outs=(f"w{n}T",))

        def prep_wf():
            wsrc = win["f"].rearrange("(j p) i -> p j i", p=128)
            for j in range(8):
                ldst, lkey = lanes_f32[j % 2]
                p.dma(lambda e, j=j, ldst=ldst: e.dma_start(
                    out=ldst, in_=wsrc[:, j, :]), outs=(lkey,))
                v(lambda e, j=j, ldst=ldst: e.tensor_reduce(
                    rowred[:, j:j+1], ldst, AX, OP.max,
                    apply_absolute_value=True), ins=(lkey,), outs=(f"wr{j}",))
            v(lambda e: e.tensor_reduce(rowred[:, 8:9], rowred[:, 0:8], AX, OP.max),
              ins=tuple(f"wr{j}" for j in range(8)), outs=("wamax_col",))
            cross_part_max("wamax_col", rowred[:, 8:9], 0, "wamax")
            v(lambda e: e.tensor_scalar(scal[0:1, 1:2], scal[0:1, 0:1], 1e-12,
                                        None, OP.max), ins=("wamax",), outs=("wt1",))
            v(lambda e: e.reciprocal(scal[0:1, 2:3], scal[0:1, 1:2]),
              ins=("wt1",), outs=("wt2",))
            v(lambda e: e.tensor_scalar(scal[0:1, 3:4], scal[0:1, 2:3], 448.0,
                                        448.0, OP.mult, OP.min),
              ins=("wt2",), outs=("s_w",))
            v(lambda e: e.reciprocal(scal[0:1, 4:5], scal[0:1, 3:4]),
              ins=("s_w",), outs=("rsw",))
            v(lambda e: e.tensor_scalar(scal[0:1, 4:5], scal[0:1, 4:5],
                                        0.5, None, OP.mult),
              ins=("rsw",), outs=("rsw",))
            bcast_scalar(3, "s_w", 0)

        def wf_pair(jp, extra=()):
            wsrc = win["f"].rearrange("(j p) i -> p j i", p=128)
            if True:
                j0 = 2 * jp
                p.dma(lambda e, j0=j0: e.dma_start(
                    out=stage[:], in_=wsrc[:, j0, :]), outs=("stage",))
                p.dma(lambda e, j0=j0: e.dma_start(
                    out=kTf, in_=wsrc[:, j0 + 1, :]), outs=("kT",))
                a(lambda e: e.activation(sqT[:, 0:1024], stage[:], AF.Identity,
                                         scale=scol[:, 0:1]),
                  ins=("stage", "scol0"), outs=("sqT",))
                a(lambda e: e.activation(sqT[:, 1024:2048], kTf, AF.Identity,
                                         scale=scol[:, 0:1]),
                  ins=("kT", "scol0"), outs=("sqT",))
                v(lambda e: e.tensor_scalar(
                    skT[:, 0:2048].bitcast(U16), sqT[:, 0:2048].bitcast(U16),
                    0x7F80, None, OP.bitwise_and),
                  ins=("sqT",), outs=("skT",))
                v(lambda e: e.tensor_scalar(skT[:, 0:2048], skT[:, 0:2048],
                                            24.0, 0.375, OP.mult, OP.max),
                  ins=("skT",), outs=("skT",))
                v(lambda e: e.tensor_add(sqT[:, 0:2048], sqT[:, 0:2048],
                                         skT[:, 0:2048]),
                  ins=("sqT", "skT"), outs=("sqT",))
                v(lambda e: e.tensor_sub(sqT[:, 0:2048], sqT[:, 0:2048],
                                         skT[:, 0:2048]),
                  ins=("sqT", "skT"), outs=("sqT",))
                for dj in range(2):
                    j = j0 + dj
                    pb, pbkey = (ps_w1, "psw1") if dj == 0 else (ps_w2, "psw2")
                    for cidx in range(8):
                        t(lambda e, dj=dj, cidx=cidx, pb=pb: e.transpose(
                            pb[:, cidx*64:(cidx+1)*64].bitcast(BF16),
                            sqT[:, dj*1024+cidx*128:dj*1024+(cidx+1)*128],
                            idb[:]),
                          ins=("sqT", "idb"), outs=(pbkey,))
                    if dj == 0:
                        v(lambda e, j=j, pb=pb: e.tensor_copy(
                            wT["f"].ap().rearrange("p (c o) -> p c o", o=D)
                            [:, :, 128*j:128*j+128],
                            pb[:, 0:512].bitcast(BF16).rearrange(
                                "p (c o) -> p c o", o=128)),
                            ins=(pbkey,), outs=("wfT",) + extra)
                    else:
                        a(lambda e, j=j, pb=pb: e.activation(
                            wT["f"].ap().rearrange("p (c o) -> p c o", o=D)
                            [:, :, 128*j:128*j+128],
                            pb[:, 0:512].bitcast(BF16).rearrange(
                                "p (c o) -> p c o", o=128),
                            AF.Copy),
                            ins=(pbkey,), outs=("wfT",) + extra)

        prep_wf()

        p.dma(lambda e: e.dma_start(out=beta_row[:], in_=beta_in), outs=("beta_row",))
        a(lambda e: e.activation(rows[0:1, 144:144+H], beta_row[0:1, :], AF.Exp, scale=-1.0),
          ins=("beta_row",), outs=("bexp",))
        v(lambda e: e.tensor_scalar(rows[0:1, 144:144+H], rows[0:1, 144:144+H], 1.0, None, OP.add),
          ins=("bexp",), outs=("bexp",))
        v(lambda e: e.reciprocal(rows[0:1, 160:160+H], rows[0:1, 144:144+H]),
          ins=("bexp",), outs=("beta_v",))
        v(lambda e: e.tensor_scalar(rows[0:1, 176:176+H], rows[0:1, 160:160+H], -1.0, 1.0,
                                    OP.mult, OP.add),
          ins=("beta_v",), outs=("bomb",))
        t(lambda e: e.matmul(ps_w1[0:4, 0:H], ones1f[0:1, 0:4], rows[0:1, 176:176+H],
                             start=True, stop=True, **MM),
          ins=("bomb", "ones1f"), outs=("psw1",))
        v(lambda e: e.tensor_copy(b4[:], ps_w1[0:4, 0:H]), ins=("psw1",), outs=("b4",))
        v(lambda e: e.tensor_copy(
            betaE[:].rearrange("p (j tt) -> p j tt", tt=64),
            b4[0:2, 0:16].rearrange("p (j s) -> p j s", s=2)[:, :, 0:1]
            .broadcast_to([2, 8, 64])),
          ins=("b4",), outs=("betaE",))
        v(lambda e: e.tensor_copy(
            betaO[:].rearrange("p (j tt) -> p j tt", tt=64),
            b4[0:2, 0:16].rearrange("p (j s) -> p j s", s=2)[:, :, 1:2]
            .broadcast_to([2, 8, 64])),
          ins=("b4",), outs=("betaO",))
        t(lambda e: e.matmul(ps_w2[:, 0:H], ones1f[:], rows[0:1, 160:160+H],
                             start=True, stop=True, **MM),
          ins=("beta_v", "ones1f"), outs=("psw2",))
        v(lambda e: e.tensor_copy(betac[:, 0:16], ps_w2[:, 0:H]),
          ins=("psw2",), outs=("betac",))

        def ln_stats(tt):
            xs = x_tok[:, tt*D:(tt+1)*D]
            v(lambda e, xs=xs, tt=tt: e.tensor_reduce(stats[:, tt:tt+1], xs, AX, OP.add),
              ins=(xq(tt),), outs=(f"sum{tt}",))
            a(lambda e, xs=xs, tt=tt: e.activation(
                scr1[:, 0:D], xs, AF.Square, accum_out=stats[:, 16+tt:17+tt]),
              ins=(xq(tt),), outs=("scr1", "scr1b", f"sq{tt}"))
            v(lambda e, tt=tt: e.tensor_scalar(stats[:, tt:tt+1], stats[:, tt:tt+1],
                                               1.0/D, None, OP.mult),
              ins=(f"sum{tt}",), outs=(f"m{tt}",))
            v(lambda e, tt=tt: e.tensor_scalar(stats[:, 16+tt:17+tt],
                                               stats[:, 16+tt:17+tt], 1.0/D, None,
                                               OP.mult),
              ins=(f"sq{tt}",), outs=(f"sq{tt}",))
            v(lambda e, tt=tt: e.tensor_tensor(stats[:, 48+tt:49+tt], stats[:, tt:tt+1],
                                               stats[:, tt:tt+1], OP.mult),
              ins=(f"m{tt}",), outs=(f"msq{tt}",))
            v(lambda e, tt=tt: e.tensor_sub(stats[:, 16+tt:17+tt],
                                            stats[:, 16+tt:17+tt],
                                            stats[:, 48+tt:49+tt]),
              ins=(f"sq{tt}", f"msq{tt}"), outs=(f"var{tt}",))
            a(lambda e, tt=tt: e.activation(stats[:, 32+tt:33+tt],
                                            stats[:, 16+tt:17+tt], AF.Ln, bias=1e-5),
              ins=(f"var{tt}",), outs=(f"lnv{tt}",))
            a(lambda e, tt=tt: e.activation(stats[:, 32+tt:33+tt],
                                            stats[:, 32+tt:33+tt], AF.Exp, scale=-0.5),
              ins=(f"lnv{tt}",), outs=(f"rstd{tt}",))
            v(lambda e, tt=tt: e.tensor_scalar(
                stats[:, 64+tt:65+tt], stats[:, tt:tt+1],
                stats[:, 32+tt:33+tt], -1.0, OP.mult, OP.mult),
              ins=(f"m{tt}", f"rstd{tt}"), outs=(f"nmr{tt}",))

        def ln_stats_pair(tt):
            xv2 = x_tok[:, tt*D:(tt+2)*D].rearrange("p (two d) -> p two d", d=D)
            ik = (xq(tt), xq(tt + 1))
            sk2 = (f"sum{tt}", f"sum{tt+1}")
            qk2 = (f"sq{tt}", f"sq{tt+1}")
            mk2 = (f"m{tt}", f"m{tt+1}")
            vk2 = (f"var{tt}", f"var{tt+1}")
            rk2 = (f"rstd{tt}", f"rstd{tt+1}")
            nk2 = (f"nmr{tt}", f"nmr{tt+1}")
            v(lambda e, tt=tt, xv2=xv2: e.tensor_reduce(
                stats[:, tt:tt+2], xv2, AX, OP.add),
              ins=ik, outs=sk2)
            for dj in range(2):
                a(lambda e, tt=tt, dj=dj: e.activation(
                    scr1[:, 0:D], x_tok[:, (tt+dj)*D:(tt+dj+1)*D], AF.Square,
                    accum_out=stats[:, 16+tt+dj:17+tt+dj]),
                  ins=(xq(tt + dj),), outs=("scr1", "scr1b", qk2[dj]))
            v(lambda e, tt=tt: e.tensor_scalar(stats[:, tt:tt+2],
                                               stats[:, tt:tt+2],
                                               1.0/D, None, OP.mult),
              ins=sk2, outs=mk2)
            v(lambda e, tt=tt: e.tensor_scalar(stats[:, 16+tt:18+tt],
                                               stats[:, 16+tt:18+tt], 1.0/D,
                                               None, OP.mult),
              ins=qk2, outs=qk2)
            v(lambda e, tt=tt: e.tensor_tensor(stats[:, 48+tt:50+tt],
                                               stats[:, tt:tt+2],
                                               stats[:, tt:tt+2], OP.mult),
              ins=mk2, outs=(f"msq{tt}",))
            v(lambda e, tt=tt: e.tensor_sub(stats[:, 16+tt:18+tt],
                                            stats[:, 16+tt:18+tt],
                                            stats[:, 48+tt:50+tt]),
              ins=qk2 + (f"msq{tt}",), outs=vk2)
            a(lambda e, tt=tt: e.activation(stats[:, 32+tt:34+tt],
                                            stats[:, 16+tt:18+tt], AF.Ln,
                                            bias=1e-5),
              ins=vk2, outs=(f"lnv{tt}",))
            a(lambda e, tt=tt: e.activation(stats[:, 32+tt:34+tt],
                                            stats[:, 32+tt:34+tt], AF.Exp,
                                            scale=-0.5),
              ins=(f"lnv{tt}",), outs=rk2)
            v(lambda e, tt=tt: e.tensor_tensor(
                stats[:, 64+tt:66+tt], stats[:, tt:tt+2],
                stats[:, 32+tt:34+tt], OP.mult),
              ins=mk2 + rk2, outs=nk2)
            v(lambda e, tt=tt: e.tensor_scalar(
                stats[:, 64+tt:66+tt], stats[:, 64+tt:66+tt], -1.0, None,
                OP.mult),
              ins=nk2, outs=nk2)

        def ln_tile(tt, j):
            a(lambda e, tt=tt, j=j: e.activation(
                ln_tok[:, j*D:(j+1)*D], x_tok[:, tt*D:(tt+1)*D], AF.Identity,
                bias=stats[:, 64+tt:65+tt], scale=stats[:, 32+tt:33+tt]),
              ins=(xq(tt), f"rstd{tt}", f"nmr{tt}"),
              outs=("lnbuf" if j == 0 else "lnbufB",))

        def ffn_stats(tiles, wf_pairs=False, qkvo=()):
            qk = list(qkvo)
            for tt in tiles:
                if tt % 2 == 0 and tt + 1 in tiles:
                    ln_stats_pair(tt)
                elif tt % 2 == 1 and tt - 1 in tiles:
                    pass
                else:
                    ln_stats(tt)
                if wf_pairs and tt % 4 == 3:
                    wf_pair(tt // 4)
                for nn, jj in qk[tt*2:tt*2+2]:
                    emit_prep_block(nn, jj)
            if wf_pairs:
                for jp in range((max(tiles) + 1) // 4, 4):
                    wf_pair(jp)
            for nn, jj in qk[2*(max(tiles)+1 if tiles else 0):]:
                emit_prep_block(nn, jj)

        def ffn_amax(tiles):
            for tt in tiles:
                v(lambda e, tt=tt: e.tensor_reduce(
                    rowred[:, 16+tt:17+tt], x_tok[:, tt*D:(tt+1)*D], AX, OP.max),
                  ins=(xq(tt),), outs=(f"amx{tt}",))
                v(lambda e, tt=tt: e.tensor_reduce(
                    rowred[:, 32+tt:33+tt], x_tok[:, tt*D:(tt+1)*D], AX, OP.min),
                  ins=(xq(tt),), outs=(f"ami{tt}",))
                v(lambda e, tt=tt: e.tensor_sub(
                    rowred[:, 16+tt:17+tt], rowred[:, 16+tt:17+tt],
                    stats[:, tt:tt+1]),
                  ins=(f"amx{tt}", f"m{tt}"), outs=(f"amx{tt}",))
                v(lambda e, tt=tt: e.tensor_sub(
                    rowred[:, 32+tt:33+tt], stats[:, tt:tt+1],
                    rowred[:, 32+tt:33+tt]),
                  ins=(f"ami{tt}", f"m{tt}"), outs=(f"ami{tt}",))
                v(lambda e, tt=tt: e.tensor_max(
                    rowred[:, 16+tt:17+tt], rowred[:, 16+tt:17+tt],
                    rowred[:, 32+tt:33+tt]),
                  ins=(f"amx{tt}", f"ami{tt}"), outs=(f"amx{tt}",))
                v(lambda e, tt=tt: e.tensor_mul(
                    rowred[:, 16+tt:17+tt], rowred[:, 16+tt:17+tt],
                    stats[:, 32+tt:33+tt]),
                  ins=(f"amx{tt}", f"rstd{tt}"), outs=(f"am{tt}",))

        def ffn(tag, prep_blocks=(), store=False, stats_done=False):
            if not stats_done:
                ffn_stats(range(NT))
                ffn_amax(range(NT))
            v(lambda e: e.tensor_reduce(rowred[:, 0:1], rowred[:, 16:16+NT], AX,
                                        OP.max),
              ins=tuple(f"am{tt}" for tt in range(NT)), outs=("amax_col",))
            cross_part_max("amax_col", rowred[:, 0:1], 5, "amax")
            v(lambda e: e.tensor_scalar(scal[0:1, 6:7], scal[0:1, 5:6], 1e-12, None,
                                        OP.max), ins=("amax",), outs=("am1",))
            v(lambda e: e.reciprocal(scal[0:1, 7:8], scal[0:1, 6:7]),
              ins=("am1",), outs=("am2",))
            v(lambda e: e.tensor_scalar(scal[0:1, 8:9], scal[0:1, 7:8], 448.0, 448.0,
                                        OP.mult, OP.min), ins=("am2",), outs=("s_x",))
            v(lambda e: e.reciprocal(scal[0:1, 9:10], scal[0:1, 8:9]),
              ins=("s_x",), outs=("rsx",))
            v(lambda e: e.tensor_tensor(scal[0:1, 10:11], scal[0:1, 9:10],
                                        scal[0:1, 4:5], OP.mult),
              ins=("rsx", "rsw"), outs=("deq",))
            bcast_scalar(8, "s_x", 1)
            bcast_scalar(10, "deq", 2)
            v(lambda e: e.reciprocal(scal[0:1, 11:12], scal[0:1, 10:11]),
              ins=("deq",), outs=("ideq",))
            p.dma(lambda e: e.dma_start(out=stage[0:1, :], in_=bias_in),
                  outs=("stage",))
            v(lambda e: e.tensor_scalar(bias_bc[0:1, :], stage[0:1, :],
                                        scal[0:1, 11:12], 0.5, OP.mult,
                                        OP.mult),
              ins=("stage", "ideq"), outs=("bias_bc",))
            for tt in range(NT):
                v(lambda e, tt=tt: e.tensor_tensor(
                    stats[:, 48+tt:49+tt], stats[:, 32+tt:33+tt], scol[:, 1:2],
                    OP.mult),
                  ins=(f"rstd{tt}", "scol1"), outs=(f"rsts{tt}",))
                v(lambda e, tt=tt: e.tensor_tensor(
                    stats[:, 96+tt:97+tt], stats[:, 64+tt:65+tt], scol[:, 1:2],
                    OP.mult),
                  ins=(f"nmr{tt}", "scol1"), outs=(f"nms{tt}",))
            W2 = TPC * D

            def lanes(c):
                par = c % 2
                Y, Ykey = (v_tok, "v_tok") if par == 0 else (sk_tok, "sk_tok")
                Bq = a_s if par == 0 else ln_tok
                BKEYS = ("a_s",) if par == 0 else ("lnbuf", "lnbufB")
                Td, Tkey = (lnT, "lnT") if par == 0 else (qT, "qT")
                if par == 0:
                    pslots = [[(ps_sc, "ps_scA", 0), (ps_sc, "ps_scB", 512)],
                              [(ps_a, "ps_aA", 0), (ps_a, "ps_aB", 512)]]
                else:
                    pslots = [[(ps_m, "ps_mA", 0), (ps_m, "ps_mB", 512)],
                              [(ps_w1, "psw1", 0), (ps_w2, "psw2", 0)]]
                return Y, Ykey, Bq, BKEYS, Td, Tkey, pslots

            def q_phase(c):
                Y, Ykey, Bq, BKEYS, Td, Tkey, pslots = lanes(c)
                for j2 in range(TPC):
                    tt = c * TPC + j2
                    a(lambda e, tt=tt, j2=j2, Y=Y: e.activation(
                        Y[:, j2*D:(j2+1)*D], x_tok[:, tt*D:(tt+1)*D], AF.Identity,
                        bias=stats[:, 96+tt:97+tt], scale=stats[:, 48+tt:49+tt]),
                      ins=(xq(tt), f"rsts{tt}", f"nms{tt}"), outs=(Ykey,))
                v(lambda e, Y=Y, Bq=Bq: e.tensor_scalar(
                    Bq[:, 0:W2].bitcast(U16), Y[:, 0:W2].bitcast(U16),
                    0x7F80, None, OP.bitwise_and),
                  ins=(Ykey,), outs=BKEYS)
                v(lambda e, Bq=Bq: e.tensor_scalar(
                    Bq[:, 0:W2], Bq[:, 0:W2], 24.0, 0.375, OP.mult, OP.max),
                  ins=BKEYS, outs=BKEYS)
                v(lambda e, Y=Y, Bq=Bq: e.tensor_add(Y[:, 0:W2], Y[:, 0:W2],
                                                     Bq[:, 0:W2]),
                  ins=(Ykey,) + BKEYS, outs=(Ykey,))
                v(lambda e, Y=Y, Bq=Bq: e.tensor_sub(Y[:, 0:W2], Y[:, 0:W2],
                                                     Bq[:, 0:W2]),
                  ins=(Ykey,) + BKEYS, outs=(Ykey,))
                for j2 in range(TPC):
                    p.dma(lambda e, Y=Y, Td=Td, j2=j2: e.dma_start_transpose(
                        out=Td.ap().rearrange("p (m tt) -> p m tt",
                                              tt=CH)[:, :, 128*j2:128*j2+128],
                        in_=Y[:, j2*D:(j2+1)*D]),
                        ins=(Ykey,), outs=(Tkey,), xpose=True)

            def m_phase(c):
                Y, Ykey, Bq, BKEYS, Td, Tkey, pslots = lanes(c)
                for j2 in range(TPC):
                    tt = c * TPC + j2
                    scrX, xk0, xk1 = ((scr1, "scr1", "scr1b") if j2 % 2 == 0
                                      else (scr2, "scr2", "scr2b"))
                    for oh in range(2):
                        pst, pkey, off = pslots[j2][oh]
                        for k in range(8):
                            t(lambda e, j2=j2, oh=oh, k=k, pst=pst, off=off,
                              Td=Td: e.matmul(
                                pst[:, off:off+512],
                                Td[:, k*CH+128*j2:k*CH+128*j2+128],
                                wT["f"][:, k*D+oh*512:k*D+(oh+1)*512],
                                start=(k == 0), stop=False, **MM),
                              ins=(Tkey, "wfT"), outs=(pkey,))
                        t(lambda e, oh=oh, pst=pst, off=off: e.matmul(
                            pst[:, off:off+512], ones1[:],
                            bias_bc[0:1, oh*512:(oh+1)*512],
                            start=False, stop=True, **MM),
                          ins=("ones1", "bias_bc"), outs=(pkey,))
                        a(lambda e, oh=oh, pst=pst, off=off, scrX=scrX:
                          e.activation(
                            scrX[:, oh*512:(oh+1)*512], pst[:, off:off+512],
                            AF.Identity, scale=scol[:, 2:3]),
                          ins=(pkey, "scol2"),
                          outs=(xk0 if oh == 0 else xk1,))
                    g(lambda e, tt=tt, scrX=scrX: e.tensor_add(
                        x_tok[:, tt*D:(tt+1)*D], x_tok[:, tt*D:(tt+1)*D],
                        scrX[:, 0:D]),
                      ins=(xk0, xk1, xq(tt)), outs=(xq(tt),))

            pb_iter = list(prep_blocks)
            per = -(-len(pb_iter) // NCH) if pb_iter else 0

            def emit_some(c):
                for nn, jj in pb_iter[c*per:(c+1)*per]:
                    emit_prep_block(nn, jj)

            def store_chunk(c):
                lo, hi = c * TPC, (c + 1) * TPC
                p.dma(lambda e, lo=lo, hi=hi: e.dma_start(
                    out=yv[:, lo:hi, :],
                    in_=x_tok[:, lo*D:hi*D].rearrange("p (tt d) -> p tt d", d=D)),
                    ins=(xq(lo), xq(lo + 1)), outs=(f"y{lo}",))

            depth = min(2, NCH)
            for c in range(depth):
                q_phase(c)
            for c in range(depth, NCH):
                m_phase(c - depth)
                emit_some(c - depth)
                if store:
                    store_chunk(c - depth)
                q_phase(c)
            for c in range(NCH - depth, NCH):
                m_phase(c)
                emit_some(c)
                if store:
                    store_chunk(c)

        wfb = wT["f"]

        def quartet(c):
            if c % 2 == 0:
                return (sqT.ap(), "sqT", skT.ap(), "skT", sk_tok.ap(),
                        "sk_tok", v_tok.ap(), "v_tok")
            return (wfb[:, 0:2048], "sqTo", wfb[:, 2048:4096], "skTo",
                    wfb[:, 4096:6144], "sk_toko", wfb[:, 6144:8192], "v_toko")

        def acol(h):
            return (h % 2) * 512 + 64 * (h // 2)

        def attn_front_parts(c):
            sq, sqk, sk, skk, skt, sktk, vt, vtk = quartet(c)
            pwi = [0]

            def grp():
                r = [(ps_w1, "psw1"), (ps_w2, "psw2")][pwi[0] % 2]
                pwi[0] += 1
                return r

            def sigma(srcT, skey, dst, dkey):
                for hh in range(2):
                    sl = (slice(None), slice(hh*1024, (hh+1)*1024))
                    xb = ln_tok[:, hh*1024:(hh+1)*1024]
                    xbk = "lnbuf" if hh == 0 else "lnbufB"
                    v(lambda e, sl=sl, srcT=srcT, xb=xb: e.tensor_scalar(
                        xb, srcT[sl], 0.0, None, OP.min),
                      ins=(skey,), outs=(xbk,))
                    a(lambda e, xb=xb: e.activation(xb, xb, AF.Exp),
                      ins=(xbk,), outs=(xbk,))
                    v(lambda e, sl=sl, srcT=srcT, dst=dst, xb=xb:
                      e.scalar_tensor_tensor(
                        dst[sl], srcT[sl], 1.0, xb, OP.add, OP.max),
                      ins=(skey, xbk), outs=(dkey,))

            def proj_fm(wtag, wkey, dst, dkey):
                for oj in range(8):
                    pb, pbk = grp()
                    for k in range(8):
                        t(lambda e, oj=oj, k=k, pb=pb, wtag=wtag: e.matmul(
                            pb[:, 0:CH], wT[wtag][:, k*D+128*oj:k*D+128*oj+128],
                            lnT[:, k*CH:(k+1)*CH], start=(k == 0),
                            stop=(k == 7), **MM),
                          ins=("lnT", wkey), outs=(pbk,))
                    if oj % 2 == 0:
                        a(lambda e, oj=oj, pb=pb, dst=dst: e.activation(
                            dst[:, oj*CH:(oj+1)*CH], pb[:, 0:CH], AF.Copy),
                          ins=(pbk,), outs=(dkey,))
                    else:
                        v(lambda e, oj=oj, pb=pb, dst=dst: e.tensor_copy(
                            dst[:, oj*CH:(oj+1)*CH], pb[:, 0:CH]),
                          ins=(pbk,), outs=(dkey,))

            def p0():
                if TPC == 2:
                    ln_stats_pair(c * TPC)
                    for j in range(TPC):
                        ln_tile(c * TPC + j, j)
                else:
                    for j in range(TPC):
                        tt = c * TPC + j
                        ln_stats(tt)
                        ln_tile(tt, j)
                for j in range(TPC):
                    p.dma(lambda e, j=j: e.dma_start_transpose(
                        out=lnT.ap().rearrange("p (m tt) -> p m tt",
                                               tt=CH)[:, :, 128*j:128*j+128],
                        in_=ln_tok[:, j*D:(j+1)*D]),
                        ins=("lnbuf" if j == 0 else "lnbufB",), outs=("lnT",),
                        xpose=True)

            def p1():
                proj_fm("q", "wqT", qT, "qT")
                sigma(qT, "qT", sq, sqk)

            def p2():
                proj_fm("k", "wkT", kT, "kT")
                sigma(kT, "kT", sk, skk)

            def p3():
                for j in range(TPC):
                    for oh in range(2):
                        pb, pbk = grp()
                        for k in range(8):
                            t(lambda e, j=j, oh=oh, k=k, pb=pb: e.matmul(
                                pb[:, 0:512],
                                lnT[:, k*CH+128*j:k*CH+128*j+128],
                                wT["v"][:, k*D+oh*512:k*D+(oh+1)*512],
                                start=(k == 0), stop=(k == 7), **MM),
                              ins=("lnT", "wvT"), outs=(pbk,))
                        a(lambda e, j=j, oh=oh, pb=pb, vt=vt: e.activation(
                            vt[:, j*D+oh*512:j*D+(oh+1)*512], pb[:, 0:512],
                            AF.Copy),
                          ins=(pbk,), outs=(vtk,))
                for k in range(8):
                    p.dma(lambda e, k=k, sk=sk, skt=skt: e.dma_start_transpose(
                        out=skt[:, 0:TPC*D].rearrange(
                            "p (j dd) -> p j dd", dd=D)[:, :, 128*k:128*k+128],
                        in_=sk[:, k*CH:(k+1)*CH]),
                        ins=(skk,), outs=(sktk,), xpose=True)
                pb, pbk = grp()
                for j in range(TPC):
                    for k in range(8):
                        t(lambda e, j=j, k=k, pb=pb, skt=skt: e.matmul(
                            pb[:, j*16+2*k:j*16+2*k+2],
                            skt[:, j*D+128*k:j*D+128*k+128], oneh2[:],
                            start=True, stop=True, **MM),
                          ins=(sktk, "oneh2"), outs=(pbk,))
                v(lambda e, pb=pb: e.tensor_copy(csum_f[:, 0:TPC*16],
                                                 pb[:, 0:TPC*16]),
                  ins=(pbk,), outs=("csum_f",))
                for s in range(2 * TPC):
                    j, sg = divmod(s, 2)
                    e0 = j * 4 + sg * 2
                    v(lambda e, e0=e0: e.tensor_copy(
                        z2[0:64, :]
                        .rearrange("p (kk eight) -> p kk eight",
                                   eight=8)[:, :, e0:e0+1],
                        zfm[0:64, :].rearrange("p (kk o) -> p kk o", o=1)),
                      ins=("zfm",), outs=("z2",))
                    v(lambda e, e0=e0: e.tensor_copy(
                        z2[64:128, :]
                        .rearrange("p (kk eight) -> p kk eight",
                                   eight=8)[:, :, e0+1:e0+2],
                        zfm[64:128, :].rearrange("p (kk o) -> p kk o", o=1)),
                      ins=("zfm",), outs=("z2",))
                    v(lambda e, j=j, sg=sg: e.tensor_add(
                        zfm[:].rearrange("p (kk o) -> p kk o", o=1),
                        zfm[:].rearrange("p (kk o) -> p kk o", o=1),
                        csum_f[:, j*16:(j+1)*16]
                        .rearrange("p (kk two) -> p kk two", two=2)[:, :, sg:sg+1]),
                      ins=("csum_f", "zfm"), outs=("zfm",))
                pb, pbk = grp()
                for j in range(TPC):
                    for k in range(8):
                        t(lambda e, j=j, k=k, pb=pb, sq=sq: e.matmul(
                            pb[:, k*8+j*4:k*8+j*4+4],
                            sq[:, k*CH+j*128:k*CH+j*128+128],
                            z2[:, k*8+j*4:k*8+j*4+4],
                            start=True, stop=True, **MM),
                          ins=(sqk, "z2"), outs=(pbk,))
                        t(lambda e, j=j, k=k, pb=pb, sk=sk: e.matmul(
                            pb[:, 64+k*8+j*4:64+k*8+j*4+4],
                            sk[:, k*CH+j*128:k*CH+j*128+128],
                            z2[:, k*8+j*4:k*8+j*4+4],
                            start=True, stop=True, **MM),
                          ins=(skk, "z2"), outs=(pbk,))
                v(lambda e, pb=pb: e.tensor_copy(dq_sb[:, 0:128],
                                                 pb[:, 0:128]),
                  ins=(pbk,), outs=("dq_sb",))
                v(lambda e: e.tensor_scalar(dq_sb[:, 0:128],
                                            dq_sb[:, 0:128], 1e-6,
                                            None, OP.add),
                  ins=("dq_sb",), outs=("dq_sb",))
                v(lambda e: e.reciprocal(rdq[:, 0:128], dq_sb[:, 0:128]),
                  ins=("dq_sb",), outs=("rdq",))
            return [p0, p1, p2, p3]

        def attn_mid(c):
            sq, sqk, sk, skk, skt, sktk, vt, vtk = quartet(c)
            for j in range(TPC):
                cp, cpk = (c1p, "c1p") if j == 0 else (c1pB, "c1pB")
                for sg, lo in ((0, 0), (1, 64)):
                    e0 = j * 4 + sg * 2
                    v(lambda e, cp=cp, lo=lo, e0=e0: e.tensor_tensor(
                        cp[lo:lo+64, :].rearrange("p (i two) -> p i two", two=2),
                        rdq[lo:lo+64, 0:64]
                        .rearrange("p (kk eight) -> p kk eight",
                                   eight=8)[:, :, e0:e0+2],
                        betac[lo:lo+64, :].rearrange("p (i two) -> p i two",
                                                     two=2),
                        OP.mult),
                      ins=("rdq", "betac"), outs=(cpk,))
                    v(lambda e, j=j, lo=lo, e0=e0: e.tensor_copy(
                        c3_sb[lo:lo+64, j*16:(j+1)*16]
                        .rearrange("p (i two) -> p i two", two=2),
                        rdq[lo:lo+64, 64:128]
                        .rearrange("p (kk eight) -> p kk eight",
                                   eight=8)[:, :, e0:e0+2]),
                      ins=("rdq",), outs=("c3_sb",))
            for j in range(TPC):
                g(lambda e, j=j, sk=sk, skt=skt: e.tensor_tensor(
                    sk[:, j*D:(j+1)*D].rearrange("p (hh kk) -> p hh kk", kk=64),
                    skt[:, j*D:(j+1)*D].rearrange("p (hh kk) -> p hh kk", kk=64),
                    c3_sb[:, j*16:(j+1)*16].rearrange("p (hh o) -> p hh o", o=1)
                    .broadcast_to([128, 16, 64]), OP.mult),
                  ins=(sktk, "c3_sb", skk), outs=(skk,))
            for j in range(TPC):
                pt, ptk = (pt_un, "pt_un") if j == 0 else (scr1.ap(), "scr1")
                pc, pck = (pt_c2, "pt_c2") if j == 0 else (scr2.ap(), "scr2")
                ptks = (ptk,) if j == 0 else ("scr1", "scr1b")
                pcks = (pck,) if j == 0 else ("scr2", "scr2b")
                for sg in range(2):
                    for par in range(2):
                        for h in range(par, H, 2):
                            t(lambda e, h=h, sg=sg, j=j: e.matmul(
                                ps_sc[sg*64:sg*64+64, acol(h):acol(h)+64],
                                kT[(h % 2)*64:(h % 2)*64+64,
                                   (h//2)*CH+j*128+sg*64:(h//2)*CH+j*128+sg*64+64],
                                qT[(h % 2)*64:(h % 2)*64+64,
                                   (h//2)*CH+j*128+sg*64:(h//2)*CH+j*128+sg*64+64],
                                start=True, stop=True,
                                tile_position=((h % 2)*64, sg*64), **MM),
                              ins=("kT", "qT"),
                              outs=("ps_scA" if par == 0 else "ps_scB",))
                for par in range(2):
                    a(lambda e, par=par, pt=pt: e.activation(
                        pt[:, par*512:(par+1)*512], ps_sc[:, par*512:(par+1)*512],
                        AF.Exp, bias=-40.0, scale=0.125),
                      ins=("ps_scA" if par == 0 else "ps_scB",), outs=ptks)
                for lo in (0, 64):
                    g(lambda e, pt=pt, lo=lo: e.affine_select(
                        pt[lo:lo+64, :].rearrange("p (hh tt) -> p hh tt", tt=64),
                        pt[lo:lo+64, :].rearrange("p (hh tt) -> p hh tt", tt=64),
                        pattern=[[0, 16], [1, 64]], compare_op=OP.is_ge, fill=0.0,
                        base=0, channel_multiplier=-1),
                      ins=ptks, outs=ptks)
                t(lambda e, pt=pt: e.matmul(ps_w1[0:2, 0:512], oneh2[:],
                                            pt[:, 0:512], start=True, stop=True,
                                            **MM),
                  ins=("oneh2",) + ptks, outs=("psw1",))
                t(lambda e, pt=pt: e.matmul(ps_w2[0:2, 0:512], oneh2[:],
                                            pt[:, 512:1024], start=True,
                                            stop=True, **MM),
                  ins=("oneh2",) + ptks, outs=("psw2",))
                for pw, pwk in ((ps_w1, "psw1"), (ps_w2, "psw2")):
                    a(lambda e, pw=pw: e.activation(pw[0:2, 0:512],
                                                    pw[0:2, 0:512], AF.Ln),
                      ins=(pwk,), outs=(pwk,))
                    a(lambda e, pw=pw: e.activation(pw[0:2, 0:512],
                                                    pw[0:2, 0:512], AF.Exp,
                                                    scale=-1.0),
                      ins=(pwk,), outs=(pwk,))
                v(lambda e: e.tensor_tensor(c2bE[:], ps_w1[0:2, 0:512], betaE[:],
                                            OP.mult),
                  ins=("psw1", "betaE"), outs=("c2pool",))
                v(lambda e: e.tensor_tensor(c2bO[:], ps_w2[0:2, 0:512], betaO[:],
                                            OP.mult),
                  ins=("psw2", "betaO"), outs=("c2pool",))
                t(lambda e: e.matmul(ps_w1[:, 0:512], bt2[:], c2bE[:],
                                     start=True, stop=True, **MM),
                  ins=("bt2", "c2pool"), outs=("psw1",))
                t(lambda e: e.matmul(ps_w2[:, 0:512], bt2[:], c2bO[:],
                                     start=True, stop=True, **MM),
                  ins=("bt2", "c2pool"), outs=("psw2",))
                v(lambda e, pt=pt, pc=pc: e.tensor_tensor(
                    pc[:, 0:512], pt[:, 0:512], ps_w1[:, 0:512], OP.mult),
                  ins=ptks + ("psw1",), outs=pcks)
                v(lambda e, pt=pt, pc=pc: e.tensor_tensor(
                    pc[:, 512:1024], pt[:, 512:1024], ps_w2[:, 0:512], OP.mult),
                  ins=ptks + ("psw2",), outs=pcks)

        def attn_back_seg(c, s):
            sq, sqk, sk, skk, skt, sktk, vt, vtk = quartet(c)
            j, sg = divmod(s, 2)
            pc, pcks = (pt_c2, ("pt_c2",)) if j == 0 \
                else (scr2.ap(), ("scr2", "scr2b"))
            cp, cpk = (c1p, "c1p") if j == 0 else (c1pB, "c1pB")
            ib, ibk = (ib_sb.ap(), "ib_sb") if s % 2 == 0 \
                else (bias_bc[:, 0:512], "bias_bc")
            for par in range(2):
                for h in range(par, H, 2):
                    t(lambda e, h=h, sg=sg, j=j, sk=sk, skt=skt: e.matmul(
                        ps_sc[(h % 2)*64:(h % 2)*64+64, 64*(h//2):64*(h//2)+64],
                        sk[sg*64:sg*64+64, j*D+64*h:j*D+64*h+64],
                        skt[sg*64:sg*64+64, j*D+64*h:j*D+64*h+64],
                        start=True, stop=True,
                        tile_position=(sg*64, (h % 2)*64), **MM),
                      ins=(skk, sktk), outs=("ps_scA",))
            v(lambda e, ib=ib: e.scalar_tensor_tensor(
                ib[:, 0:512].rearrange("p (i kk) -> p i kk", kk=64),
                ps_sc[:, 0:512].rearrange("p (i kk) -> p i kk", kk=64), -1.0,
                irep[:].rearrange("p (o kk) -> p o kk", o=1)
                .broadcast_to([128, 8, 64]),
                OP.mult, OP.add),
              ins=("ps_scA", "irep"), outs=(ibk,))
            for par in range(2):
                for h in range(par, H, 2):
                    t(lambda e, h=h, sg=sg, j=j, sq=sq: e.matmul(
                        ps_a[sg*64:sg*64+64, acol(h):acol(h)+64],
                        sq[(h % 2)*64:(h % 2)*64+64,
                           (h//2)*CH+j*128+sg*64:(h//2)*CH+j*128+sg*64+64],
                        m_sb[(h % 2)*64:(h % 2)*64+64, 64*(h//2):64*(h//2)+64],
                        start=True, stop=True,
                        tile_position=((h % 2)*64, sg*64), **MM),
                      ins=(sqk, "m_sbA" if h % 2 == 0 else "m_sbB"),
                      outs=("ps_aA" if h % 2 == 0 else "ps_aB",))
            for par in range(2):
                for h in range(par, H, 2):
                    pw = ps_w1 if h % 2 == 0 else ps_w2
                    t(lambda e, h=h, sg=sg, j=j, pc=pc, vt=vt, pw=pw: e.matmul(
                        pw[sg*64:sg*64+64, 64*(h//2):64*(h//2)+64],
                        pc[sg*64:sg*64+64, acol(h):acol(h)+64],
                        vt[sg*64:sg*64+64, j*D+64*h:j*D+64*h+64],
                        start=True, stop=True,
                        tile_position=(sg*64, sg*64), **MM),
                      ins=pcks + (vtk,),
                      outs=("psw1" if h % 2 == 0 else "psw2",))
            for par in range(2):
                asl = a_s[:, j*D:(j+1)*D].rearrange(
                    "p (i f) -> p i f", f=128)[sg*64:sg*64+64, :,
                                               par*64:par*64+64]
                v(lambda e, asl=asl, sg=sg, par=par, cp=cp: e.tensor_tensor(
                    asl,
                    ps_a[sg*64:sg*64+64, par*512:par*512+512]
                    .rearrange("p (i kk) -> p i kk", kk=64),
                    cp[sg*64:sg*64+64, :]
                    .rearrange("p (i two) -> p i two", two=2)[:, :, par:par+1]
                    .broadcast_to([64, 8, 64]),
                    OP.mult),
                  ins=("ps_aA" if par == 0 else "ps_aB", cpk), outs=("a_s",))
                pw = ps_w1 if par == 0 else ps_w2
                v(lambda e, asl=asl, sg=sg, par=par, pw=pw: e.tensor_add(
                    asl, asl,
                    pw[sg*64:sg*64+64, 0:512]
                    .rearrange("p (i kk) -> p i kk", kk=64)),
                  ins=("a_s", "psw1" if par == 0 else "psw2"),
                  outs=("a_s",))
            for par in range(2):
                for h in range(par, H, 2):
                    t(lambda e, h=h, par=par, ib=ib: e.matmul(
                        ps_m[(h % 2)*64:(h % 2)*64+64, acol(h):acol(h)+64],
                        ib[(h % 2)*64:(h % 2)*64+64, 64*(h//2):64*(h//2)+64],
                        m_sb[(h % 2)*64:(h % 2)*64+64, 64*(h//2):64*(h//2)+64],
                        start=(h == par), stop=False,
                        tile_position=((h % 2)*64, (h % 2)*64), **MM),
                      ins=(ibk, "m_sbA" if h % 2 == 0 else "m_sbB"),
                      outs=("ps_mA" if h % 2 == 0 else "ps_mB",))
            for par in range(2):
                for h in range(par, H, 2):
                    t(lambda e, h=h, sg=sg, j=j, skt=skt, vt=vt: e.matmul(
                        ps_m[(h % 2)*64:(h % 2)*64+64, acol(h):acol(h)+64],
                        skt[sg*64:sg*64+64, j*D+64*h:j*D+64*h+64],
                        vt[sg*64:sg*64+64, j*D+64*h:j*D+64*h+64],
                        start=False, stop=True,
                        tile_position=(sg*64, (h % 2)*64), **MM),
                      ins=(sktk, vtk),
                      outs=("ps_mA" if h % 2 == 0 else "ps_mB",))
            a(lambda e: e.activation(m_sb[0:64, :], ps_m[0:64, 0:512],
                                     AF.Copy),
              ins=("ps_mA",), outs=("m_sbA",))
            a(lambda e: e.activation(m_sb[64:128, :], ps_m[64:128, 512:1024],
                                     AF.Copy),
              ins=("ps_mB",), outs=("m_sbB",))

        def attn_back_tail(c):
            for j in range(TPC):
                p.dma(lambda e, j=j: e.dma_start_transpose(
                    out=asT.ap().rearrange("p (m tt) -> p m tt",
                                           tt=CH)[:, :, 128*j:128*j+128],
                    in_=a_s[:, j*D:(j+1)*D]),
                    ins=("a_s",), outs=("pt_un", "pt_c2"), xpose=True)
            for j in range(TPC):
                tt = c * TPC + j
                for oh in range(2):
                    for k in range(8):
                        t(lambda e, j=j, oh=oh, k=k: e.matmul(
                            ps_m[:, oh*512:(oh+1)*512],
                            asT[:, k*CH+128*j:k*CH+128*j+128],
                            wT["o"][:, k*D+oh*512:k*D+(oh+1)*512],
                            start=(k == 0), stop=(k == 7), **MM),
                          ins=("pt_un", "pt_c2", "woT"),
                          outs=("ps_mA" if oh == 0 else "ps_mB",))
                    v(lambda e, tt=tt, oh=oh: e.tensor_add(
                        x_tok[:, tt*D+oh*512:tt*D+(oh+1)*512],
                        x_tok[:, tt*D+oh*512:tt*D+(oh+1)*512],
                        ps_m[:, oh*512:(oh+1)*512]),
                      ins=("ps_mA" if oh == 0 else "ps_mB", xq(tt)),
                      outs=(xq(tt),))

        def rebuild_for_ffn2():
            for jp in range(4):
                wf_pair(jp, extra=("sqTo", "skTo", "sk_toko", "v_toko"))

        yv = y_out.rearrange("(tt p) d -> p tt d", p=128)
        qkvo_blocks = [(nn, jj) for nn in ("q", "k", "v", "o") for jj in range(8)]
        ffn_stats(range(NT), wf_pairs=True, qkvo=qkvo_blocks)
        ffn_amax(range(NT))
        ffn("f1", stats_done=True)
        for f in attn_front_parts(0):
            f()
        for c in range(NCH):
            attn_mid(c)
            nxt = attn_front_parts(c + 1) if c + 1 < NCH else None
            for s in range(2 * TPC):
                attn_back_seg(c, s)
                if nxt is not None and s < 4:
                    nxt[s]()
            if c == NCH - 1:
                rebuild_for_ffn2()
            attn_back_tail(c)
            ffn_stats([c * TPC + j for j in range(TPC)])
            ffn_amax([c * TPC + j for j in range(TPC)])
        ffn("f2", store=True, stats_done=True)
        p.dma_fence()

        @blk.tensor
        def _(e):
            for gi, fn in p.body["te"]:
                if TRUNC is None or gi <= TRUNC:
                    fn(e)

        @blk.vector
        def _(e):
            for gi, fn in p.body["dv"]:
                if TRUNC is None or gi <= TRUNC:
                    fn(e)

        @blk.scalar
        def _(e):
            for gi, fn in p.body["ac"]:
                if TRUNC is None or gi <= TRUNC:
                    fn(e)

        @blk.gpsimd
        def _(e):
            for gi, fn in p.body["gp"]:
                if TRUNC is None or gi <= TRUNC:
                    fn(e)

        @blk.sync
        def _(e):
            for gi, fn in p.body["sy"]:
                if TRUNC is None or gi <= TRUNC:
                    fn(e)

    return nc


def kernel(L0_attn_x, L0_attn_W_q, L0_attn_W_k, L0_attn_W_v, L0_attn_W_o,
           L0_attn_beta_param, L0_ffn_weight, L0_ffn_bias):
    x = np.ascontiguousarray(np.asarray(L0_attn_x, dtype=np.float32))
    B, T, _ = x.shape
    nc = build_program(T=T)
    wmap = {
        "wq": np.ascontiguousarray(np.asarray(L0_attn_W_q, np.float32)),
        "wk": np.ascontiguousarray(np.asarray(L0_attn_W_k, np.float32)),
        "wv": np.ascontiguousarray(np.asarray(L0_attn_W_v, np.float32)),
        "wo": np.ascontiguousarray(np.asarray(L0_attn_W_o, np.float32)),
        "wf": np.ascontiguousarray(np.asarray(L0_ffn_weight, np.float32)),
        "fbias": np.ascontiguousarray(np.asarray(L0_ffn_bias, np.float32)).reshape(1, -1),
        "beta": np.ascontiguousarray(np.asarray(L0_attn_beta_param, np.float32)).reshape(1, -1),
    }
    in_maps = [dict(wmap, x=np.ascontiguousarray(x[i])) for i in range(B)]
    res = run_bass_kernel_spmd(nc, in_maps, core_ids=list(range(B)))
    out = np.stack([np.asarray(res.results[i]["y"]) for i in range(B)], axis=0)
    return out.astype(np.float32)
